# version 15
# baseline (speedup 1.0000x reference)
import os
import traceback
import numpy as np
import ml_dtypes
from contextlib import ExitStack

DEBUG_EMIT = os.environ.get("MK_DEBUG_EMIT", "0") == "1"
import concourse.bass as bass
import concourse.mybir as mybir
from concourse.bass_utils import run_bass_kernel_spmd

F32 = mybir.dt.float32
BF16 = mybir.dt.bfloat16
I32 = mybir.dt.int32
ALU = mybir.AluOpType
AF = mybir.ActivationFunctionType
AX = mybir.AxisListType
BF = ml_dtypes.bfloat16

T = 2048
D = 4096
I_FFN = 11008
NT = T // 128
ND = D // 128

PERM_H = np.array([1024 * (kk % 4) + 8 * r + (kk // 4)
                   for kk in range(32) for r in range(128)])
PERM_I = np.array([8 * (ip % 1376) + ip // 1376 for ip in range(I_FFN)])


class Stream:
    def __init__(self, es, nc, name):
        self.sem = es.enter_context(nc.semaphore(name))
        self.name = name
        self.n = 0


class Buf:
    __slots__ = ("writer", "readers", "name", "dstream")

    def __init__(self, name=""):
        self.writer = None
        self.readers = {}
        self.name = name
        self.dstream = None


class Ctx:
    ENGS = ["sync", "tensor", "vector", "scalar", "gpsimd"]

    def __init__(self, es, nc):
        self.nc = nc
        self.es = es
        self.ops = {e: [] for e in self.ENGS}
        self.t = Stream(es, nc, "t")
        self.v = Stream(es, nc, "v")
        self.s = Stream(es, nc, "s")
        self.p = Stream(es, nc, "p")
        self.waited = {}
        self.nops = 0
        self.dstreams = []
        self.free_dstreams = []
        self.nsem = 0

    def dstream_for(self, key):
        if key.dstream is None:
            if self.free_dstreams:
                key.dstream = self.free_dstreams.pop()
            else:
                key.dstream = Stream(self.es, self.nc, f"d{self.nsem}")
                self.nsem += 1
                self.dstreams.append(key.dstream)
        return key.dstream

    def emit(self, eng, fn):
        if DEBUG_EMIT:
            fn._tb = traceback.format_stack()[-4:-1]
        self.ops[eng].append(fn)
        self.nops += 1

    def _edges(self, stream, ins, outs):
        need = {}
        for b in ins:
            if b.writer:
                s, v = b.writer
                if not (s is stream and s.name == "t"):
                    need[s] = max(need.get(s, 0), v)
        for b in outs:
            if b.writer:
                s, v = b.writer
                if not (s is stream and s.name == "t"):
                    need[s] = max(need.get(s, 0), v)
            for s, v in b.readers.items():
                if not (s is stream and s.name == "t"):
                    need[s] = max(need.get(s, 0), v)
        waits = []
        for s, v in need.items():
            key = (stream, s)
            if self.waited.get(key, 0) >= v:
                continue
            self.waited[key] = v
            waits.append((s, v))
        return waits

    def _post(self, stream, ins, outs):
        for b in outs:
            b.writer = (stream, stream.n)
            b.readers = {}
        for b in ins:
            b.readers[stream] = stream.n

    def op(self, eng, fn, ins=(), outs=(), extra_waits=()):
        stream = {"tensor": self.t, "vector": self.v, "scalar": self.s,
                  "gpsimd": self.p}[eng]
        waits = self._edges(stream, ins, outs) + list(extra_waits)
        stream.n += 1

        def f(e, fn=fn, waits=waits, stream=stream):
            for s, v in waits:
                e.wait_ge(s.sem, v)
            fn(e).then_inc(stream.sem, 1)
        self.emit(eng, f)
        self._post(stream, ins, outs)
        return stream.n

    def dma(self, eng, out_ap, in_ap, key, ins=(), outs=(), extra_waits=(),
            **kw):
        stream = self.dstream_for(key)
        waits = self._edges(stream, ins, outs) + list(extra_waits)
        stream.n += 16

        def f(e, out_ap=out_ap, in_ap=in_ap, waits=waits, stream=stream, kw=kw):
            for s, v in waits:
                e.wait_ge(s.sem, v)
            e.dma_start(out=out_ap, in_=in_ap, **kw).then_inc(stream.sem, 16)
        self.emit(eng, f)
        self._post(stream, ins, outs)
        return stream.n

    def gather(self, out_ap, in_ap, idx_ap, key, ins=(), outs=(),
               extra_waits=()):
        stream = self.dstream_for(key)
        waits = self._edges(stream, ins, outs) + list(extra_waits)
        stream.n += 16

        def f(e, out_ap=out_ap, in_ap=in_ap, idx_ap=idx_ap, waits=waits):
            for s, v in waits:
                e.wait_ge(s.sem, v)
            e.indirect_dma_start(
                out_ap, None, in_ap,
                bass.IndirectOffsetOnAxis(idx_ap, 0)).then_inc(stream.sem, 16)
        self.emit("gpsimd", f)
        self._post(stream, ins, outs)
        return stream.n

    def barrier_wait(self, eng, stream, val):
        def f(e, stream=stream, val=val):
            e.wait_ge(stream.sem, val)
        self.emit(eng, f)

    def fence(self, skip_streams=(), skip_engines=()):
        skip = set(skip_streams)
        snap = [(st, st.n) for st in
                [self.t, self.v, self.s, self.p] + self.dstreams
                if st.n > 0 and st not in skip]
        engs = {"sync": None, "tensor": self.t, "vector": self.v,
                "scalar": self.s, "gpsimd": self.p}
        for eng, own in engs.items():
            if eng in skip_engines:
                continue
            for (st, v) in snap:
                if st is own:
                    continue
                self.barrier_wait(eng, st, v)
        for dst_st in [self.t, self.v, self.s, self.p] + self.dstreams:
            if dst_st in skip:
                continue
            for (st, v) in snap:
                self.waited[(dst_st, st)] = max(
                    self.waited.get((dst_st, st), 0), v)

    def final_waits(self, eng="gpsimd"):
        for st in [self.t, self.v, self.s, self.p] + self.dstreams:
            if st.n > 0:
                self.barrier_wait(eng, st, st.n)

    def run_block(self):
        nc = self.nc
        def _runall(e, fns):
            for f in fns:
                try:
                    f(e)
                except Exception:
                    if hasattr(f, "_tb"):
                        print("EMIT SITE:\n" + "".join(f._tb))
                    raise
        with nc.Block() as block:
            @block.sync
            def _(e):
                _runall(e, self.ops["sync"])

            @block.tensor
            def _(e):
                _runall(e, self.ops["tensor"])

            @block.vector
            def _(e):
                _runall(e, self.ops["vector"])

            @block.scalar
            def _(e):
                _runall(e, self.ops["scalar"])

            @block.gpsimd
            def _(e):
                _runall(e, self.ops["gpsimd"])


def emit_ln(ctx, B, x_dram, xnp_dram, names, *, permuted, rsum=None,
            rsum_lo=0, rsum_hi=0):
    nc = ctx.nc
    eps = 1e-5
    sb, bf = B.sb, B.buf
    idn = sb["ident"][:]
    if rsum is not None:
        B.bufonly(rsum + "_part")
        ctx.op("vector", lambda e: e.memset(sb[rsum][:], 0.0),
               ins=[], outs=[bf[rsum]])
    for tt in range(NT):
        sl = tt % 2
        xt, xtb = sb[f"{names}xt{sl}"], bf[f"{names}xt{sl}"]
        xnb, xnbb = sb[f"{names}xnb{sl}"], bf[f"{names}xnb{sl}"]
        sq, sqb = sb[f"{names}sq"], bf[f"{names}sq"]
        mom, momb = sb[f"{names}mom"], bf[f"{names}mom"]
        ctx.dma("sync", xt[:], x_dram.ap[tt * 128:(tt + 1) * 128, :], xtb,
                ins=[x_dram], outs=[xtb])
        ctx.op("vector", lambda e, xt=xt, mom=mom: e.reduce_sum(
            mom[:, 0:1], xt[:], AX.X), ins=[xtb], outs=[momb])
        ctx.op("scalar", lambda e, xt=xt, sq=sq: e.activation(
            sq[:], xt[:], AF.Square), ins=[xtb], outs=[sqb])
        ctx.op("vector", lambda e, sq=sq, mom=mom: e.reduce_sum(
            mom[:, 1:2], sq[:], AX.X), ins=[sqb], outs=[momb])
        ctx.op("vector", lambda e, mom=mom: e.tensor_scalar(
            mom[:, 0:1], mom[:, 0:1], 1.0 / D, None, ALU.mult),
            ins=[momb], outs=[momb])
        ctx.op("vector", lambda e, mom=mom: e.tensor_scalar(
            mom[:, 1:2], mom[:, 1:2], 1.0 / D, None, ALU.mult),
            ins=[momb], outs=[momb])
        ctx.op("vector", lambda e, mom=mom: e.tensor_tensor(
            mom[:, 2:3], mom[:, 0:1], mom[:, 0:1], ALU.mult),
            ins=[momb], outs=[momb])
        ctx.op("vector", lambda e, mom=mom: e.tensor_tensor(
            mom[:, 1:2], mom[:, 1:2], mom[:, 2:3], ALU.subtract),
            ins=[momb], outs=[momb])
        ctx.op("vector", lambda e, mom=mom: e.tensor_scalar(
            mom[:, 1:2], mom[:, 1:2], float(eps), None, ALU.add),
            ins=[momb], outs=[momb])
        ctx.op("scalar", lambda e, mom=mom: e.activation(
            mom[:, 2:3], mom[:, 1:2], AF.Sqrt),
            ins=[momb], outs=[momb])
        ctx.op("vector", lambda e, mom=mom: e.reciprocal(
            mom[:, 1:2], mom[:, 2:3]), ins=[momb], outs=[momb])
        ctx.op("vector", lambda e, mom=mom: e.tensor_tensor(
            mom[:, 3:4], mom[:, 0:1], mom[:, 1:2], ALU.mult),
            ins=[momb], outs=[momb])
        ctx.op("vector", lambda e, mom=mom: e.tensor_scalar(
            mom[:, 3:4], mom[:, 3:4], -1.0, None, ALU.mult),
            ins=[momb], outs=[momb])
        ctx.op("scalar", lambda e, xt=xt, xnb=xnb, mom=mom: e.activation(
            xnb[:], xt[:], AF.Identity, bias=mom[:, 3:4], scale=mom[:, 1:2]),
            ins=[xtb, momb], outs=[xnbb])
        for kk in range(ND):
            psl = (tt * ND + kk) % 2
            sl8 = (tt * ND + kk) % 8
            pst = B.ps_ap[:, 512 * psl:512 * psl + 64].bitcast(BF16)
            pstb = bf[f"pst{psl}"]
            tout, toutb = sb[f"{names}to{sl8}"], bf[f"{names}to{sl8}"]
            if permuted:
                j, b = kk // 4, kk % 4
                src = xnb[:, 1024 * b + j: 1024 * b + j + 1017: 8]
            else:
                src = xnb[:, kk * 128:(kk + 1) * 128]
            ctx.op("tensor", lambda e, src=src, pst=pst: e.transpose(
                pst, src, idn), ins=[xnbb, bf["ident"]], outs=[pstb])
            ctx.op("vector", lambda e, pst=pst, tout=tout: e.tensor_copy(
                tout[:], pst), ins=[pstb], outs=[toutb])
            if rsum is not None:
                lo = max(rsum_lo, tt * 128) - tt * 128
                hi = min(rsum_hi, (tt + 1) * 128) - tt * 128
                if hi > lo:
                    rs, rsb = sb[rsum], bf[rsum]
                    ctx.op("vector",
                           lambda e, tout=tout, rs=rs, kk=kk, lo=lo, hi=hi:
                           e.tensor_reduce(rs[:, 32 + kk:33 + kk],
                                           tout[:, lo:hi], AX.X, ALU.add),
                           ins=[toutb], outs=[bf[rsum + "_part"]])
                    ctx.op("vector",
                           lambda e, rs=rs, kk=kk: e.tensor_tensor(
                               rs[:, kk:kk + 1], rs[:, kk:kk + 1],
                               rs[:, 32 + kk:33 + kk], ALU.add),
                           ins=[bf[rsum + "_part"], rsb], outs=[rsb])
            ctx.dma("sync",
                    xnp_dram.ap[kk, :, tt * 128:(tt + 1) * 128], tout[:],
                    toutb, ins=[toutb], outs=[xnp_dram])


class DramBuf(Buf):
    __slots__ = ("ap",)

    def __init__(self, name, ap):
        super().__init__(name)
        self.ap = ap


class Builder:
    def __init__(self, es, nc, ctx=None):
        self.nc, self.es = nc, es
        self.sb = {}
        self.buf = {}
        self.ps_ap = None
        self.scopes = []
        self.scope_bufs = [[]]
        self.ctx = ctx

    def push_scope(self):
        st = ExitStack()
        self.scopes.append(st)
        self.scope_bufs.append([])
        return st

    def pop_scope(self):
        self.scopes.pop().close()
        for b in self.scope_bufs.pop():
            if b.dstream is not None and self.ctx is not None:
                self.ctx.free_dstreams.append(b.dstream)
                b.dstream = None

    def sbuf(self, name, shape, dtype):
        stack = self.scopes[-1] if self.scopes else self.es
        t = stack.enter_context(self.nc.sbuf_tensor(name, shape, dtype))
        self.sb[name] = t
        b = Buf(name)
        self.buf[name] = b
        self.scope_bufs[-1].append(b)
        return t

    def bufonly(self, name):
        self.buf[name] = Buf(name)
        return self.buf[name]

    def psum_main(self, cols=2048):
        t = self.es.enter_context(self.nc.psum_tensor("psmain", [128, cols], F32))
        self.ps_ap = t[:]
        self.buf["pst0"] = Buf("pst0")
        self.buf["pst1"] = Buf("pst1")
        return t

    def dram(self, name, shape, dtype, kind):
        t = self.nc.dram_tensor(name, list(shape), dtype, kind=kind)
        return DramBuf(name, t.ap())

    def ln_bufs(self, names):
        for sl in range(2):
            self.sbuf(f"{names}xt{sl}", [128, D], F32)
            self.sbuf(f"{names}xnb{sl}", [128, D], BF16)
        for sl in range(8):
            self.sbuf(f"{names}to{sl}", [128, 128], BF16)
        self.sbuf(f"{names}sq", [128, D], F32)
        self.sbuf(f"{names}mom", [128, 8], F32)


def mm_split(ctx, ps_ap, lhsT_ap, rhs_ap, ins, outs, start, stop, nmax=512):
    n = rhs_ap.shape[-1]
    for o in range(0, n, nmax):
        w = min(nmax, n - o)
        ctx.op("tensor", lambda e, o=o, w=w: e.matmul(
            ps_ap[:, o:o + w], lhsT_ap, rhs_ap[:, o:o + w],
            start=start, stop=stop), ins=ins, outs=outs)


def emit_w0prep(ctx, B, maskT_dram, wtp_dram, p4_dram, sel4_dram, w0t_dram):
    sb, bf = B.sb, B.buf
    p4 = B.sbuf("p4", [128, 32, 128], F32)
    sel4 = B.sbuf("sel4", [128, 4, 128], BF16)
    ctx.dma("sync", p4[:], p4_dram.ap.rearrange("a p b -> p a b"),
            bf["p4"], ins=[p4_dram], outs=[bf["p4"]])
    ctx.dma("sync", sel4[:], sel4_dram.ap.rearrange("a p b -> p a b"),
            bf["sel4"], ins=[sel4_dram], outs=[bf["sel4"]])
    for sl in range(2):
        B.sbuf(f"w2m{sl}", [128, D], I32)
        B.sbuf(f"w2mf{sl}", [128, D], F32)
        B.sbuf(f"w2w{sl}", [128, D], F32)
        B.sbuf(f"w2o{sl}", [128, D], BF16)
    bmf = B.sbuf("bmf", [128, D], BF16)
    B.bufonly("pbs0")
    B.bufonly("pbs1")
    B.bufonly("pem0")
    B.bufonly("pem1")
    for Tt in range(32):
        sl = Tt % 2
        mt, mtb = sb[f"w2m{sl}"], bf[f"w2m{sl}"]
        mf, mfb = sb[f"w2mf{sl}"], bf[f"w2mf{sl}"]
        ctx.dma("sync", mt[:], maskT_dram.ap[Tt, :, :], mtb,
                ins=[maskT_dram], outs=[mtb])
        ctx.op("vector", lambda e, mt=mt, mf=mf: e.tensor_copy(mf[:], mt[:]),
               ins=[mtb], outs=[mfb])
        for h in range(2):
            mm_split(ctx, B.ps_ap[:, 2048 * h:2048 * h + 2048], p4[:, Tt, :],
                     mf[:, 2048 * h:2048 * h + 2048],
                     ins=[mfb, bf["p4"]], outs=[bf[f"pbs{h}"]],
                     start=(Tt == 0), stop=(Tt == 31))
    ctx.op("vector", lambda e: e.tensor_scalar(
        bmf[:, 0:2048], B.ps_ap[:, 0:2048], 0.0, None, ALU.is_gt),
        ins=[bf["pbs0"]], outs=[bf["bmf"]])
    ctx.op("vector", lambda e: e.tensor_scalar(
        bmf[:, 2048:4096], B.ps_ap[:, 2048:4096], 0.0, None, ALU.is_gt),
        ins=[bf["pbs1"]], outs=[bf["bmf"]])
    for kk in range(32):
        sl = kk % 2
        wt, wtb = sb[f"w2w{sl}"], bf[f"w2w{sl}"]
        wo, wob = sb[f"w2o{sl}"], bf[f"w2o{sl}"]
        ctx.dma("sync", wt[:], wtp_dram.ap[kk, :, :], wtb,
                ins=[wtp_dram], outs=[wtb])
        b = kk % 4
        for h in range(2):
            pe0 = bf[f"pem{h}"]
            mm_split(ctx, B.ps_ap[:, 2048 * h:2048 * h + 2048],
                     sel4[:, b, :], bmf[:, 2048 * h:2048 * h + 2048],
                     ins=[bf["bmf"], bf["sel4"]], outs=[pe0],
                     start=True, stop=True)
            ctx.op("vector", lambda e, wt=wt, wo=wo, h=h: e.tensor_tensor(
                wo[:, 2048 * h:2048 * h + 2048],
                wt[:, 2048 * h:2048 * h + 2048],
                B.ps_ap[:, 2048 * h:2048 * h + 2048], ALU.mult),
                ins=[wtb, pe0], outs=[wob])
        ctx.dma("sync", w0t_dram.ap[kk, :, :], wo[:], wob,
                ins=[wob], outs=[w0t_dram])


def emit_resident_load(ctx, B, xnp_dram, name="xr"):
    xr = B.sbuf(name, [128, ND, T], BF16)
    ctx.dma("sync", xr[:, :, :],
            xnp_dram.ap.rearrange("k p t -> p k t"), B.buf[name],
            ins=[xnp_dram], outs=[B.buf[name]])
    return xr


def emit_s0(ctx, B, w0t_dram, s0_dram):
    sb, bf = B.sb, B.buf
    xr = sb["xr"]
    B.bufonly("ps30")
    B.bufonly("ps3t0")
    B.bufonly("ps3t1")
    for sl in range(2):
        B.sbuf(f"s3W{sl}", [128, 32, 128], BF16)
    for sl in range(2):
        B.sbuf(f"s3e{sl}", [128, 2048], F32)
    for sl in range(8):
        B.sbuf(f"s3o{sl}", [128, 128], F32)
    for nc_ in range(ND):
        ps = B.ps_ap[:, 0:2048]
        psb = bf["ps30"]
        sl = nc_ % 2
        wt, wtb = sb[f"s3W{sl}"], bf[f"s3W{sl}"]
        ctx.dma(["sync", "scalar"][nc_ % 2], wt[:],
                w0t_dram.ap[:, :, nc_ * 128:(nc_ + 1) * 128].rearrange(
                    "k p c -> p k c"),
                wtb, ins=[w0t_dram], outs=[wtb])
        for kk in range(ND):
            mm_split(ctx, ps, wt[:, kk, :], xr[:, kk, :],
                     ins=[wtb, bf["xr"]], outs=[psb],
                     start=(kk == 0), stop=(kk == 31))
        sl2 = nc_ % 2
        ev, evb = sb[f"s3e{sl2}"], bf[f"s3e{sl2}"]
        ctx.op("vector", lambda e, ev=ev, ps=ps: e.tensor_copy(ev[:], ps),
               ins=[psb], outs=[evb])
        for tc in range(16):
            psl = tc % 2
            sl8 = tc % 8
            pst = B.ps_ap[:, 2048 + 512 * psl:2048 + 512 * psl + 128]
            pstb = bf[f"ps3t{psl}"]
            so, sob = sb[f"s3o{sl8}"], bf[f"s3o{sl8}"]
            ctx.op("tensor", lambda e, ev=ev, tc=tc, pst=pst: e.transpose(
                pst, ev[:, tc * 128:(tc + 1) * 128], sb["identf"][:]),
                ins=[evb, bf["identf"]], outs=[pstb])
            ctx.op("vector", lambda e, so=so, pst=pst: e.tensor_copy(
                so[:], pst), ins=[pstb], outs=[sob])
            ctx.dma("sync",
                    s0_dram.ap[tc * 128:(tc + 1) * 128, nc_ * 128:(nc_ + 1) * 128],
                    so[:], sob, ins=[sob], outs=[s0_dram])


CNTS = [128] * 10 + [96]


def emit_gather(ctx, B, cb_dram, idx_dram, sgu_dram, sd_dram, core,
                dma_eng="sync"):
    sb, bf = B.sb, B.buf
    ctx.gather_streams = getattr(ctx, "gather_streams", set())
    cbsrc = cb_dram.ap
    idxt = B.sbuf("g4idx", [128, 4096], I32)
    for sl in range(2):
        B.sbuf(f"g4d{sl}", [128, 128, 8], F32)
        B.sbuf(f"g4s{sl}", [128, 8, 128], BF16)
    call = 0
    seg = 0
    idx_chunk = -1
    last_reorder = [0, 0]

    def idx_col(k):
        nonlocal idx_chunk
        cc = k // 4096
        if cc != idx_chunk:
            w = min(4096, idx_dram.ap.shape[1] - cc * 4096)
            ctx.dma(dma_eng, idxt[:, 0:w],
                    idx_dram.ap[:, cc * 4096:cc * 4096 + w], bf["g4idx"],
                    ins=[idx_dram], outs=[bf["g4idx"]])
            ctx.gather_streams.add(bf["g4idx"].dstream)
            idx_chunk = cc
        return idxt[:, (k % 4096):(k % 4096) + 1]

    def do_segment(ncalls, npart, store_fn):
        nonlocal call, seg
        sl = seg % 2
        seg += 1
        dstb, dstbb = sb[f"g4d{sl}"], bf[f"g4d{sl}"]
        stg, stgb = sb[f"g4s{sl}"], bf[f"g4s{sl}"]
        ds = ctx.dstream_for(dstbb)
        ctx.gather_streams.add(ds)
        for s in range(ncalls):
            ew = [(ctx.v, last_reorder[sl])] if s == 0 and last_reorder[sl] else ()
            ctx.gather(dstb[0:npart, s, :], cbsrc, idx_col(call + s),
                       key=dstbb, ins=[cb_dram, bf["g4idx"]], outs=(),
                       extra_waits=ew)
        src = dstb[0:128, 0:ncalls, 0:8].rearrange("p s j -> p j s")
        ctx.op("vector", lambda e, src=src, stg=stg, ncalls=ncalls:
               e.tensor_copy(stg[:, :, 0:ncalls], src),
               ins=[], outs=[stgb], extra_waits=[(ds, ds.n)])
        last_reorder[sl] = ctx.v.n
        store_fn(stg, stgb, ncalls)
        call += ncalls

    for m in range(2):
        for cdc in range(11):
            cnt = CNTS[cdc]
            for b in range(4):
                def store_gu(stg, stgb, ncalls, m=m, cdc=cdc, b=b, cnt=cnt):
                    for j in range(8):
                        ctx.dma(dma_eng,
                                sgu_dram.ap[m, cdc, :, b, j, 0:cnt],
                                stg[:, j, 0:cnt], stgb,
                                ins=[stgb], outs=[sgu_dram])
                        ctx.gather_streams.add(stgb.dstream)
                do_segment(cnt, 128, store_gu)
    for dci in range(4):
        for cdc in range(11):
            cnt = CNTS[cdc]
            def store_d(stg, stgb, ncalls, dci=dci, cdc=cdc, cnt=cnt):
                for j in range(8):
                    ctx.dma(dma_eng,
                            sd_dram.ap[dci, 0:cnt, j, cdc, :],
                            stg[0:cnt, j, 0:128], stgb,
                            ins=[stgb], outs=[sd_dram])
                    ctx.gather_streams.add(stgb.dstream)
            do_segment(128, cnt, store_d)


def host_idxcalls(idx_gate, idx_up, idx_down, core):
    cols = []
    jj = core
    for idx_m in (idx_gate, idx_up):
        for cdc in range(11):
            cnt = CNTS[cdc]
            for b in range(4):
                for s in range(cnt):
                    row = 8 * (cdc * 128 + s) + jj
                    cols.append(idx_m[row, 128 * b:128 * b + 128])
    for dci in range(4):
        dc = 4 * core + dci
        for cdc in range(11):
            cnt = CNTS[cdc]
            for s in range(128):
                d = dc * 128 + s
                col = np.zeros(128, dtype=np.int32)
                col[0:cnt] = idx_down[d, 128 * cdc:128 * cdc + cnt]
                cols.append(col)
    arr = np.stack(cols, axis=1).astype(np.int32)
    pad = (-arr.shape[1]) % 4096
    if pad:
        arr = np.concatenate([arr, np.zeros((128, pad), np.int32)], axis=1)
    return np.ascontiguousarray(arr)


def emit_ffn_up(ctx, B, sgu_dram, h_dram, bg=None):
    sb, bf = B.sb, B.buf
    xr = sb["xr"]
    for sl in range(2):
        for m in range(2):
            B.sbuf(f"f5w{m}{sl}", [128, 32, 128], BF16)
        B.sbuf(f"f5h{sl}", [128, T], BF16)
    B.sbuf("f5a", [128, T], F32)
    B.bufonly("ps5g")
    B.bufonly("ps5u")
    for jj in range(8):
        for cdc in range(11):
            if bg is not None:
                bg()
            iic = jj * 11 + cdc
            cnt = CNTS[cdc]
            sl = iic % 2
            for m in range(2):
                w, wb = sb[f"f5w{m}{sl}"], bf[f"f5w{m}{sl}"]
                if len(sgu_dram.ap.shape) == 7:
                    src_ap = sgu_dram.ap[jj, m, cdc].rearrange(
                        "c b j i -> c (b j) i")
                else:
                    src_ap = sgu_dram.ap[m, iic].rearrange(
                        "c b j i -> c (b j) i")
                ctx.dma("sync", w[:], src_ap,
                        wb, ins=[sgu_dram], outs=[wb])
                ps = B.ps_ap[:, 2048 * m:2048 * m + 2048]
                psb = bf["ps5g"] if m == 0 else bf["ps5u"]
                for b in range(4):
                    for j in range(8):
                        kk = j * 4 + b
                        mm_split(ctx, ps[0:cnt, :], w[:, b * 8 + j, 0:cnt],
                                 xr[:, kk, :], ins=[wb, bf["xr"]], outs=[psb],
                                 start=(b == 0 and j == 0),
                                 stop=(b == 3 and j == 7))
            act, actb = sb["f5a"], bf["f5a"]
            hb, hbb = sb[f"f5h{sl}"], bf[f"f5h{sl}"]
            ctx.op("scalar", lambda e, act=act, cnt=cnt: e.activation(
                act[0:cnt, :], B.ps_ap[0:cnt, 0:2048], AF.Silu),
                ins=[bf["ps5g"]], outs=[actb])
            ctx.op("vector", lambda e, act=act, hb=hb, cnt=cnt: e.tensor_tensor(
                hb[0:cnt, :], act[0:cnt, :], B.ps_ap[0:cnt, 2048:4096],
                ALU.mult), ins=[actb, bf["ps5u"]], outs=[hbb])
            ctx.dma("sync", h_dram.ap[iic, 0:cnt, :], hb[0:cnt, :], hbb,
                    ins=[hbb], outs=[h_dram])


def emit_ffn_down(ctx, B, sd_dram, h_dram, x_dram, s0_dram,
                  x1_dram, bg=None, dc_order=None, dc_waits=None):
    sb, bf = B.sb, B.buf
    hq = B.sbuf("f6h", [128, 88, 512], BF16)
    for sl in range(2):
        B.sbuf(f"f6w{sl}", [128, 88, 128], BF16)
        B.sbuf(f"f6g{sl}", [128, 512], F32)
        B.bufonly(f"ps6y{sl}")
        B.bufonly(f"ps6t{sl}")
    for sl in range(8):
        B.sbuf(f"f6x{sl}", [128, 128], F32)
        B.sbuf(f"f6s{sl}", [128, 128], F32)
        B.sbuf(f"f6o{sl}", [128, 128], F32)
        B.sbuf(f"f6t{sl}", [128, 128], F32)
    for tq in range(4):
        ctx.dma("sync", hq[:, :, :],
                h_dram.ap[:, :, tq * 512:(tq + 1) * 512].rearrange(
                    "i p t -> p i t"), bf["f6h"],
                ins=[h_dram], outs=[bf["f6h"]])
        for dci_, dc in enumerate(dc_order or range(32)):
            if bg is not None:
                bg()
            sl = dci_ % 2
            w, wb = sb[f"f6w{sl}"], bf[f"f6w{sl}"]
            if len(sd_dram.ap.shape) == 6:
                src_ap = sd_dram.ap[dc // 4, dc % 4].rearrange(
                    "p j c d -> p (j c) d")
            else:
                src_ap = sd_dram.ap[dc].rearrange("p j c d -> p (j c) d")
            ew = tuple(dc_waits.get(dc, ())) if dc_waits else ()
            ctx.dma("sync", w[:], src_ap,
                    wb, ins=([] if dc_waits else [sd_dram]), outs=[wb],
                    extra_waits=ew)
            ps = B.ps_ap[:, 512 * sl:512 * sl + 512]
            psb = bf[f"ps6y{sl}"]
            for jj in range(8):
                for cdc in range(11):
                    i88 = jj * 11 + cdc
                    cnt = CNTS[cdc]
                    ctx.op("tensor", lambda e, w=w, i88=i88, cnt=cnt, ps=ps:
                           e.matmul(ps, w[0:cnt, i88, :], hq[0:cnt, i88, :],
                                    start=(i88 == 0), stop=(i88 == 87)),
                           ins=[wb, bf["f6h"]], outs=[psb])
            g0, g0b = sb[f"f6g{sl}"], bf[f"f6g{sl}"]
            ctx.op("scalar", lambda e, g0=g0, ps=ps: e.activation(
                g0[:], ps, AF.Silu), ins=[psb], outs=[g0b])
            for ts in range(4):
                sl8 = (dci_ * 4 + ts) % 8
                pst = B.ps_ap[:, 1024 + 512 * sl:1024 + 512 * sl + 128]
                pstb = bf[f"ps6t{sl}"]
                gt, gtb = sb[f"f6t{sl8}"], bf[f"f6t{sl8}"]
                xt, xtb = sb[f"f6x{sl8}"], bf[f"f6x{sl8}"]
                st, stb = sb[f"f6s{sl8}"], bf[f"f6s{sl8}"]
                ot, otb = sb[f"f6o{sl8}"], bf[f"f6o{sl8}"]
                rows = slice((tq * 4 + ts) * 128, (tq * 4 + ts) * 128 + 128)
                cols = slice(dc * 128, dc * 128 + 128)
                ctx.op("tensor", lambda e, g0=g0, ts=ts, pst=pst: e.transpose(
                    pst, g0[:, ts * 128:(ts + 1) * 128], sb["identf"][:]),
                    ins=[g0b, bf["identf"]], outs=[pstb])
                ctx.op("vector", lambda e, gt=gt, pst=pst: e.tensor_copy(
                    gt[:], pst), ins=[pstb], outs=[gtb])
                ctx.dma("sync", xt[:], x_dram.ap[rows, cols], xtb,
                        ins=[x_dram], outs=[xtb])
                ctx.dma("sync", st[:], s0_dram.ap[rows, cols], stb,
                        ins=[s0_dram], outs=[stb])
                ctx.op("vector", lambda e, gt=gt, st=st: e.tensor_tensor(
                    gt[:], gt[:], st[:], ALU.mult), ins=[gtb, stb], outs=[gtb])
                ctx.op("vector", lambda e, gt=gt, xt=xt, ot=ot: e.tensor_tensor(
                    ot[:], gt[:], xt[:], ALU.add), ins=[gtb, xtb], outs=[otb])
                ctx.dma("sync", x1_dram.ap[rows, cols], ot[:], otb,
                        ins=[otb], outs=[x1_dram])


def emit_gather_fused(ctx, B, cb_dram, idx_dram, rgu, rd, agu, agd,
                      agst, agu_bufs, agd_bufs):
    nc = ctx.nc
    es = ctx.es
    idxt = B.sbuf("gfidx", [128, 2, 1024], I32)
    d0 = B.sbuf("gfd0", [128, 128, 8], F32)
    d1 = B.sbuf("gfd1", [128, 128, 8], F32)
    slots = [d0, d1]
    sg = [Stream(es, nc, "gfg0"), Stream(es, nc, "gfg1")]
    ss = [Stream(es, nc, "gfs0"), Stream(es, nc, "gfs1")]
    si = Stream(es, nc, "gfi")
    cbsrc = cb_dram.ap
    nchunks = idx_dram.ap.shape[1] // 1024
    state = {"call": 0, "chunk": -1, "seg": 0, "si_waited": 0, "loaded": 0}

    def load_chunk(cc):
        def fl(e, cc=cc):
            e.dma_start(out=idxt[:, cc % 2, :],
                        in_=idx_dram.ap[:, cc * 1024:cc * 1024 + 1024]
                        ).then_inc(si.sem, 16)
        ctx.emit("gpsimd", fl)
        si.n += 16
        state["loaded"] = cc + 1

    load_chunk(0)
    load_chunk(1)

    def idx_col(k, npart):
        cc = k // 1024
        if cc != state["chunk"]:
            state["chunk"] = cc
            if state["loaded"] < nchunks and state["loaded"] <= cc + 1:
                load_chunk(state["loaded"])
        return (idxt[0:npart, cc % 2, (k % 1024):(k % 1024) + 1],
                16 * (cc + 1))

    def do_segment(ncalls, npart, dst_ap):
        sl = state["seg"] % 2
        state["seg"] += 1
        dstb = slots[sl]
        base = state["call"]
        for s in range(ncalls):
            icol, sival = idx_col(base + s, npart)
            waits = []
            if sival > state["si_waited"]:
                waits.append((si.sem, sival))
                state["si_waited"] = sival
            if s == 0 and ss[sl].n > 0:
                waits.append((ss[sl].sem, ss[sl].n))

            def fg(e, icol=icol, dst=dstb[0:npart, s, :], waits=tuple(waits),
                   sl=sl):
                for sem, v in waits:
                    e.wait_ge(sem, v)
                e.indirect_dma_start(
                    dst, None, cbsrc,
                    bass.IndirectOffsetOnAxis(icol, 0)).then_inc(
                        sg[sl].sem, 16)
            ctx.emit("gpsimd", fg)
            sg[sl].n += 16
        state["call"] += ncalls

        def fs(e, dst_ap=dst_ap, src=(dstb[:, 0:ncalls, :] if npart == 128
                                      else dstb[0:npart, :, :]),
               gv=sg[sl].n, sl=sl):
            e.wait_ge(sg[sl].sem, gv)
            e.dma_start(out=dst_ap, in_=src).then_inc(ss[sl].sem, 16)
        ctx.emit("gpsimd", fs)
        ss[sl].n += 16

    def emit_ag(in2d, out2d, out_buf):
        w = [(ss[0].sem, ss[0].n), (ss[1].sem, ss[1].n)]
        agst.n += 1

        def fc(e, in2d=in2d, out2d=out2d, w=tuple(w)):
            for sem, v in w:
                e.wait_ge(sem, v)
            e.collective_compute(
                "AllGather", ALU.bypass,
                replica_groups=[list(range(8))],
                ins=[in2d.opt()], outs=[out2d.opt()]).then_inc(agst.sem, 1)
        ctx.emit("gpsimd", fc)
        out_buf.writer = (agst, agst.n)
        out_buf.readers = {}

    for cdc in range(11):
        cnt = CNTS[cdc]
        for m in range(2):
            for b in range(4):
                do_segment(cnt, 128, rgu.ap[cdc, m, b, :, 0:cnt, :])
        agu_h = agu[0] if cdc < 6 else agu[1]
        emit_ag(rgu.ap[cdc].rearrange("m b c s j -> (m b c) (s j)"),
                agu_h.ap[cdc if cdc < 6 else cdc - 6].rearrange(
                    "r u c s j -> (r u c) (s j)"),
                agu_bufs[cdc])
    for dci in range(4):
        for cdc in range(11):
            cnt = CNTS[cdc]
            do_segment(128, cnt, rd.ap[dci, cdc, 0:cnt, :, :])
        emit_ag(rd.ap[dci].rearrange("c p s j -> (c p) (s j)"),
                agd.ap[dci].rearrange("r c p s j -> (r c p) (s j)"),
                agd_bufs[dci])


def host_idxcalls_f(idx_gate, idx_up, idx_down, core):
    cols = []
    jj = core
    for cdc in range(11):
        cnt = CNTS[cdc]
        for idx_m in (idx_gate, idx_up):
            for b in range(4):
                for s in range(cnt):
                    row = 8 * (cdc * 128 + s) + jj
                    cols.append(idx_m[row, 128 * b:128 * b + 128])
    for dci in range(4):
        dc = 4 * core + dci
        for cdc in range(11):
            cnt = CNTS[cdc]
            for s in range(128):
                d = dc * 128 + s
                col = np.zeros(128, dtype=np.int32)
                col[0:cnt] = idx_down[d, 128 * cdc:128 * cdc + cnt]
                cols.append(col)
    arr = np.stack(cols, axis=1).astype(np.int32)
    pad = (-arr.shape[1]) % 2048
    if pad:
        arr = np.concatenate([arr, np.zeros((128, pad), np.int32)], axis=1)
    return np.ascontiguousarray(arr)


def emit_ffn_up_raw(ctx, B, agu, agu_bufs, h_dram):
    sb, bf = B.sb, B.buf
    xr = sb["xr"]
    for sl in range(2):
        B.sbuf(f"u5r{sl}", [128, 128, 8], F32)
        B.sbuf(f"u5w{sl}", [128, 8, 128], BF16)
        B.sbuf(f"f5h{sl}", [128, T], BF16)
    B.sbuf("f5a", [128, T], F32)
    B.bufonly("ps5g")
    B.bufonly("ps5u")
    unit = 0
    nchunk = 0
    for cdc in range(11):
        for jj in range(8):
            iic = jj * 11 + cdc
            cnt = CNTS[cdc]
            hsl = nchunk % 2
            nchunk += 1
            for m in range(2):
                ps = B.ps_ap[:, 2048 * m:2048 * m + 2048]
                psb = bf["ps5g"] if m == 0 else bf["ps5u"]
                for b in range(4):
                    sl = unit % 2
                    unit += 1
                    fr, frb = sb[f"u5r{sl}"], bf[f"u5r{sl}"]
                    w2, w2b = sb[f"u5w{sl}"], bf[f"u5w{sl}"]
                    agu_h = agu[0] if cdc < 6 else agu[1]
                    ctx.dma(["sync", "scalar"][unit % 2], fr[:, 0:cnt, :],
                            agu_h.ap[cdc if cdc < 6 else cdc - 6, jj,
                                     m * 4 + b, :, 0:cnt, :], frb,
                            ins=[agu_bufs[cdc]], outs=[frb])
                    ctx.op("vector", lambda e, fr=fr, w2=w2, cnt=cnt:
                           e.tensor_copy(w2[:, :, 0:cnt],
                                         fr[:, 0:cnt, :].rearrange(
                                             "c s j -> c j s")),
                           ins=[frb], outs=[w2b])
                    for j in range(8):
                        kk = j * 4 + b
                        mm_split(ctx, ps[0:cnt, :], w2[:, j, 0:cnt],
                                 xr[:, kk, :], ins=[w2b, bf["xr"]], outs=[psb],
                                 start=(b == 0 and j == 0),
                                 stop=(b == 3 and j == 7))
            act, actb = sb["f5a"], bf["f5a"]
            hb, hbb = sb[f"f5h{hsl}"], bf[f"f5h{hsl}"]
            ctx.op("scalar", lambda e, act=act, cnt=cnt: e.activation(
                act[0:cnt, :], B.ps_ap[0:cnt, 0:2048], AF.Silu),
                ins=[bf["ps5g"]], outs=[actb])
            ctx.op("vector", lambda e, act=act, hb=hb, cnt=cnt: e.tensor_tensor(
                hb[0:cnt, :], act[0:cnt, :], B.ps_ap[0:cnt, 2048:4096],
                ALU.mult), ins=[actb, bf["ps5u"]], outs=[hbb])
            ctx.dma("sync", h_dram.ap[iic, 0:cnt, :], hb[0:cnt, :], hbb,
                    ins=[hbb], outs=[h_dram])


def make_dcast(ctx, B, agd, agd_bufs, sdF, dc_waits):
    sb, bf = B.sb, B.buf
    for sl in range(2):
        B.sbuf(f"dc6r{sl}", [128, 128, 8], F32)
        B.sbuf(f"dc6w{sl}", [128, 8, 128], BF16)
    B.bufonly("dc6st")
    state = {"u": 0}
    stb = bf["dc6st"]

    def group(dci):
        for rank in range(8):
            dc = 4 * rank + dci
            for cdc in range(11):
                cnt = CNTS[cdc]
                sl = state["u"] % 2
                state["u"] += 1
                fr, frb = sb[f"dc6r{sl}"], bf[f"dc6r{sl}"]
                w2, w2b = sb[f"dc6w{sl}"], bf[f"dc6w{sl}"]
                ctx.dma("sync", fr[0:cnt, :, :],
                        agd.ap[dci, rank, cdc, 0:cnt, :, :], frb,
                        ins=[agd_bufs[dci]], outs=[frb])
                ctx.op("vector", lambda e, fr=fr, w2=w2, cnt=cnt:
                       e.tensor_copy(w2[0:cnt, :, :],
                                     fr[0:cnt, :, :].rearrange(
                                         "c d j -> c j d")),
                       ins=[frb], outs=[w2b])
                ctx.dma("sync", sdF.ap[dc, 0:cnt, :, cdc, :],
                        w2[0:cnt, :, :], stb, ins=[w2b], outs=[sdF])
            st = ctx.dstream_for(stb)
            dc_waits[dc] = [(st, st.n)]
    return group


def make_ffn_down(ctx, B, sd_dram, h_dram, x_dram, s0_dram, x1_dram,
                  dc_waits):
    sb, bf = B.sb, B.buf
    hq = B.sbuf("f6h", [128, 88, 512], BF16)
    for sl in range(2):
        B.sbuf(f"f6w{sl}", [128, 88, 128], BF16)
        B.sbuf(f"f6g{sl}", [128, 512], F32)
        B.bufonly(f"ps6y{sl}")
        B.bufonly(f"ps6t{sl}")
    for sl in range(8):
        B.sbuf(f"f6x{sl}", [128, 128], F32)
        B.sbuf(f"f6s{sl}", [128, 128], F32)
        B.sbuf(f"f6o{sl}", [128, 128], F32)
        B.sbuf(f"f6t{sl}", [128, 128], F32)
    state = {"tq": -1, "u": 0}

    def part(tq, dcs):
        if tq != state["tq"]:
            ctx.dma("sync", hq[:, :, :],
                    h_dram.ap[:, :, tq * 512:(tq + 1) * 512].rearrange(
                        "i p t -> p i t"), bf["f6h"],
                    ins=[h_dram], outs=[bf["f6h"]])
            state["tq"] = tq
        for dc in dcs:
            u = state["u"]
            state["u"] += 1
            sl = u % 2
            w, wb = sb[f"f6w{sl}"], bf[f"f6w{sl}"]
            src_ap = sd_dram.ap[dc].rearrange("p j c d -> p (j c) d")
            ew = tuple(dc_waits.get(dc, ()))
            ctx.dma("sync", w[:], src_ap, wb, ins=[], outs=[wb],
                    extra_waits=ew)
            ps = B.ps_ap[:, 512 * sl:512 * sl + 512]
            psb = bf[f"ps6y{sl}"]
            for jj in range(8):
                for cdc in range(11):
                    i88 = jj * 11 + cdc
                    cnt = CNTS[cdc]
                    ctx.op("tensor", lambda e, w=w, i88=i88, cnt=cnt, ps=ps:
                           e.matmul(ps, w[0:cnt, i88, :], hq[0:cnt, i88, :],
                                    start=(i88 == 0), stop=(i88 == 87)),
                           ins=[wb, bf["f6h"]], outs=[psb])
            g0, g0b = sb[f"f6g{sl}"], bf[f"f6g{sl}"]
            ctx.op("scalar", lambda e, g0=g0, ps=ps: e.activation(
                g0[:], ps, AF.Silu), ins=[psb], outs=[g0b])
            for ts in range(4):
                sl8 = (u * 4 + ts) % 8
                pst = B.ps_ap[:, 1024 + 512 * sl:1024 + 512 * sl + 128]
                pstb = bf[f"ps6t{sl}"]
                gt, gtb = sb[f"f6t{sl8}"], bf[f"f6t{sl8}"]
                xt, xtb = sb[f"f6x{sl8}"], bf[f"f6x{sl8}"]
                st, stb = sb[f"f6s{sl8}"], bf[f"f6s{sl8}"]
                ot, otb = sb[f"f6o{sl8}"], bf[f"f6o{sl8}"]
                rows = slice((tq * 4 + ts) * 128, (tq * 4 + ts) * 128 + 128)
                cols = slice(dc * 128, dc * 128 + 128)
                ctx.op("tensor", lambda e, g0=g0, ts=ts, pst=pst: e.transpose(
                    pst, g0[:, ts * 128:(ts + 1) * 128], sb["identf"][:]),
                    ins=[g0b, bf["identf"]], outs=[pstb])
                ctx.op("vector", lambda e, gt=gt, pst=pst: e.tensor_copy(
                    gt[:], pst), ins=[pstb], outs=[gtb])
                ctx.dma("sync", xt[:], x_dram.ap[rows, cols], xtb,
                        ins=[x_dram], outs=[xtb])
                ctx.dma("sync", st[:], s0_dram.ap[rows, cols], stb,
                        ins=[s0_dram], outs=[stb])
                ctx.op("vector", lambda e, gt=gt, st=st: e.tensor_tensor(
                    gt[:], gt[:], st[:], ALU.mult), ins=[gtb, stb], outs=[gtb])
                ctx.op("vector", lambda e, gt=gt, xt=xt, ot=ot: e.tensor_tensor(
                    ot[:], gt[:], xt[:], ALU.add), ins=[gtb, xtb], outs=[otb])
                ctx.dma("sync", x1_dram.ap[rows, cols], ot[:], otb,
                        ins=[otb], outs=[x1_dram])
    return part



def make_cast_bg(ctx, B, pairs, shared="cg"):
    for sl in range(2):
        B.sbuf(f"{shared}f{sl}", [128, 1024], F32)
        B.sbuf(f"{shared}b{sl}", [128, 1024], BF16)
    units = []
    for src, dst in pairs:
        n, prt, mcols = src.ap.shape
        for i in range(n):
            for c0 in range(0, mcols, 1024):
                units.append((src, dst, i, c0, min(1024, mcols - c0), prt))
    state = {"k": 0}

    def bg():
        k = state["k"]
        if k >= len(units):
            return False
        state["k"] = k + 1
        src, dst, i, c0, w, prt = units[k]
        sl = k % 2
        f, fb = B.sb[f"{shared}f{sl}"], B.buf[f"{shared}f{sl}"]
        b, bb = B.sb[f"{shared}b{sl}"], B.buf[f"{shared}b{sl}"]
        ctx.dma("sync", f[0:prt, 0:w], src.ap[i, :, c0:c0 + w], fb,
                ins=[src], outs=[fb])
        ctx.op("vector", lambda e, f=f, b=b, prt=prt, w=w:
               e.tensor_copy(b[0:prt, 0:w], f[0:prt, 0:w]),
               ins=[fb], outs=[bb])
        ctx.dma("sync", dst.ap[i, :, c0:c0 + w], b[0:prt, 0:w], bb,
                ins=[bb], outs=[dst])
        return True
    return bg


def emit_cast_bf16(ctx, B, src_dram, dst_dram, name, shared="cast"):
    n, prt, mcols = src_dram.ap.shape
    if f"{shared}f0" not in B.sb:
        for sl in range(2):
            B.sbuf(f"{shared}f{sl}", [128, 4096], F32)
            B.sbuf(f"{shared}b{sl}", [128, 4096], BF16)
    for i in range(n):
        sl = i % 2
        f, fb = B.sb[f"{shared}f{sl}"], B.buf[f"{shared}f{sl}"]
        b, bb = B.sb[f"{shared}b{sl}"], B.buf[f"{shared}b{sl}"]
        ctx.dma("sync", f[0:prt, 0:mcols], src_dram.ap[i], fb,
                ins=[src_dram], outs=[fb])
        ctx.op("vector", lambda e, f=f, b=b, prt=prt, mcols=mcols:
               e.tensor_copy(b[0:prt, 0:mcols], f[0:prt, 0:mcols]),
               ins=[fb], outs=[bb])
        ctx.dma("sync", dst_dram.ap[i], b[0:prt, 0:mcols], bb,
                ins=[bb], outs=[dst_dram])


HT = 1024


def emit_attn_pre(ctx, B, W, xn1_dram):
    sb, bf = B.sb, B.buf
    qhat, khat, vtm = sb["qhat"], sb["khat"], sb["vtm"]
    halo = B.sbuf("a8halo", [128, 12, 2], BF16)
    halo2 = B.sbuf("a8halo2", [128, 10, 2], BF16)
    vsh = B.sbuf("a8vsh", [128, HT], BF16)
    for nm in ("p8a0", "p8a1", "p8b", "p8c", "p8t0", "p8t1"):
        B.bufonly(nm)
    swapm = B.sbuf("swapm_s", [128, 128], F32)
    ctx.dma("sync", swapm[:], W["swapm"].ap[:, :], bf["swapm_s"],
            ins=[W["swapm"]], outs=[bf["swapm_s"]])
    qkv = B.sbuf("a8qkv", [128, 2, HT], BF16)
    qp1 = B.sbuf("a8qp1", [128, 8, HT + 2], BF16)
    kp1 = B.sbuf("a8kp1", [128, 2, HT + 2], BF16)
    qp2 = B.sbuf("a8qp2", [128, 8, HT + 2], BF16)
    kp2 = B.sbuf("a8kp2", [128, 2, HT + 2], BF16)
    kmean = B.sbuf("a8km", [128, HT], BF16)
    qmean = B.sbuf("a8qm", [128, HT], BF16)
    qf = B.sbuf("a8qf", [128, 8, HT], BF16)
    kf = B.sbuf("a8kf", [128, 2, HT], BF16)
    rsq = B.sbuf("a8rsq", [128, HT], F32)
    rsk = B.sbuf("a8rsk", [128, HT], F32)
    rbc = B.sbuf("a8rbc", [128, HT], F32)
    tA = B.sbuf("a8tA", [128, HT], F32)
    tB = B.sbuf("a8tB", [128, HT], F32)
    tC = B.sbuf("a8tC", [128, HT], F32)
    cosf = B.sbuf("a8cos", [128, HT], F32)
    sinf = B.sbuf("a8sin", [128, HT], F32)
    ktl = B.sbuf("a8kt", [128, 1], F32)
    qc1b = B.sbuf("a8q1b", [128, 8], F32)
    qc2b = B.sbuf("a8q2b", [128, 8], F32)
    kc1b = B.sbuf("a8k1b", [128, 2], F32)
    kc2b = B.sbuf("a8k2b", [128, 2], F32)
    mkp = B.sbuf("a8mk", [128, 2, 64], BF16)
    mqp = B.sbuf("a8mq", [128, 8, 64], BF16)
    bqp = B.sbuf("a8bq", [128, 8, 128], BF16)
    bkp = B.sbuf("a8bk", [128, 2, 128], BF16)
    sqp = B.sbuf("a8sq", [128, 8, 16], F32)
    skp = B.sbuf("a8sk", [128, 2, 4], F32)
    rsbq = B.sbuf("a8rq", [128, 8, 128], F32)
    rsbk = B.sbuf("a8rk", [128, 2, 128], F32)
    wt = [B.sbuf(f"a8w{sl}", [128, 128], BF16) for sl in range(4)]
    wq2 = [B.sbuf(f"a8wq{sl}", [128, 32, 128], BF16) for sl in range(2)]
    wtf = [B.sbuf(f"a8wf{sl}", [128, 128], F32) for sl in range(2)]
    xt_ = [B.sbuf(f"a8x{sl}", [128, HT], BF16) for sl in range(4)]
    for nm, src in [("a8kt", "ktemp"), ("a8q1b", "qc1b"), ("a8q2b", "qc2b"),
                    ("a8k1b", "kc1b"), ("a8k2b", "kc2b")]:
        ctx.dma("sync", sb[nm][:], W[src].ap[:, :], bf[nm],
                ins=[W[src]], outs=[bf[nm]])
    for nm, src in [("a8mk", "mk"), ("a8mq", "mq"), ("a8bq", "bq"),
                    ("a8bk", "bk"), ("a8sq", "sq"), ("a8sk", "sk"),
                    ("a8rq", "rsbq"), ("a8rk", "rsbk")]:
        np_ = W[src].ap.shape[1]
        ctx.dma("sync", sb[nm][0:np_, :, :],
                W[src].ap.rearrange("a p b -> p a b"),
                bf[nm], ins=[W[src]], outs=[bf[nm]])

    def PS(idx, width=HT):
        return B.ps_ap[:, 1024 * idx:1024 * idx + width]

    for hf in range(2):
        t0 = hf * HT
        ctx.dma("sync", cosf[:, :], W["cosf"].ap[:, t0:t0 + HT],
                bf["a8cos"], ins=[W["cosf"]], outs=[bf["a8cos"]])
        ctx.dma("sync", sinf[:, :], W["sinf"].ap[:, t0:t0 + HT],
                bf["a8sin"], ins=[W["sinf"]], outs=[bf["a8sin"]])
        if hf == 1:
            ctx.op("vector", lambda e: e.tensor_copy(
                halo[:, 0:8, :], qp1[:, :, HT:HT + 2]), ins=[bf["a8qp1"]],
                outs=[bf["a8halo"]])
            ctx.op("vector", lambda e: e.tensor_copy(
                halo[:, 8:10, :], kp1[:, :, HT:HT + 2]), ins=[bf["a8kp1"]],
                outs=[bf["a8halo"]])
            ctx.op("vector", lambda e: e.tensor_copy(
                halo[:, 10:12, :], qkv[:, 0:2, HT - 2:HT]),
                ins=[bf["a8qkv"]], outs=[bf["a8halo"]])
        for nm in ("a8qp1", "a8kp1", "a8qp2", "a8kp2"):
            if hf == 0:
                ctx.op("vector", lambda e, nm=nm: e.memset(
                    sb[nm][:, :, 0:2], 0.0), ins=[], outs=[bf[nm]])
        if hf == 1:
            ctx.op("vector", lambda e: e.tensor_copy(
                halo2[:, 0:8, :], qp2[:, :, HT:HT + 2]),
                ins=[bf["a8qp2"]], outs=[bf["a8halo2"]])
            ctx.op("vector", lambda e: e.tensor_copy(
                halo2[:, 8:10, :], kp2[:, :, HT:HT + 2]),
                ins=[bf["a8kp2"]], outs=[bf["a8halo2"]])
            ctx.op("vector", lambda e: e.tensor_copy(
                qp1[:, :, 0:2], halo[:, 0:8, :]),
                ins=[bf["a8halo"]], outs=[bf["a8qp1"]])
            ctx.op("vector", lambda e: e.tensor_copy(
                kp1[:, :, 0:2], halo[:, 8:10, :]),
                ins=[bf["a8halo"]], outs=[bf["a8kp1"]])
            ctx.op("vector", lambda e: e.tensor_copy(
                qp2[:, :, 0:2], halo2[:, 0:8, :]),
                ins=[bf["a8halo2"]], outs=[bf["a8qp2"]])
            ctx.op("vector", lambda e: e.tensor_copy(
                kp2[:, :, 0:2], halo2[:, 8:10, :]),
                ins=[bf["a8halo2"]], outs=[bf["a8kp2"]])
        for oc in range(12):
            sl = oc % 2
            ps, psb = PS(sl), bf[f"p8a{sl}"]
            wq, wqb = sb[f"a8wq{sl}"], bf[f"a8wq{sl}"]
            ctx.dma(["sync", "scalar"][oc % 2], wq[:],
                    W["wqkvT_b"].ap[:, :, oc * 128:(oc + 1) * 128].rearrange(
                        "k p c -> p k c"), wqb,
                    ins=[W["wqkvT_b"]], outs=[wqb])
            for kk in range(32):
                xsl = (oc * 32 + kk) % 4
                x_, xb = xt_[xsl], bf[f"a8x{xsl}"]
                ctx.dma(["sync", "scalar", "gpsimd"][kk % 3], x_[:],
                        xn1_dram.ap[kk, :, t0:t0 + HT], xb,
                        ins=[xn1_dram], outs=[xb])
                mm_split(ctx, ps, wq[:, kk, :], x_[:], ins=[wqb, xb],
                         outs=[psb], start=(kk == 0), stop=(kk == 31))
            if oc < 8:
                dst, db = qp1[:, oc, 2:2 + HT], bf["a8qp1"]
            elif oc < 10:
                dst, db = kp1[:, oc - 8, 2:2 + HT], bf["a8kp1"]
            else:
                dst, db = qkv[:, oc - 10, :], bf["a8qkv"]
            ctx.op("vector", lambda e, dst=dst, ps=ps: e.tensor_copy(dst, ps),
                   ins=[psb], outs=[db])
        for c2 in range(2):
            mm_split(ctx, PS(2)[0:64, :], mkp[:, c2, :], kp1[:, c2, 2:2 + HT],
                     ins=[bf["a8mk"], bf["a8kp1"]], outs=[bf["p8b"]],
                     start=(c2 == 0), stop=(c2 == 1))
        ctx.op("vector", lambda e: e.tensor_copy(kmean[0:64, :], PS(2)[0:64, :]),
               ins=[bf["p8b"]], outs=[bf["a8km"]])
        for c8 in range(8):
            mm_split(ctx, PS(2)[0:64, :], mqp[:, c8, :], qp1[:, c8, 2:2 + HT],
                     ins=[bf["a8mq"], bf["a8qp1"]], outs=[bf["p8b"]],
                     start=(c8 == 0), stop=(c8 == 7))
        ctx.op("vector", lambda e: e.tensor_copy(qmean[0:64, :], PS(2)[0:64, :]),
               ins=[bf["p8b"]], outs=[bf["a8qm"]])
        for oc in range(8):
            sl = oc % 2
            ps, psb = PS(sl), bf[f"p8a{sl}"]
            for tap in range(3):
                wsl = (oc * 3 + tap) % 4
                w, wb = wt[wsl], bf[f"a8w{wsl}"]
                ctx.dma("sync", w[:], W["qc1bd_b"].ap[oc * 3 + tap], wb,
                        ins=[W["qc1bd_b"]], outs=[wb])
                mm_split(ctx, ps, w[:], qp1[:, oc, tap:tap + HT],
                         ins=[wb, bf["a8qp1"]], outs=[psb],
                         start=(tap == 0), stop=(tap == 2))
            ctx.op("scalar", lambda e, oc=oc, ps=ps: e.activation(
                qp2[:, oc, 2:2 + HT], ps, AF.Identity, bias=qc1b[:, oc:oc + 1]),
                ins=[psb, bf["a8q1b"]], outs=[bf["a8qp2"]])
        for oc in range(2):
            sl = oc % 2
            ps, psb = PS(sl), bf[f"p8a{sl}"]
            for tap in range(3):
                wsl = (oc * 3 + tap) % 4
                w, wb = wt[wsl], bf[f"a8w{wsl}"]
                ctx.dma("sync", w[:], W["kc1bd_b"].ap[oc * 3 + tap], wb,
                        ins=[W["kc1bd_b"]], outs=[wb])
                mm_split(ctx, ps, w[:], kp1[:, oc, tap:tap + HT],
                         ins=[wb, bf["a8kp1"]], outs=[psb],
                         start=(tap == 0), stop=(tap == 2))
            ctx.op("scalar", lambda e, oc=oc, ps=ps: e.activation(
                kp2[:, oc, 2:2 + HT], ps, AF.Identity, bias=kc1b[:, oc:oc + 1]),
                ins=[psb, bf["a8k1b"]], outs=[bf["a8kp2"]])
        for oc in range(8):
            sl = oc % 2
            ps, psb = PS(sl), bf[f"p8a{sl}"]
            wq, wqb = sb[f"a8wq{sl}"], bf[f"a8wq{sl}"]
            ctx.dma(["sync", "scalar"][oc % 2], wq[:, 0:24, :],
                    W["qc2T_b"].ap[:, :, oc * 128:(oc + 1) * 128].rearrange(
                        "k p c -> p k c"), wqb,
                    ins=[W["qc2T_b"]], outs=[wqb])
            first = True
            for tap in range(3):
                for ic in range(8):
                    mm_split(ctx, ps, wq[:, tap * 8 + ic, :],
                             qp2[:, ic, tap:tap + HT],
                             ins=[wqb, bf["a8qp2"]], outs=[psb],
                             start=first, stop=False)
                    first = False
            mm_split(ctx, ps, bqp[0:64, oc, :], kmean[0:64, :],
                     ins=[bf["a8bq"], bf["a8km"]], outs=[psb],
                     start=False, stop=True)
            ctx.op("scalar", lambda e, oc=oc, ps=ps: e.activation(
                qf[:, oc, :], ps, AF.Identity, bias=qc2b[:, oc:oc + 1]),
                ins=[psb, bf["a8q2b"]], outs=[bf["a8qf"]])
        for oc in range(2):
            sl = oc % 2
            ps, psb = PS(sl), bf[f"p8a{sl}"]
            first = True
            for tap in range(3):
                for ic in range(2):
                    wsl = (tap * 2 + ic) % 4
                    w, wb = wt[wsl], bf[f"a8w{wsl}"]
                    ctx.dma("sync", w[:],
                            W["kc2T"].ap[tap * 2 + ic, :,
                                         oc * 128:(oc + 1) * 128], wb,
                            ins=[W["kc2T"]], outs=[wb])
                    mm_split(ctx, ps, w[:], kp2[:, ic, tap:tap + HT],
                             ins=[wb, bf["a8kp2"]], outs=[psb],
                             start=first, stop=False)
                    first = False
            mm_split(ctx, ps, bkp[0:64, oc, :], qmean[0:64, :],
                     ins=[bf["a8bk"], bf["a8qm"]], outs=[psb],
                     start=False, stop=True)
            ctx.op("scalar", lambda e, oc=oc, ps=ps: e.activation(
                kf[:, oc, :], ps, AF.Identity, bias=kc2b[:, oc:oc + 1]),
                ins=[psb, bf["a8k2b"]], outs=[bf["a8kf"]])
        for (nch, src, srcb, pat, patb, dstn, nr) in [
                (8, qf, bf["a8qf"], sqp, bf["a8sq"], "a8rsq", 16),
                (2, kf, bf["a8kf"], skp, bf["a8sk"], "a8rsk", 4)]:
            for oc in range(nch):
                ctx.op("scalar", lambda e, src=src, oc=oc: e.activation(
                    tA[:], src[:, oc, :], AF.Square), ins=[srcb],
                    outs=[bf["a8tA"]])
                mm_split(ctx, PS(2)[0:nr, :], pat[:, oc, 0:nr], tA[:],
                         ins=[bf["a8tA"], patb], outs=[bf["p8b"]],
                         start=(oc == 0), stop=(oc == nch - 1))
            dst, dstb = sb[dstn], bf[dstn]
            ctx.op("scalar", lambda e, dst=dst, nr=nr: e.activation(
                dst[0:nr, :], PS(2)[0:nr, :], AF.Sqrt),
                ins=[bf["p8b"]], outs=[dstb])
            ctx.op("vector", lambda e, dst=dst, nr=nr: e.tensor_scalar(
                dst[0:nr, :], dst[0:nr, :], 1e-12, None, ALU.max),
                ins=[dstb], outs=[dstb])
            ctx.op("vector", lambda e, dst=dst, nr=nr: e.reciprocal(
                dst[0:nr, :], dst[0:nr, :]), ins=[dstb], outs=[dstb])
        ctx.op("vector", lambda e: e.tensor_scalar(
            rsk[0:4, :], rsk[0:4, :], ktl[0:4, :], None, ALU.mult),
            ins=[bf["a8rsk"], bf["a8kt"]], outs=[bf["a8rsk"]])
        for (nch, srcf, srcb, rpat, rpatb, rsn, nr, dsth, dsthb) in [
                (8, qf, bf["a8qf"], rsbq, bf["a8rq"], "a8rsq", 16, qhat,
                 bf["qhat"]),
                (2, kf, bf["a8kf"], rsbk, bf["a8rk"], "a8rsk", 4, khat,
                 bf["khat"])]:
            rs, rsb2 = sb[rsn], bf[rsn]
            for oc in range(nch):
                mm_split(ctx, PS(2)[:, :], rpat[0:nr, oc, :], rs[0:nr, :],
                         ins=[rpatb, rsb2], outs=[bf["p8b"]],
                         start=True, stop=True)
                ctx.op("vector", lambda e: e.tensor_copy(rbc[:], PS(2)[:, :]),
                       ins=[bf["p8b"]], outs=[bf["a8rbc"]])
                ctx.op("vector", lambda e, srcf=srcf, oc=oc: e.tensor_tensor(
                    tC[:], srcf[:, oc, :], rbc[:], ALU.mult),
                    ins=[srcb, bf["a8rbc"]], outs=[bf["a8tC"]])
                mm_split(ctx, PS(3)[:, :], sb["swapm_s"][:], tC[:],
                         ins=[bf["a8tC"], bf["swapm_s"]], outs=[bf["p8c"]],
                         start=True, stop=True)
                ctx.op("vector", lambda e: e.tensor_tensor(
                    tA[:], tC[:], cosf[:], ALU.mult),
                    ins=[bf["a8tC"], bf["a8cos"]], outs=[bf["a8tA"]])
                ctx.op("vector", lambda e: e.tensor_tensor(
                    tB[:], PS(3)[:, :], sinf[:], ALU.mult),
                    ins=[bf["p8c"], bf["a8sin"]], outs=[bf["a8tB"]])
                ctx.op("vector", lambda e, dsth=dsth, oc=oc, t0=t0:
                       e.tensor_tensor(dsth[:, oc, t0:t0 + HT], tA[:], tB[:],
                                       ALU.add),
                       ins=[bf["a8tA"], bf["a8tB"]], outs=[dsthb])
                if "dbg_qh3" in W and nch == 8 and hf == 0 and oc == 0:
                    ctx.dma("sync", W["dbg_qh3"].ap[:, 0:HT],
                            qhat[:, 0, 0:HT],
                            dsthb, ins=[dsthb], outs=[W["dbg_qh3"]])
                if "dbg_tc" in W and nch == 8 and oc == 0:
                    ctx.dma("sync", W["dbg_tc"].ap[hf, 0], tC[:],
                            bf["a8tC"], ins=[bf["a8tC"]], outs=[W["dbg_tc"]])
                    ctx.dma("sync", W["dbg_tc"].ap[hf, 1], tA[:],
                            bf["a8tA"], ins=[bf["a8tA"]], outs=[W["dbg_tc"]])
                    ctx.dma("sync", W["dbg_tc"].ap[hf, 2], tB[:],
                            bf["a8tB"], ins=[bf["a8tB"]], outs=[W["dbg_tc"]])
                    ctx.dma("sync", W["dbg_tc"].ap[hf, 3], rbc[:],
                            bf["a8rbc"], ins=[bf["a8rbc"]], outs=[W["dbg_tc"]])
        if "dbg_qh2" in W and hf == 1:
            ctx.dma("sync", W["dbg_qh2"].ap[0], qhat[:, :, :],
                    bf["qhat"], ins=[bf["qhat"]], outs=[W["dbg_qh2"]])
        if "dbg_qp1" in W:
            ctx.dma("sync", W["dbg_qp1"].ap[hf], qp1[:, :, :],
                    bf["a8qp1"], ins=[bf["a8qp1"]], outs=[W["dbg_qp1"]])
            ctx.dma("sync", W["dbg_qp2"].ap[hf], qp2[:, :, :],
                    bf["a8qp2"], ins=[bf["a8qp2"]], outs=[W["dbg_qp2"]])
            ctx.dma("sync", W["dbg_qf"].ap[hf], qf[:, :, :],
                    bf["a8qf"], ins=[bf["a8qf"]], outs=[W["dbg_qf"]])
            ctx.dma("sync", W["dbg_rsq"].ap[hf], rsq[:, :],
                    bf["a8rsq"], ins=[bf["a8rsq"]], outs=[W["dbg_rsq"]])
            ctx.dma("sync", W["dbg_km"].ap[hf], kmean[:, :],
                    bf["a8km"], ins=[bf["a8km"]], outs=[W["dbg_km"]])
        for kvh in range(4):
            base = 64 * (kvh % 2)
            kc = kvh // 2
            mm_split(ctx, PS(2)[:, :],
                     sb["dup64"][base:base + 64, :],
                     khat[base:base + 64, kc, t0:t0 + HT],
                     ins=[bf["khat"], bf["dup64"]], outs=[bf["p8b"]],
                     start=True, stop=True)
            ctx.op("vector", lambda e, kvh=kvh, t0=t0: e.tensor_copy(
                sb["khat4"][:, kvh, t0:t0 + HT], PS(2)[:, :]),
                ins=[bf["p8b"]], outs=[bf["khat4"]])
        if hf == 0:
            ctx.op("vector", lambda e: e.memset(vsh[:, 0:1], 0.0),
                   ins=[], outs=[bf["a8vsh"]])
        else:
            ctx.op("vector", lambda e: e.tensor_copy(
                vsh[:, 0:1], halo[:, 11, 1:2]), ins=[bf["a8halo"]],
                outs=[bf["a8vsh"]])
        ctx.op("vector", lambda e: e.tensor_copy(
            vsh[:, 1:HT], qkv[:, 1, 0:HT - 1]), ins=[bf["a8qkv"]],
            outs=[bf["a8vsh"]])
        for kv in range(4):
            base = 64 * (kv % 2)
            if kv < 2:
                def srcap(tt, kv=kv):
                    return qkv[64 * kv:64 * kv + 64, 0,
                               tt * 128:(tt + 1) * 128]
                sb_in = bf["a8qkv"]
            else:
                def srcap(tt, kv=kv):
                    return vsh[64 * (kv - 2):64 * (kv - 2) + 64,
                               tt * 128:(tt + 1) * 128]
                sb_in = bf["a8vsh"]
            for tt in range(HT // 128):
                psl = tt % 2
                pst = B.ps_ap[:, 3072 + 512 * psl:3072 + 512 * psl + 32]\
                    .bitcast(BF16)[:, 0:64]
                pstb = bf[f"p8t{psl}"]
                ctx.op("tensor", lambda e, srcap=srcap, tt=tt, pst=pst,
                       base=base:
                       e.transpose(pst, srcap(tt),
                                   sb["dup64"][base:base + 64, 0:64]),
                       ins=[sb_in, bf["dup64"], bf["p8c"]], outs=[pstb])
                gtt = t0 // 128 + tt
                ctx.op("vector", lambda e, kv=kv, gtt=gtt, pst=pst:
                       e.tensor_copy(vtm[:, kv, gtt, :], pst),
                       ins=[pstb], outs=[bf["vtm"]])


def emit_attn(ctx, B, W, s1_dram):
    sb, bf = B.sb, B.buf
    qhat, khat4, vtm = sb["qhat"], sb["khat4"], sb["vtm"]
    afm = B.sbuf("a9afm", [128, 8, T], BF16)
    cm = B.sbuf("a9cm", [128, 128], F32)
    Pb = [B.sbuf(f"a9P{sl}", [128, T], BF16) for sl in range(2)]
    pt2 = [[B.sbuf(f"a9ptt{S}{psl}", [128, 128], BF16) for psl in range(2)]
           for S in range(2)]
    for S in range(2):
        for psl in range(2):
            B.bufonly(f"a9pt{S}{psl}")
    av2 = [B.sbuf(f"a9avv{S}", [128, 64], BF16) for S in range(2)]
    for S in range(2):
        B.bufonly(f"a9av{S}")
    mx = B.sbuf("a9mx", [128, 8], F32)
    for nm in ("p9A0", "p9A1", "p9t0", "p9t1"):
        B.bufonly(nm)
    ctx.dma("sync", cm[:], W["cmask"].ap[:, :], bf["a9cm"],
            ins=[W["cmask"]], outs=[bf["a9cm"]])

    def head(a, qt):
        kvh = a // 4
        qc, qh = a // 2, a % 2
        ext = (qt + 1) * 128
        S = qt % 2
        base = 2048 * S
        slb = bf[f"p9A{S}"]
        ps = B.ps_ap[:, base:base + ext]
        lhsT = qhat[64 * qh:64 * qh + 64, qc, qt * 128:(qt + 1) * 128]
        rhs = khat4[64 * qh:64 * qh + 64, kvh, 0:ext]
        mm_split(ctx, ps, lhsT, rhs, ins=[bf["qhat"], bf["khat4"]],
                 outs=[slb], start=True, stop=True)
        ctx.op("vector", lambda e, qt=qt, ext=ext, base=base:
               e.tensor_tensor(
                   B.ps_ap[:, base + qt * 128:base + ext],
                   B.ps_ap[:, base + qt * 128:base + ext], cm[:],
                   ALU.add), ins=[slb, bf["a9cm"]], outs=[slb])
        ctx.op("vector", lambda e, ext=ext, base=base, S=S: e.reduce_max(
            mx[:, 4 * S:4 * S + 1], B.ps_ap[:, base:base + ext], AX.X),
            ins=[slb], outs=[bf["a9mx"]])
        ctx.op("vector", lambda e, S=S: e.tensor_scalar(
            mx[:, 4 * S + 1:4 * S + 2], mx[:, 4 * S:4 * S + 1], -1.0,
            None, ALU.mult), ins=[bf["a9mx"]], outs=[bf["a9mx"]])
        P_, Pbb = Pb[S], bf[f"a9P{S}"]
        ctx.op("scalar", lambda e, P_=P_, ext=ext, base=base, S=S:
               e.activation(P_[:, 0:ext], B.ps_ap[:, base:base + ext],
                            AF.Exp, bias=mx[:, 4 * S + 1:4 * S + 2]),
               ins=[slb, bf["a9mx"]], outs=[Pbb])
        ctx.op("vector", lambda e, P_=P_, ext=ext, S=S: e.reduce_sum(
            mx[:, 4 * S + 2:4 * S + 3], P_[:, 0:ext], AX.X),
            ins=[Pbb], outs=[bf["a9mx"]])
        ctx.op("vector", lambda e, S=S: e.reciprocal(
            mx[:, 4 * S + 3:4 * S + 4], mx[:, 4 * S + 2:4 * S + 3]),
            ins=[bf["a9mx"]], outs=[bf["a9mx"]])

    def tail(a, qt):
        kvh = a // 4
        qc, qh = a // 2, a % 2
        S = qt % 2
        base = 2048 * S
        slb = bf[f"p9A{S}"]
        P_, Pbb = Pb[S], bf[f"a9P{S}"]
        for kt in range(qt + 1):
            psl = kt % 2
            pstp = B.ps_ap[:, base + 512 * psl:base + 512 * psl + 64]\
                .bitcast(BF16)[:, 0:128]
            ctx.op("tensor", lambda e, P_=P_, kt=kt, pstp=pstp:
                   e.transpose(pstp, P_[:, kt * 128:(kt + 1) * 128],
                               sb["ident"][:]),
                   ins=[Pbb, bf["ident"]], outs=[slb])
            ctx.op("vector", lambda e, psl=psl, pstp=pstp, S=S:
                   e.tensor_copy(pt2[S][psl][:], pstp),
                   ins=[slb], outs=[bf[f"a9pt{S}{psl}"]])
            ctx.op("tensor", lambda e, psl=psl, kvh=kvh, kt=kt, qt=qt,
                   base=base, S=S:
                   e.matmul(B.ps_ap[:, base + 1024:base + 1088],
                            pt2[S][psl][:], vtm[:, kvh, kt, :],
                            start=(kt == 0), stop=(kt == qt),
                            skip_group_check=True),
                   ins=[bf[f"a9pt{S}{psl}"], bf["vtm"]], outs=[slb])
        ctx.op("vector", lambda e, base=base, S=S: e.tensor_scalar(
            av2[S][:], B.ps_ap[:, base + 1024:base + 1088],
            mx[:, 4 * S + 3:4 * S + 4], None, ALU.mult),
            ins=[slb, bf["a9mx"]], outs=[bf[f"a9av{S}"]])
        pso = B.ps_ap[:, base + 1536:base + 1600]\
            .bitcast(BF16)[0:64, 0:128]
        ctx.op("tensor", lambda e, pso=pso, S=S: e.transpose(
            pso, av2[S][:], sb["ident"][:]),
            ins=[bf[f"a9av{S}"], bf["ident"]], outs=[slb])
        ctx.op("vector", lambda e, qh=qh, qc=qc, qt=qt, pso=pso:
               e.tensor_copy(
                   afm[64 * qh:64 * qh + 64, qc, qt * 128:(qt + 1) * 128],
                   pso), ins=[slb], outs=[bf["a9afm"]])

    iters = [(a, qt) for a in range(16) for qt in range(NT)]
    for i, (a, qt) in enumerate(iters):
        head(a, qt)
        if i > 0:
            tail(*iters[i - 1])
    tail(*iters[-1])
    wo = [B.sbuf(f"a9wo{sl}", [128, 8, 128], BF16) for sl in range(2)]
    ev = B.sbuf("a9ev", [128, T], F32)
    so = [B.sbuf(f"a9so{sl}", [128, 128], F32) for sl in range(8)]
    B.bufonly("p9wo")
    for nc_ in range(ND):
        ps = B.ps_ap[:, 0:2048]
        wsl = nc_ % 2
        w, wb = wo[wsl], bf[f"a9wo{wsl}"]
        ctx.dma(["sync", "scalar"][nc_ % 2], w[:],
                W["woT_b"].ap[:, :, nc_ * 128:(nc_ + 1) * 128].rearrange(
                    "l p c -> p l c"), wb,
                ins=[W["woT_b"]], outs=[wb])
        for lc in range(8):
            mm_split(ctx, ps, w[:, lc, :], afm[:, lc, :], ins=[wb, bf["a9afm"]],
                     outs=[bf["p9wo"]], start=(lc == 0), stop=(lc == 7))
        ctx.op("vector", lambda e, ps=ps: e.tensor_copy(ev[:], ps),
               ins=[bf["p9wo"]], outs=[bf["a9ev"]])
        for tc in range(16):
            psl = tc % 2
            sl8 = tc % 8
            pst = B.ps_ap[:, 2048 + 512 * psl:2048 + 512 * psl + 128]
            s_, s_b = so[sl8], bf[f"a9so{sl8}"]
            ctx.op("tensor", lambda e, tc=tc, pst=pst: e.transpose(
                pst, ev[:, tc * 128:(tc + 1) * 128], sb["identf"][:]),
                ins=[bf["a9ev"], bf["identf"]], outs=[bf[f"p9t{psl}"]])
            ctx.op("vector", lambda e, s_=s_, pst=pst: e.tensor_copy(
                s_[:], pst), ins=[bf[f"p9t{psl}"]], outs=[s_b])
            ctx.dma("sync",
                    s1_dram.ap[tc * 128:(tc + 1) * 128,
                               nc_ * 128:(nc_ + 1) * 128],
                    s_[:], s_b, ins=[s_b], outs=[s1_dram])


def emit_ffn1(ctx, B, W, s1_dram, x1_dram, out_dram):
    sb, bf = B.sb, B.buf
    xr = sb["xr1"]
    rsum, rsumb = sb["rsum"], bf["rsum"]
    rwt = B.sbuf("aArwt", [128, 32, 8], F32)
    rbs = B.sbuf("aArb", [128, 8], F32)
    e8 = B.sbuf("aAe8", [128, 128], F32)
    qr = B.sbuf("aAqr", [128, 32], F32)
    rt = B.sbuf("aArt", [128, 8], F32)
    rsc = B.sbuf("aArsc", [128, 1], F32)
    tl = B.sbuf("aAtl", [128, T], BF16)
    g1 = B.sbuf("aAg1", [128, T], F32)
    wt = [B.sbuf(f"aAw{sl}", [128, 128], BF16) for sl in range(4)]
    xt_ = [B.sbuf(f"aAx{sl}", [128, 128], F32) for sl in range(8)]
    st_ = [B.sbuf(f"aAs{sl}", [128, 128], F32) for sl in range(8)]
    gt_ = [B.sbuf(f"aAgt{sl}", [128, 128], F32) for sl in range(8)]
    ot_ = [B.sbuf(f"aAo{sl}", [128, 128], F32) for sl in range(8)]
    b1s = B.sbuf("aAb1", [128, 32], F32)
    for nm in ("pA0", "pA1", "pAt0", "pAt1", "pAr"):
        B.bufonly(nm)
    ctx.dma("sync", rwt[:], W["rwt"].ap.rearrange("a p b -> p a b"),
            bf["aArwt"], ins=[W["rwt"]], outs=[bf["aArwt"]])
    ctx.dma("sync", rbs[0:1, :], W["rb"].ap[:, :], bf["aArb"],
            ins=[W["rb"]], outs=[bf["aArb"]])
    ctx.dma("sync", e8[0:8, :], W["e8"].ap[:, :], bf["aAe8"],
            ins=[W["e8"]], outs=[bf["aAe8"]])
    ctx.dma("sync", b1s[:], W["b1s"].ap[:, :], bf["aAb1"],
            ins=[W["b1s"]], outs=[bf["aAb1"]])
    ctx.op("vector", lambda e: e.tensor_scalar(
        qr[:], rsum[:, 0:32], 1.0 / 1436.0, None, ALU.mult),
        ins=[rsumb], outs=[bf["aAqr"]])
    for kk in range(32):
        ctx.op("tensor", lambda e, kk=kk: e.matmul(
            B.ps_ap[0:1, 0:8], qr[:, kk:kk + 1], rwt[:, kk, :],
            start=(kk == 0), stop=(kk == 31)),
            ins=[bf["aAqr"], bf["aArwt"]], outs=[bf["pAr"]])
    ctx.op("vector", lambda e: e.tensor_tensor(
        rt[0:1, :], B.ps_ap[0:1, 0:8], rbs[0:1, :], ALU.add),
        ins=[bf["pAr"], bf["aArb"]], outs=[bf["aArt"]])
    ctx.op("vector", lambda e: e.reduce_max(
        rsc[0:1, 0:1], rt[0:1, :], AX.X), ins=[bf["aArt"]], outs=[bf["aArsc"]])
    ctx.op("vector", lambda e: e.tensor_scalar(
        rsc[0:1, 0:1], rsc[0:1, 0:1], -1.0, None, ALU.mult),
        ins=[bf["aArsc"]], outs=[bf["aArsc"]])
    ctx.op("scalar", lambda e: e.activation(
        rt[0:1, :], rt[0:1, :], AF.Exp, bias=rsc[0:1, 0:1]),
        ins=[bf["aArt"], bf["aArsc"]], outs=[bf["aArt"]])
    ctx.op("vector", lambda e: e.reduce_sum(
        rsc[0:1, 0:1], rt[0:1, :], AX.X), ins=[bf["aArt"]], outs=[bf["aArsc"]])
    ctx.op("vector", lambda e: e.reciprocal(
        rsc[0:1, 0:1], rsc[0:1, 0:1]), ins=[bf["aArsc"]], outs=[bf["aArsc"]])
    ctx.op("vector", lambda e: e.tensor_scalar(
        rt[0:1, :], rt[0:1, :], rsc[0:1, 0:1], None, ALU.mult),
        ins=[bf["aArt"], bf["aArsc"]], outs=[bf["aArt"]])
    ctx.op("tensor", lambda e: e.transpose(
        B.ps_ap[0:8, 512:513], rt[0:1, 0:8], sb["identf"][0:1, 0:1]),
        ins=[bf["aArt"], bf["identf"]], outs=[bf["pAt0"]])
    ctx.op("vector", lambda e: e.tensor_copy(
        rt[0:8, 0:1], B.ps_ap[0:8, 512:513]), ins=[bf["pAt0"]],
        outs=[bf["aArt"]])
    ctx.op("tensor", lambda e: e.matmul(
        B.ps_ap[:, 512:513], e8[0:8, :], rt[0:8, 0:1], start=True, stop=True),
        ins=[bf["aAe8"], bf["aArt"]], outs=[bf["pAt0"]])
    ctx.op("vector", lambda e: e.tensor_copy(
        rsc[:, 0:1], B.ps_ap[:, 512:513]), ins=[bf["pAt0"]],
        outs=[bf["aArsc"]])
    w32 = [B.sbuf(f"aAW{sl}", [128, 32, 128], BF16) for sl in range(2)]
    wB, wBb = w32[0], bf["aAW0"]
    ctx.dma("sync", wB[:], W["atp_b"].ap.rearrange("k p c -> p k c"),
            wBb, ins=[W["atp_b"]], outs=[wBb])
    for kk in range(32):
        mm_split(ctx, B.ps_ap[:, 0:2048], wB[:, kk, :], xr[:, kk, :],
                 ins=[wBb, bf["xr1"]], outs=[bf["pA0"]],
                 start=(kk == 0), stop=(kk == 31))
    ctx.op("vector", lambda e: e.tensor_scalar(
        tl[:], B.ps_ap[:, 0:2048], rsc[:, 0:1], None, ALU.mult),
        ins=[bf["pA0"], bf["aArsc"]], outs=[bf["aAtl"]])
    for oc in range(ND):
        ps = B.ps_ap[:, 0:2048]
        wsl = oc % 2
        w1, w1b = w32[wsl], bf[f"aAW{wsl}"]
        ctx.dma(["sync", "scalar"][oc % 2], w1[:],
                W["w1t_b"].ap[:, :, oc * 128:(oc + 1) * 128].rearrange(
                    "k p c -> p k c"),
                w1b, ins=[W["w1t_b"]], outs=[w1b])
        for kk in range(32):
            mm_split(ctx, ps, w1[:, kk, :], xr[:, kk, :], ins=[w1b, bf["xr1"]],
                     outs=[bf["pA0"]], start=(kk == 0), stop=False)
        w, wb = wt[(32) % 4], bf[f"aAw{0}"]
        ctx.dma("sync", w[:], W["btp_b"].ap[0, :, oc * 128:(oc + 1) * 128],
                wb, ins=[W["btp_b"]], outs=[wb])
        mm_split(ctx, ps, w[:], tl[:], ins=[wb, bf["aAtl"]],
                 outs=[bf["pA0"]], start=False, stop=True)
        ctx.op("scalar", lambda e, oc=oc, ps=ps: e.activation(
            g1[:], ps, AF.Silu, bias=b1s[:, oc:oc + 1]),
            ins=[bf["pA0"], bf["aAb1"]], outs=[bf["aAg1"]])
        for tc in range(16):
            psl = tc % 2
            sl8 = tc % 8
            pst = B.ps_ap[:, 2048 + 512 * psl:2048 + 512 * psl + 128]
            gt, gtb = gt_[sl8], bf[f"aAgt{sl8}"]
            xt, xtb = xt_[sl8], bf[f"aAx{sl8}"]
            st, stb = st_[sl8], bf[f"aAs{sl8}"]
            ot, otb = ot_[sl8], bf[f"aAo{sl8}"]
            rows = slice(tc * 128, tc * 128 + 128)
            cols = slice(oc * 128, oc * 128 + 128)
            ctx.op("tensor", lambda e, tc=tc, pst=pst: e.transpose(
                pst, g1[:, tc * 128:(tc + 1) * 128], sb["identf"][:]),
                ins=[bf["aAg1"], bf["identf"]], outs=[bf[f"pAt{psl}"]])
            ctx.op("vector", lambda e, gt=gt, pst=pst: e.tensor_copy(
                gt[:], pst), ins=[bf[f"pAt{psl}"]], outs=[gtb])
            ctx.dma("sync", xt[:], x1_dram.ap[rows, cols], xtb,
                    ins=[x1_dram], outs=[xtb])
            ctx.dma("sync", st[:], s1_dram.ap[rows, cols], stb,
                    ins=[s1_dram], outs=[stb])
            ctx.op("vector", lambda e, gt=gt, st=st: e.tensor_tensor(
                gt[:], gt[:], st[:], ALU.mult), ins=[gtb, stb], outs=[gtb])
            ctx.op("vector", lambda e, gt=gt, xt=xt, ot=ot: e.tensor_tensor(
                ot[:], gt[:], xt[:], ALU.add), ins=[gtb, xtb], outs=[otb])
            ctx.dma("sync", out_dram.ap[rows, cols], ot[:], otb,
                    ins=[otb], outs=[out_dram])


def _pd():
    pd = np.zeros(64, dtype=np.int64)
    for p in range(32):
        pd[p] = 2 * p
        pd[32 + p] = 2 * p + 1
    return pd


def host_l1(inp):
    pd = _pd()
    W = {}
    W["wqkvT"] = np.ascontiguousarray(
        inp["L1_attn_w_qkv"].T).reshape(32, 128, 1536).astype(np.float32)
    qc1 = np.asarray(inp["L1_attn_qc1_w"], dtype=np.float32)
    bd = np.zeros((24, 128, 128), np.float32)
    for oc in range(8):
        for tap in range(3):
            for o_rel in range(128):
                o = oc * 128 + o_rel
                g = o // 64
                for i in range(64):
                    r_in = g * 64 + i - oc * 128
                    bd[oc * 3 + tap, r_in, o_rel] = qc1[o, i, tap]
    W["qc1bd"] = bd
    W["qc1b"] = np.ascontiguousarray(
        np.asarray(inp["L1_attn_qc1_b"], np.float32).reshape(8, 128).T)
    qc2 = np.asarray(inp["L1_attn_qc2_w"], dtype=np.float32)
    q2 = np.zeros((24, 128, 1024), np.float32)
    operm = np.concatenate([(o // 64) * 64 + pd[o % 64]
                            for o in range(1024)], axis=None) \
        if False else np.array([(o // 64) * 64 + pd[o % 64]
                                for o in range(1024)])
    for tap in range(3):
        for ic in range(8):
            q2[tap * 8 + ic] = qc2[operm, ic * 128:(ic + 1) * 128, tap].T
    W["qc2T"] = q2
    W["qc2b"] = np.ascontiguousarray(
        np.asarray(inp["L1_attn_qc2_b"], np.float32)[operm].reshape(8, 128).T)
    kc1 = np.asarray(inp["L1_attn_kc1_w"], dtype=np.float32)
    bdk = np.zeros((6, 128, 128), np.float32)
    for oc in range(2):
        for tap in range(3):
            for o_rel in range(128):
                o = oc * 128 + o_rel
                g = o // 64
                for i in range(64):
                    r_in = g * 64 + i - oc * 128
                    bdk[oc * 3 + tap, r_in, o_rel] = kc1[o, i, tap]
    W["kc1bd"] = bdk
    W["kc1b"] = np.ascontiguousarray(
        np.asarray(inp["L1_attn_kc1_b"], np.float32).reshape(2, 128).T)
    kc2 = np.asarray(inp["L1_attn_kc2_w"], dtype=np.float32)
    kperm = np.array([(o // 64) * 64 + pd[o % 64] for o in range(256)])
    k2 = np.zeros((6, 128, 256), np.float32)
    for tap in range(3):
        for ic in range(2):
            k2[tap * 2 + ic] = kc2[kperm, ic * 128:(ic + 1) * 128, tap].T
    W["kc2T"] = k2
    W["kc2b"] = np.ascontiguousarray(
        np.asarray(inp["L1_attn_kc2_b"], np.float32)[kperm].reshape(2, 128).T)
    mk_ = np.zeros((2, 128, 64), np.float32)
    for c2 in range(2):
        for r in range(128):
            mk_[c2, r, (c2 * 128 + r) % 64] = 0.25
    W["mk"] = mk_
    mq = np.zeros((8, 128, 64), np.float32)
    for c8 in range(8):
        for r in range(128):
            mq[c8, r, r % 64] = 1.0 / 16.0
    W["mq"] = mq
    bq = np.zeros((8, 64, 128), np.float32)
    for oc in range(8):
        for o_rel in range(128):
            bq[oc, pd[(oc * 128 + o_rel) % 64], o_rel] = 0.5
    W["bq"] = bq
    bk = np.zeros((2, 64, 128), np.float32)
    for oc in range(2):
        for o_rel in range(128):
            bk[oc, pd[(oc * 128 + o_rel) % 64], o_rel] = 0.5
    W["bk"] = bk
    sq = np.zeros((8, 128, 16), np.float32)
    for oc in range(8):
        for r in range(128):
            sq[oc, r, (oc * 128 + r) // 64] = 1.0
    W["sq"] = sq
    sk = np.zeros((2, 128, 4), np.float32)
    for oc in range(2):
        for r in range(128):
            sk[oc, r, (oc * 128 + r) // 64] = 1.0
    W["sk"] = sk
    rq = np.zeros((8, 16, 128), np.float32)
    for oc in range(8):
        for o_rel in range(128):
            rq[oc, (oc * 128 + o_rel) // 64, o_rel] = 0.125
    W["rsbq"] = rq
    rk = np.zeros((2, 4, 128), np.float32)
    for oc in range(2):
        for o_rel in range(128):
            rk[oc, (oc * 128 + o_rel) // 64, o_rel] = 1.0
    W["rsbk"] = rk
    inv = 1.0 / 10000.0 ** (np.arange(0, 64, 2, dtype=np.float64) / 64.0)
    ang = np.arange(T, dtype=np.float64)[:, None] * inv[None, :]
    W["cosf"] = np.ascontiguousarray(
        np.tile(np.cos(ang).T.astype(np.float32), (4, 1)))
    sins = np.tile(np.sin(ang).T.astype(np.float32), (4, 1))
    sins[0:32] *= -1.0
    sins[64:96] *= -1.0
    W["sinf"] = np.ascontiguousarray(sins)
    swapm = np.zeros((128, 128), np.float32)
    for p in range(128):
        blk, off = p // 64, p % 64
        sw = blk * 64 + (off + 32) % 64
        swapm[sw, p] = 1.0
    W["swapm"] = swapm
    W["ktemp"] = np.broadcast_to(
        np.asarray(inp["L1_attn_key_temp"], np.float32).reshape(1, 1),
        (128, 1)).copy()
    cmask = np.zeros((128, 128), np.float32)
    for qq in range(128):
        cmask[qq, qq + 1:] = -1e30
    W["cmask"] = cmask
    W["woT"] = np.ascontiguousarray(
        np.asarray(inp["L1_attn_w_o"], np.float32).T).reshape(8, 128, 4096)
    W["W1T"] = np.ascontiguousarray(
        np.asarray(inp["L1_ffn_W"], np.float32).T).reshape(32, 128, 4096)
    A = np.asarray(inp["L1_ffn_A"], np.float32).reshape(128, 4096)
    W["ATp"] = np.ascontiguousarray(A.T).reshape(32, 128, 128)
    Bm = np.asarray(inp["L1_ffn_B"], np.float32)
    BT = np.ascontiguousarray(Bm.transpose(0, 2, 1).reshape(128, 4096))
    W["BTf"] = BT.reshape(1, 128, 4096)
    W["rwt"] = np.ascontiguousarray(
        np.asarray(inp["L1_ffn_router_W"], np.float32).T).reshape(32, 128, 8)
    W["rb"] = np.asarray(inp["L1_ffn_router_b"], np.float32).reshape(1, 8)
    e8 = np.zeros((8, 128), np.float32)
    for p in range(128):
        e8[p // 16, p] = 2.0
    W["e8"] = e8
    W["b1s"] = np.ascontiguousarray(
        np.asarray(inp["L1_ffn_b"], np.float32).reshape(32, 128).T)
    return W


def emit_allgather(ctx, B, shard_dram, ag_dram, rows, cols, key_buf):
    stream = ctx.dstream_for(key_buf)
    ctx.gather_streams.add(stream)
    waits = ctx._edges(stream, [shard_dram], [ag_dram])
    stream.n += 16

    def f(e, waits=waits, stream=stream):
        for st, v in waits:
            e.wait_ge(st.sem, v)
        e.collective_compute(
            "AllGather", ALU.bypass,
            replica_groups=[list(range(8))],
            ins=[shard_dram.ap.rearrange("a b c d e f -> (a b c d e) f")
                 if len(shard_dram.ap.shape) == 6 else
                 shard_dram.ap.rearrange("a b c d e -> (a b c d) e")],
            outs=[ag_dram.ap]).then_inc(stream.sem, 16)
    ctx.emit("gpsimd", f)
    ctx._post(stream, [shard_dram], [ag_dram])
    return stream.n



"""Driver section for kernel.py: per-launch builders + kernel()."""
import numpy as np
import ml_dtypes
from contextlib import ExitStack
import concourse.bass as bass
import concourse.mybir as mybir
from concourse.bass_utils import run_bass_kernel_spmd

WNAMES_F32 = {
    "wqkvT": (32, 128, 1536), "qc1bd": (24, 128, 128), "qc1b": (128, 8),
    "qc2T": (24, 128, 1024), "qc2b": (128, 8), "kc1bd": (6, 128, 128),
    "kc1b": (128, 2), "kc2T": (6, 128, 256), "kc2b": (128, 2),
    "mk": (2, 128, 64), "mq": (8, 128, 64), "bq": (8, 64, 128),
    "bk": (2, 64, 128), "sq": (8, 128, 16), "sk": (2, 128, 4),
    "rsbq": (8, 16, 128), "rsbk": (2, 4, 128), "cosf": (128, T),
    "sinf": (128, T), "ktemp": (128, 1), "cmask": (128, 128),
    "woT": (8, 128, 4096), "W1T": (32, 128, 4096), "ATp": (32, 128, 128),
    "BTf": (1, 128, 4096), "rwt": (32, 128, 8), "rb": (1, 8),
    "e8": (8, 128), "b1s": (128, 32), "swapm": (128, 128),
}
CASTS = [("wqkvT", "wqkvT_b"), ("qc2T", "qc2T_b"), ("qc1bd", "qc1bd_b"),
         ("kc1bd", "kc1bd_b"), ("woT", "woT_b"), ("W1T", "w1t_b"),
         ("ATp", "atp_b"), ("BTf", "btp_b"), ("bq", "bq_b"), ("mk", "mk_b"),
         ("mq", "mq_b"), ("kc2T", "kc2T_b"), ("bk", "bk_b")]


def host_consts():
    p4 = np.zeros((32, 128, 128), dtype=np.float32)
    for Tt in range(32):
        for k in range(128):
            p4[Tt, k, 4 * Tt + k // 32] = 1.0
    sel4 = np.zeros((4, 128, 128), dtype=np.float32)
    for b in range(4):
        for r in range(128):
            sel4[b, 32 * b + r // 4, r] = 1.0
    dup = np.zeros((128, 128), np.float32)
    for r in range(128):
        dup[r, r % 64] = 1.0
        dup[r, 64 + r % 64] = 1.0
    dupT = np.zeros((128, 128), np.float32)
    for r in range(128):
        for c in range(128):
            if c % 64 == r % 64:
                dupT[r, c] = 1.0
    return (p4, sel4.astype(BF), np.eye(128, dtype=np.float32).astype(BF),
            np.eye(128, dtype=np.float32), dupT.astype(BF))


def build_A():
    nc = bass.Bass("TRN2", target_bir_lowering=False)
    es = ExitStack()
    with es:
        ctx = Ctx(es, nc)
        B = Builder(es, nc, ctx)
        x = B.dram("x", [T, D], F32, "ExternalInput")
        ident_in = B.dram("ident_in", [128, 128], BF16, "ExternalInput")
        identf_in = B.dram("identf_in", [128, 128], F32, "ExternalInput")
        maskT = B.dram("maskT", [32, 128, D], I32, "ExternalInput")
        wtp = B.dram("wtp", [32, 128, D], F32, "ExternalInput")
        p4d = B.dram("p4d", [32, 128, 128], F32, "ExternalInput")
        sel4d = B.dram("sel4d", [4, 128, 128], BF16, "ExternalInput")
        xnp = B.dram("xnp", [ND, 128, T], BF16, "ExternalOutput")
        w0t = B.dram("w0t", [ND, 128, D], BF16, "Internal")
        s0 = B.dram("s0", [T, D], F32, "ExternalOutput")
        B.psum_main(4096)
        B.sbuf("ident", [128, 128], BF16)
        B.sbuf("identf", [128, 128], F32)
        ctx.dma("sync", B.sb["ident"][:], ident_in.ap[:, :], B.buf["ident"],
                ins=[ident_in], outs=[B.buf["ident"]])
        ctx.dma("sync", B.sb["identf"][:], identf_in.ap[:, :], B.buf["identf"],
                ins=[identf_in], outs=[B.buf["identf"]])
        B.push_scope()
        B.ln_bufs("l0")
        emit_ln(ctx, B, x, xnp, "l0", permuted=True)
        B.pop_scope()
        ctx.fence()
        B.push_scope()
        emit_w0prep(ctx, B, maskT, wtp, p4d, sel4d, w0t)
        B.pop_scope()
        ctx.fence()
        emit_resident_load(ctx, B, xnp, "xr")
        B.push_scope()
        emit_s0(ctx, B, w0t, s0)
        B.pop_scope()
        ctx.final_waits()
        ctx.run_block()
    mybir.codegen_inst_isa_subclasses(nc)
    return nc


def build_B(core):
    nc = bass.Bass("TRN2", target_bir_lowering=False)
    es = ExitStack()
    with es:
        ctx = Ctx(es, nc)
        B = Builder(es, nc, ctx)
        cb = B.dram("cb", [65536, 8], F32, "ExternalInput")
        idxc = B.dram("idxc", [128, 20480], I32, "ExternalInput")
        sgu = B.dram("sgu", [2, 11, 128, 4, 8, 128], BF16, "ExternalOutput")
        sd = B.dram("sd", [4, 128, 8, 11, 128], BF16, "ExternalOutput")
        B.push_scope()
        emit_gather(ctx, B, cb, idxc, sgu, sd, core)
        B.pop_scope()
        ctx.final_waits()
        ctx.run_block()
    mybir.codegen_inst_isa_subclasses(nc)
    return nc


def build_C():
    nc = bass.Bass("TRN2", target_bir_lowering=False)
    es = ExitStack()
    with es:
        ctx = Ctx(es, nc)
        B = Builder(es, nc, ctx)
        xnp = B.dram("xnp", [ND, 128, T], BF16, "ExternalInput")
        sgu = B.dram("sgu", [2, 88, 128, 4, 8, 128], BF16, "ExternalInput")
        sd = B.dram("sd", [32, 128, 8, 11, 128], BF16, "ExternalInput")
        x = B.dram("x", [T, D], F32, "ExternalInput")
        s0 = B.dram("s0", [T, D], F32, "ExternalInput")
        identf_in = B.dram("identf_in", [128, 128], F32, "ExternalInput")
        hd = B.dram("hd", [88, 128, T], BF16, "Internal")
        x1 = B.dram("x1", [T, D], F32, "ExternalOutput")
        B.psum_main(4096)
        B.sbuf("identf", [128, 128], F32)
        ctx.dma("sync", B.sb["identf"][:], identf_in.ap[:, :], B.buf["identf"],
                ins=[identf_in], outs=[B.buf["identf"]])
        B.push_scope()
        emit_resident_load(ctx, B, xnp, "xr")
        B.push_scope()
        emit_ffn_up(ctx, B, sgu, hd)
        B.pop_scope()
        B.pop_scope()
        ctx.fence()
        B.push_scope()
        emit_ffn_down(ctx, B, sd, hd, x, s0, x1)
        B.pop_scope()
        ctx.final_waits()
        ctx.run_block()
    mybir.codegen_inst_isa_subclasses(nc)
    return nc


def build_D():
    nc = bass.Bass("TRN2", target_bir_lowering=False)
    es = ExitStack()
    with es:
        ctx = Ctx(es, nc)
        B = Builder(es, nc, ctx)
        drams = {}
        drams["x1"] = B.dram("x1", [T, D], F32, "ExternalInput")
        for nm, shp in WNAMES_F32.items():
            drams[nm] = B.dram(nm, list(shp), F32, "ExternalInput")
        drams["ident_in"] = B.dram("ident_in", [128, 128], BF16,
                                   "ExternalInput")
        drams["identf_in"] = B.dram("identf_in", [128, 128], F32,
                                    "ExternalInput")
        drams["dup64_in"] = B.dram("dup64_in", [128, 128], BF16,
                                   "ExternalInput")
        drams["xn1"] = B.dram("xn1", [ND, 128, T], BF16, "Internal")
        drams["s1"] = B.dram("s1", [T, D], F32, "Internal")
        drams["out"] = B.dram("out", [T, D], F32, "ExternalOutput")
        for src, dst in CASTS:
            drams[dst] = B.dram(dst, list(WNAMES_F32[src]), BF16, "Internal")
        B.psum_main(4096)
        B.sbuf("ident", [128, 128], BF16)
        B.sbuf("identf", [128, 128], F32)
        ctx.dma("sync", B.sb["ident"][:], drams["ident_in"].ap[:, :],
                B.buf["ident"], ins=[drams["ident_in"]], outs=[B.buf["ident"]])
        ctx.dma("sync", B.sb["identf"][:], drams["identf_in"].ap[:, :],
                B.buf["identf"], ins=[drams["identf_in"]],
                outs=[B.buf["identf"]])
        B.sbuf("rsum", [128, 64], F32)
        B.push_scope()
        B.ln_bufs("l1")
        emit_ln(ctx, B, drams["x1"], drams["xn1"], "l1", permuted=False,
                rsum="rsum", rsum_lo=611, rsum_hi=2047)
        B.pop_scope()
        ctx.fence()
        B.push_scope()
        for i, (src, dst) in enumerate(CASTS):
            emit_cast_bf16(ctx, B, drams[src], drams[dst], f"c{i}")
        B.pop_scope()
        ctx.fence()
        B.push_scope()
        B.sbuf("qhat", [128, 8, T], BF16)
        B.sbuf("khat", [128, 2, T], BF16)
        B.sbuf("khat4", [128, 4, T], BF16)
        B.sbuf("vtm", [128, 4, 16, 64], BF16)
        B.sbuf("dup64", [128, 128], BF16)
        ctx.dma("sync", B.sb["dup64"][:], drams["dup64_in"].ap[:, :],
                B.buf["dup64"], ins=[drams["dup64_in"]],
                outs=[B.buf["dup64"]])
        B.push_scope()
        Wd = dict(drams)
        for src, dst in CASTS:
            Wd[src if src not in ("wqkvT", "qc2T", "qc1bd", "kc1bd", "woT")
               else src] = drams[dst]
        Wd["wqkvT_b"] = drams["wqkvT_b"]
        Wd["qc2T_b"] = drams["qc2T_b"]
        Wd["qc1bd_b"] = drams["qc1bd_b"]
        Wd["kc1bd_b"] = drams["kc1bd_b"]
        Wd["bq"] = drams["bq_b"]
        Wd["mk"] = drams["mk_b"]
        Wd["mq"] = drams["mq_b"]
        Wd["kc2T"] = drams["kc2T_b"]
        Wd["bk"] = drams["bk_b"]
        emit_attn_pre(ctx, B, Wd, drams["xn1"])
        B.pop_scope()
        ctx.fence()
        B.push_scope()
        Wd2 = dict(drams)
        Wd2["woT_b"] = drams["woT_b"]
        emit_attn(ctx, B, Wd2, drams["s1"])
        B.pop_scope()
        B.pop_scope()
        ctx.fence()
        B.push_scope()
        emit_resident_load(ctx, B, drams["xn1"], "xr1")
        Wd3 = dict(drams)
        Wd3["atp_b"] = drams["atp_b"]
        Wd3["w1t_b"] = drams["w1t_b"]
        Wd3["btp_b"] = drams["btp_b"]
        emit_ffn1(ctx, B, Wd3, drams["s1"], drams["x1"], drams["out"])
        B.pop_scope()
        ctx.final_waits()
        ctx.run_block()
    mybir.codegen_inst_isa_subclasses(nc)
    return nc



def build_F(with_gather=True):
    nc = bass.Bass("TRN2", target_bir_lowering=False)
    es = ExitStack()
    with es:
        ctx = Ctx(es, nc)
        B = Builder(es, nc, ctx)
        dr = {}
        dr["x"] = B.dram("x", [T, D], F32, "ExternalInput")
        dr["maskT"] = B.dram("maskT", [32, 128, D], I32, "ExternalInput")
        dr["wtp"] = B.dram("wtp", [32, 128, D], F32, "ExternalInput")
        dr["cb"] = B.dram("cb", [65536, 8], F32, "ExternalInput")
        dr["idxc"] = B.dram("idxc", [128, 20480], I32, "ExternalInput")
        dr["p4d"] = B.dram("p4d", [32, 128, 128], F32, "ExternalInput")
        dr["sel4d"] = B.dram("sel4d", [4, 128, 128], BF16, "ExternalInput")
        dr["ident_in"] = B.dram("ident_in", [128, 128], BF16, "ExternalInput")
        dr["identf_in"] = B.dram("identf_in", [128, 128], F32,
                                 "ExternalInput")
        dr["dup64_in"] = B.dram("dup64_in", [128, 128], BF16, "ExternalInput")
        for nm, shp in WNAMES_F32.items():
            dr[nm] = B.dram(nm, list(shp), F32, "ExternalInput")
        dr["xnp"] = B.dram("xnp", [ND, 128, T], BF16, "Internal")
        dr["w0t"] = B.dram("w0t", [ND, 128, D], BF16, "Internal")
        dr["s0"] = B.dram("s0", [T, D], F32, "Internal")
        if with_gather:
            dr["sgu_sh"] = B.dram("sgu_sh", [2, 11, 128, 4, 8, 128], BF16,
                                  "Internal")
            dr["sd_sh"] = B.dram("sd_sh", [4, 128, 8, 11, 128], BF16,
                                 "Internal")
            dr["ag_gu"] = B.dram("ag_gu", [8, 2, 11, 128, 4, 8, 128], BF16,
                                 "Internal")
            dr["ag_sd"] = B.dram("ag_sd", [8, 4, 128, 8, 11, 128], BF16,
                                 "Internal")
        else:
            dr["ag_gu"] = B.dram("sguF", [2, 88, 128, 4, 8, 128], BF16,
                                 "ExternalInput")
            dr["ag_sd"] = B.dram("sdF", [32, 128, 8, 11, 128], BF16,
                                 "ExternalInput")
        dr["hd"] = B.dram("hd", [88, 128, T], BF16, "Internal")
        dr["x1"] = B.dram("x1", [T, D], F32, "Internal")
        dr["xn1"] = B.dram("xn1", [ND, 128, T], BF16, "Internal")
        dr["s1"] = B.dram("s1", [T, D], F32, "Internal")
        dr["out"] = B.dram("out", [T, D], F32, "ExternalOutput")
        for src, dst in CASTS:
            dr[dst] = B.dram(dst, list(WNAMES_F32[src]), BF16, "Internal")

        B.psum_main(4096)
        B.sbuf("ident", [128, 128], BF16)
        B.sbuf("identf", [128, 128], F32)
        ctx.dma("sync", B.sb["ident"][:], dr["ident_in"].ap[:, :],
                B.buf["ident"], ins=[dr["ident_in"]], outs=[B.buf["ident"]])
        ctx.dma("sync", B.sb["identf"][:], dr["identf_in"].ap[:, :],
                B.buf["identf"], ins=[dr["identf_in"]],
                outs=[B.buf["identf"]])

        if with_gather:
            B.push_scope()
            emit_gather(ctx, B, dr["cb"], dr["idxc"], dr["sgu_sh"],
                        dr["sd_sh"], core=0, dma_eng="gpsimd")
            emit_allgather(ctx, B, dr["sgu_sh"], dr["ag_gu"],
                           2 * 11 * 4 * 8 * 128, 128, B.bufonly("agk1"))
            emit_allgather(ctx, B, dr["sd_sh"], dr["ag_sd"],
                           4 * 8 * 11 * 128, 128, B.bufonly("agk2"))
        gs = set(getattr(ctx, "gather_streams", ()))
        skipE = ("gpsimd",) if with_gather else ()

        B.push_scope()
        B.ln_bufs("l0")
        emit_ln(ctx, B, dr["x"], dr["xnp"], "l0", permuted=True)
        B.pop_scope()
        ctx.fence(skip_streams=gs, skip_engines=skipE)
        B.push_scope()
        emit_w0prep(ctx, B, dr["maskT"], dr["wtp"], dr["p4d"], dr["sel4d"],
                    dr["w0t"])
        B.pop_scope()
        ctx.fence(skip_streams=gs, skip_engines=skipE)
        ctx.fence()
        if with_gather:
            B.pop_scope()
        B.push_scope()
        cast_pairs = [(dr[src], dr[dst]) for src, dst in CASTS]
        bg = make_cast_bg(ctx, B, cast_pairs)
        B.push_scope()
        emit_resident_load(ctx, B, dr["xnp"], "xr")
        B.push_scope()
        emit_s0(ctx, B, dr["w0t"], dr["s0"])
        B.pop_scope()
        B.push_scope()
        emit_ffn_up(ctx, B, dr["ag_gu"], dr["hd"], bg=None)
        B.pop_scope()
        B.pop_scope()
        ctx.fence()
        B.push_scope()
        emit_ffn_down(ctx, B, dr["ag_sd"], dr["hd"], dr["x"], dr["s0"],
                      dr["x1"], bg=None)
        B.pop_scope()
        while bg():
            pass
        B.pop_scope()
        ctx.fence()

        B.sbuf("rsum", [128, 64], F32)
        B.push_scope()
        B.ln_bufs("l1")
        emit_ln(ctx, B, dr["x1"], dr["xn1"], "l1", permuted=False,
                rsum="rsum", rsum_lo=611, rsum_hi=2047)
        B.pop_scope()
        ctx.fence()
        B.push_scope()
        B.sbuf("qhat", [128, 8, T], BF16)
        B.sbuf("khat", [128, 2, T], BF16)
        B.sbuf("khat4", [128, 4, T], BF16)
        B.sbuf("vtm", [128, 4, 16, 64], BF16)
        B.sbuf("dup64", [128, 128], BF16)
        ctx.dma("sync", B.sb["dup64"][:], dr["dup64_in"].ap[:, :],
                B.buf["dup64"], ins=[dr["dup64_in"]], outs=[B.buf["dup64"]])
        B.push_scope()
        Wd = dict(dr)
        Wd["bq"] = dr["bq_b"]
        Wd["mk"] = dr["mk_b"]
        Wd["mq"] = dr["mq_b"]
        Wd["kc2T"] = dr["kc2T_b"]
        Wd["bk"] = dr["bk_b"]
        emit_attn_pre(ctx, B, Wd, dr["xn1"])
        B.pop_scope()
        ctx.fence()
        B.push_scope()
        emit_attn(ctx, B, dr, dr["s1"])
        B.pop_scope()
        B.pop_scope()
        ctx.fence()
        B.push_scope()
        emit_resident_load(ctx, B, dr["xn1"], "xr1")
        emit_ffn1(ctx, B, dr, dr["s1"], dr["x1"], dr["out"])
        B.pop_scope()
        ctx.final_waits()
        ctx.run_block()
    mybir.codegen_inst_isa_subclasses(nc)
    return nc, ctx.nops


def build_G():
    nc = bass.Bass("TRN2", target_bir_lowering=False)
    es = ExitStack()
    with es:
        ctx = Ctx(es, nc)
        B = Builder(es, nc, ctx)
        dr = {}
        dr["x"] = B.dram("x", [T, D], F32, "ExternalInput")
        dr["maskT"] = B.dram("maskT", [32, 128, D], I32, "ExternalInput")
        dr["wtp"] = B.dram("wtp", [32, 128, D], F32, "ExternalInput")
        dr["cb"] = B.dram("cb", [65536, 8], F32, "ExternalInput")
        dr["idxc"] = B.dram("idxc", [128, 18432], I32, "ExternalInput")
        dr["p4d"] = B.dram("p4d", [32, 128, 128], F32, "ExternalInput")
        dr["sel4d"] = B.dram("sel4d", [4, 128, 128], BF16, "ExternalInput")
        dr["ident_in"] = B.dram("ident_in", [128, 128], BF16, "ExternalInput")
        dr["identf_in"] = B.dram("identf_in", [128, 128], F32,
                                 "ExternalInput")
        dr["dup64_in"] = B.dram("dup64_in", [128, 128], BF16, "ExternalInput")
        for nm, shp in WNAMES_F32.items():
            dr[nm] = B.dram(nm, list(shp), F32, "ExternalInput")
        dr["rgu"] = B.dram("rgu", [11, 2, 4, 128, 128, 8], F32, "Internal")
        dr["rd"] = B.dram("rd", [4, 11, 128, 128, 8], F32, "Internal")
        dr["agu0"] = B.dram("agu0", [6, 8, 8, 128, 128, 8], F32, "Internal")
        dr["agu1"] = B.dram("agu1", [5, 8, 8, 128, 128, 8], F32, "Internal")
        dr["agd"] = B.dram("agd", [4, 8, 11, 128, 128, 8], F32, "Internal")
        dr["sdF"] = B.dram("sdF", [32, 128, 8, 11, 128], BF16, "Internal")
        dr["xnp"] = B.dram("xnp", [ND, 128, T], BF16, "Internal")
        dr["w0t"] = B.dram("w0t", [ND, 128, D], BF16, "Internal")
        dr["s0"] = B.dram("s0", [T, D], F32, "Internal")
        dr["hd"] = B.dram("hd", [88, 128, T], BF16, "Internal")
        dr["x1"] = B.dram("x1", [T, D], F32, "Internal")
        dr["xn1"] = B.dram("xn1", [ND, 128, T], BF16, "Internal")
        dr["s1"] = B.dram("s1", [T, D], F32, "Internal")
        dr["out"] = B.dram("out", [T, D], F32, "ExternalOutput")
        for src, dst in CASTS:
            dr[dst] = B.dram(dst, list(WNAMES_F32[src]), BF16, "Internal")

        B.psum_main(4096)
        B.sbuf("ident", [128, 128], BF16)
        B.sbuf("identf", [128, 128], F32)
        ctx.dma("sync", B.sb["ident"][:], dr["ident_in"].ap[:, :],
                B.buf["ident"], ins=[dr["ident_in"]], outs=[B.buf["ident"]])
        ctx.dma("sync", B.sb["identf"][:], dr["identf_in"].ap[:, :],
                B.buf["identf"], ins=[dr["identf_in"]],
                outs=[B.buf["identf"]])

        B.push_scope()
        agst = Stream(es, nc, "agc")
        agu_bufs = [Buf(f"agu{c}") for c in range(11)]
        agd_bufs = [Buf(f"agd{c}") for c in range(4)]
        agu_sel = (dr["agu0"], dr["agu1"])
        emit_gather_fused(ctx, B, dr["cb"], dr["idxc"], dr["rgu"], dr["rd"],
                          agu_sel, dr["agd"], agst, agu_bufs, agd_bufs)

        skipE = ("gpsimd",)
        B.push_scope()
        B.ln_bufs("l0")
        emit_ln(ctx, B, dr["x"], dr["xnp"], "l0", permuted=True)
        B.pop_scope()
        ctx.fence()
        B.push_scope()
        emit_w0prep(ctx, B, dr["maskT"], dr["wtp"], dr["p4d"], dr["sel4d"],
                    dr["w0t"])
        B.pop_scope()
        ctx.fence()
        B.push_scope()
        cast_pairs = [(dr[src], dr[dst]) for src, dst in CASTS]
        bg = make_cast_bg(ctx, B, cast_pairs)
        B.push_scope()
        emit_resident_load(ctx, B, dr["xnp"], "xr")
        B.push_scope()
        emit_s0(ctx, B, dr["w0t"], dr["s0"])
        B.pop_scope()
        ctx.fence()
        B.push_scope()
        emit_ffn_up_raw(ctx, B, agu_sel, agu_bufs, dr["hd"])
        B.pop_scope()
        B.pop_scope()
        ctx.fence()
        B.push_scope()
        dc_waits = {}
        dgroup = make_dcast(ctx, B, dr["agd"], agd_bufs, dr["sdF"], dc_waits)
        fdpart = make_ffn_down(ctx, B, dr["sdF"], dr["hd"], dr["x"],
                               dr["s0"], dr["x1"], dc_waits)
        for dci in range(4):
            dgroup(dci)
            fdpart(0, [4 * r + dci for r in range(8)])
        dc_all = [4 * r + dci for dci in range(4) for r in range(8)]
        for tq in (1, 2, 3):
            fdpart(tq, dc_all)
        B.pop_scope()
        while bg():
            pass
        B.pop_scope()
        B.pop_scope()
        ctx.fence()

        B.sbuf("rsum", [128, 64], F32)
        B.push_scope()
        B.ln_bufs("l1")
        emit_ln(ctx, B, dr["x1"], dr["xn1"], "l1", permuted=False,
                rsum="rsum", rsum_lo=611, rsum_hi=2047)
        B.pop_scope()
        ctx.fence()
        B.push_scope()
        B.sbuf("qhat", [128, 8, T], BF16)
        B.sbuf("khat", [128, 2, T], BF16)
        B.sbuf("khat4", [128, 4, T], BF16)
        B.sbuf("vtm", [128, 4, 16, 64], BF16)
        B.sbuf("dup64", [128, 128], BF16)
        ctx.dma("sync", B.sb["dup64"][:], dr["dup64_in"].ap[:, :],
                B.buf["dup64"], ins=[dr["dup64_in"]], outs=[B.buf["dup64"]])
        B.push_scope()
        Wd = dict(dr)
        Wd["bq"] = dr["bq_b"]
        Wd["mk"] = dr["mk_b"]
        Wd["mq"] = dr["mq_b"]
        Wd["kc2T"] = dr["kc2T_b"]
        Wd["bk"] = dr["bk_b"]
        emit_attn_pre(ctx, B, Wd, dr["xn1"])
        B.pop_scope()
        ctx.fence()
        B.push_scope()
        emit_attn(ctx, B, dr, dr["s1"])
        B.pop_scope()
        B.pop_scope()
        ctx.fence()
        B.push_scope()
        emit_resident_load(ctx, B, dr["xn1"], "xr1")
        emit_ffn1(ctx, B, dr, dr["s1"], dr["x1"], dr["out"])
        B.pop_scope()
        ctx.final_waits()
        ctx.run_block()
    mybir.codegen_inst_isa_subclasses(nc)
    return nc


def kernel(**inputs):
    inp = {k: np.asarray(v) for k, v in inputs.items()}
    x_all = inp["L0_attn_data"].astype(np.float32)
    weight = inp["L0_attn_weight"].astype(np.float32)
    mask = inp["L0_attn_mask"].astype(np.int32)
    cb = inp["L0_ffn_codebook"].astype(np.float32)
    p4, sel4, identb, identf, dup = host_consts()
    wtp = np.ascontiguousarray(weight.T[PERM_H]).reshape(32, 128, D)
    maskT = np.ascontiguousarray(mask.T).reshape(32, 128, D)
    W1 = host_l1(inp)
    nc = build_G()
    ims = []
    for c in range(8):
        im = {"x": np.ascontiguousarray(x_all[c]), "maskT": maskT,
              "wtp": wtp, "cb": cb,
              "idxc": host_idxcalls_f(inp["L0_ffn_idx_gate"],
                                      inp["L0_ffn_idx_up"],
                                      inp["L0_ffn_idx_down"], c),
              "p4d": p4, "sel4d": sel4, "ident_in": identb,
              "identf_in": identf, "dup64_in": dup}
        for nm in WNAMES_F32:
            im[nm] = np.ascontiguousarray(W1[nm]).astype(np.float32)
        ims.append(im)
    res = run_bass_kernel_spmd(nc, ims, core_ids=list(range(8)))
    out = np.stack([r["out"] for r in res.results], axis=0)
    return out.astype(np.float32)


# version 16
# speedup vs baseline: 1.1803x; 1.1803x over previous
import os
import traceback
import numpy as np
import ml_dtypes
from contextlib import ExitStack

DEBUG_EMIT = os.environ.get("MK_DEBUG_EMIT", "0") == "1"
import concourse.bass as bass
import concourse.mybir as mybir
from concourse.bass_utils import run_bass_kernel_spmd

F32 = mybir.dt.float32
BF16 = mybir.dt.bfloat16
I32 = mybir.dt.int32
ALU = mybir.AluOpType
AF = mybir.ActivationFunctionType
AX = mybir.AxisListType
BF = ml_dtypes.bfloat16

T = 2048
D = 4096
I_FFN = 11008
NT = T // 128
ND = D // 128

PERM_H = np.array([1024 * (kk % 4) + 8 * r + (kk // 4)
                   for kk in range(32) for r in range(128)])
PERM_I = np.array([8 * (ip % 1376) + ip // 1376 for ip in range(I_FFN)])


class Stream:
    def __init__(self, es, nc, name):
        self.sem = es.enter_context(nc.semaphore(name))
        self.name = name
        self.n = 0


class Buf:
    __slots__ = ("writer", "readers", "name", "dstream")

    def __init__(self, name=""):
        self.writer = None
        self.readers = {}
        self.name = name
        self.dstream = None


class Ctx:
    ENGS = ["sync", "tensor", "vector", "scalar", "gpsimd"]

    def __init__(self, es, nc):
        self.nc = nc
        self.es = es
        self.ops = {e: [] for e in self.ENGS}
        self.t = Stream(es, nc, "t")
        self.v = Stream(es, nc, "v")
        self.s = Stream(es, nc, "s")
        self.p = Stream(es, nc, "p")
        self.waited = {}
        self.nops = 0
        self.dstreams = []
        self.free_dstreams = []
        self.nsem = 0

    def dstream_for(self, key):
        if key.dstream is None:
            if self.free_dstreams:
                key.dstream = self.free_dstreams.pop()
            else:
                key.dstream = Stream(self.es, self.nc, f"d{self.nsem}")
                self.nsem += 1
                self.dstreams.append(key.dstream)
        return key.dstream

    def emit(self, eng, fn):
        if DEBUG_EMIT:
            fn._tb = traceback.format_stack()[-4:-1]
        self.ops[eng].append(fn)
        self.nops += 1

    def _edges(self, stream, ins, outs):
        need = {}
        for b in ins:
            if b.writer:
                s, v = b.writer
                if not (s is stream and s.name == "t"):
                    need[s] = max(need.get(s, 0), v)
        for b in outs:
            if b.writer:
                s, v = b.writer
                if not (s is stream and s.name == "t"):
                    need[s] = max(need.get(s, 0), v)
            for s, v in b.readers.items():
                if not (s is stream and s.name == "t"):
                    need[s] = max(need.get(s, 0), v)
        waits = []
        for s, v in need.items():
            key = (stream, s)
            if self.waited.get(key, 0) >= v:
                continue
            self.waited[key] = v
            waits.append((s, v))
        return waits

    def _post(self, stream, ins, outs):
        for b in outs:
            b.writer = (stream, stream.n)
            b.readers = {}
        for b in ins:
            b.readers[stream] = stream.n

    def op(self, eng, fn, ins=(), outs=(), extra_waits=()):
        stream = {"tensor": self.t, "vector": self.v, "scalar": self.s,
                  "gpsimd": self.p}[eng]
        waits = self._edges(stream, ins, outs) + list(extra_waits)
        stream.n += 1

        def f(e, fn=fn, waits=waits, stream=stream):
            for s, v in waits:
                e.wait_ge(s.sem, v)
            fn(e).then_inc(stream.sem, 1)
        self.emit(eng, f)
        self._post(stream, ins, outs)
        return stream.n

    def dma(self, eng, out_ap, in_ap, key, ins=(), outs=(), extra_waits=(),
            **kw):
        stream = self.dstream_for(key)
        waits = self._edges(stream, ins, outs) + list(extra_waits)
        stream.n += 16

        def f(e, out_ap=out_ap, in_ap=in_ap, waits=waits, stream=stream, kw=kw):
            for s, v in waits:
                e.wait_ge(s.sem, v)
            e.dma_start(out=out_ap, in_=in_ap, **kw).then_inc(stream.sem, 16)
        self.emit(eng, f)
        self._post(stream, ins, outs)
        return stream.n

    def gather(self, out_ap, in_ap, idx_ap, key, ins=(), outs=(),
               extra_waits=()):
        stream = self.dstream_for(key)
        waits = self._edges(stream, ins, outs) + list(extra_waits)
        stream.n += 16

        def f(e, out_ap=out_ap, in_ap=in_ap, idx_ap=idx_ap, waits=waits):
            for s, v in waits:
                e.wait_ge(s.sem, v)
            e.indirect_dma_start(
                out_ap, None, in_ap,
                bass.IndirectOffsetOnAxis(idx_ap, 0)).then_inc(stream.sem, 16)
        self.emit("gpsimd", f)
        self._post(stream, ins, outs)
        return stream.n

    def barrier_wait(self, eng, stream, val):
        def f(e, stream=stream, val=val):
            e.wait_ge(stream.sem, val)
        self.emit(eng, f)

    def fence(self, skip_streams=(), skip_engines=()):
        skip = set(skip_streams)
        snap = [(st, st.n) for st in
                [self.t, self.v, self.s, self.p] + self.dstreams
                if st.n > 0 and st not in skip]
        engs = {"sync": None, "tensor": self.t, "vector": self.v,
                "scalar": self.s, "gpsimd": self.p}
        for eng, own in engs.items():
            if eng in skip_engines:
                continue
            for (st, v) in snap:
                if st is own:
                    continue
                self.barrier_wait(eng, st, v)
        for dst_st in [self.t, self.v, self.s, self.p] + self.dstreams:
            if dst_st in skip:
                continue
            for (st, v) in snap:
                self.waited[(dst_st, st)] = max(
                    self.waited.get((dst_st, st), 0), v)

    def final_waits(self, eng="gpsimd"):
        for st in [self.t, self.v, self.s, self.p] + self.dstreams:
            if st.n > 0:
                self.barrier_wait(eng, st, st.n)

    def run_block(self):
        nc = self.nc
        def _runall(e, fns):
            for f in fns:
                try:
                    f(e)
                except Exception:
                    if hasattr(f, "_tb"):
                        print("EMIT SITE:\n" + "".join(f._tb))
                    raise
        with nc.Block() as block:
            @block.sync
            def _(e):
                _runall(e, self.ops["sync"])

            @block.tensor
            def _(e):
                _runall(e, self.ops["tensor"])

            @block.vector
            def _(e):
                _runall(e, self.ops["vector"])

            @block.scalar
            def _(e):
                _runall(e, self.ops["scalar"])

            @block.gpsimd
            def _(e):
                _runall(e, self.ops["gpsimd"])


def emit_ln(ctx, B, x_dram, xnp_dram, names, *, permuted, rsum=None,
            rsum_lo=0, rsum_hi=0):
    nc = ctx.nc
    eps = 1e-5
    sb, bf = B.sb, B.buf
    idn = sb["ident"][:]
    if rsum is not None:
        B.bufonly(rsum + "_part")
        ctx.op("vector", lambda e: e.memset(sb[rsum][:], 0.0),
               ins=[], outs=[bf[rsum]])
    for tt in range(NT):
        sl = tt % 2
        xt, xtb = sb[f"{names}xt{sl}"], bf[f"{names}xt{sl}"]
        xnb, xnbb = sb[f"{names}xnb{sl}"], bf[f"{names}xnb{sl}"]
        sq, sqb = sb[f"{names}sq"], bf[f"{names}sq"]
        mom, momb = sb[f"{names}mom"], bf[f"{names}mom"]
        ctx.dma("sync", xt[:], x_dram.ap[tt * 128:(tt + 1) * 128, :], xtb,
                ins=[x_dram], outs=[xtb])
        ctx.op("vector", lambda e, xt=xt, mom=mom: e.reduce_sum(
            mom[:, 0:1], xt[:], AX.X), ins=[xtb], outs=[momb])
        ctx.op("scalar", lambda e, xt=xt, sq=sq: e.activation(
            sq[:], xt[:], AF.Square), ins=[xtb], outs=[sqb])
        ctx.op("vector", lambda e, sq=sq, mom=mom: e.reduce_sum(
            mom[:, 1:2], sq[:], AX.X), ins=[sqb], outs=[momb])
        ctx.op("vector", lambda e, mom=mom: e.tensor_scalar(
            mom[:, 0:1], mom[:, 0:1], 1.0 / D, None, ALU.mult),
            ins=[momb], outs=[momb])
        ctx.op("vector", lambda e, mom=mom: e.tensor_scalar(
            mom[:, 1:2], mom[:, 1:2], 1.0 / D, None, ALU.mult),
            ins=[momb], outs=[momb])
        ctx.op("vector", lambda e, mom=mom: e.tensor_tensor(
            mom[:, 2:3], mom[:, 0:1], mom[:, 0:1], ALU.mult),
            ins=[momb], outs=[momb])
        ctx.op("vector", lambda e, mom=mom: e.tensor_tensor(
            mom[:, 1:2], mom[:, 1:2], mom[:, 2:3], ALU.subtract),
            ins=[momb], outs=[momb])
        ctx.op("vector", lambda e, mom=mom: e.tensor_scalar(
            mom[:, 1:2], mom[:, 1:2], float(eps), None, ALU.add),
            ins=[momb], outs=[momb])
        ctx.op("scalar", lambda e, mom=mom: e.activation(
            mom[:, 2:3], mom[:, 1:2], AF.Sqrt),
            ins=[momb], outs=[momb])
        ctx.op("vector", lambda e, mom=mom: e.reciprocal(
            mom[:, 1:2], mom[:, 2:3]), ins=[momb], outs=[momb])
        ctx.op("vector", lambda e, mom=mom: e.tensor_tensor(
            mom[:, 3:4], mom[:, 0:1], mom[:, 1:2], ALU.mult),
            ins=[momb], outs=[momb])
        ctx.op("vector", lambda e, mom=mom: e.tensor_scalar(
            mom[:, 3:4], mom[:, 3:4], -1.0, None, ALU.mult),
            ins=[momb], outs=[momb])
        ctx.op("scalar", lambda e, xt=xt, xnb=xnb, mom=mom: e.activation(
            xnb[:], xt[:], AF.Identity, bias=mom[:, 3:4], scale=mom[:, 1:2]),
            ins=[xtb, momb], outs=[xnbb])
        for kk in range(ND):
            psl = (tt * ND + kk) % 2
            sl8 = (tt * ND + kk) % 8
            pst = B.ps_ap[:, 512 * psl:512 * psl + 64].bitcast(BF16)
            pstb = bf[f"pst{psl}"]
            tout, toutb = sb[f"{names}to{sl8}"], bf[f"{names}to{sl8}"]
            if permuted:
                j, b = kk // 4, kk % 4
                src = xnb[:, 1024 * b + j: 1024 * b + j + 1017: 8]
            else:
                src = xnb[:, kk * 128:(kk + 1) * 128]
            ctx.op("tensor", lambda e, src=src, pst=pst: e.transpose(
                pst, src, idn), ins=[xnbb, bf["ident"]], outs=[pstb])
            ctx.op("vector", lambda e, pst=pst, tout=tout: e.tensor_copy(
                tout[:], pst), ins=[pstb], outs=[toutb])
            if rsum is not None:
                lo = max(rsum_lo, tt * 128) - tt * 128
                hi = min(rsum_hi, (tt + 1) * 128) - tt * 128
                if hi > lo:
                    rs, rsb = sb[rsum], bf[rsum]
                    ctx.op("vector",
                           lambda e, tout=tout, rs=rs, kk=kk, lo=lo, hi=hi:
                           e.tensor_reduce(rs[:, 32 + kk:33 + kk],
                                           tout[:, lo:hi], AX.X, ALU.add),
                           ins=[toutb], outs=[bf[rsum + "_part"]])
                    ctx.op("vector",
                           lambda e, rs=rs, kk=kk: e.tensor_tensor(
                               rs[:, kk:kk + 1], rs[:, kk:kk + 1],
                               rs[:, 32 + kk:33 + kk], ALU.add),
                           ins=[bf[rsum + "_part"], rsb], outs=[rsb])
            ctx.dma("sync",
                    xnp_dram.ap[kk, :, tt * 128:(tt + 1) * 128], tout[:],
                    toutb, ins=[toutb], outs=[xnp_dram])


class DramBuf(Buf):
    __slots__ = ("ap",)

    def __init__(self, name, ap):
        super().__init__(name)
        self.ap = ap


class Builder:
    def __init__(self, es, nc, ctx=None):
        self.nc, self.es = nc, es
        self.sb = {}
        self.buf = {}
        self.ps_ap = None
        self.scopes = []
        self.scope_bufs = [[]]
        self.ctx = ctx

    def push_scope(self):
        st = ExitStack()
        self.scopes.append(st)
        self.scope_bufs.append([])
        return st

    def pop_scope(self):
        self.scopes.pop().close()
        for b in self.scope_bufs.pop():
            if b.dstream is not None and self.ctx is not None:
                self.ctx.free_dstreams.append(b.dstream)
                b.dstream = None

    def sbuf(self, name, shape, dtype):
        stack = self.scopes[-1] if self.scopes else self.es
        t = stack.enter_context(self.nc.sbuf_tensor(name, shape, dtype))
        self.sb[name] = t
        b = Buf(name)
        self.buf[name] = b
        self.scope_bufs[-1].append(b)
        return t

    def bufonly(self, name):
        self.buf[name] = Buf(name)
        return self.buf[name]

    def psum_main(self, cols=2048):
        t = self.es.enter_context(self.nc.psum_tensor("psmain", [128, cols], F32))
        self.ps_ap = t[:]
        self.buf["pst0"] = Buf("pst0")
        self.buf["pst1"] = Buf("pst1")
        return t

    def dram(self, name, shape, dtype, kind):
        t = self.nc.dram_tensor(name, list(shape), dtype, kind=kind)
        return DramBuf(name, t.ap())

    def ln_bufs(self, names):
        for sl in range(2):
            self.sbuf(f"{names}xt{sl}", [128, D], F32)
            self.sbuf(f"{names}xnb{sl}", [128, D], BF16)
        for sl in range(8):
            self.sbuf(f"{names}to{sl}", [128, 128], BF16)
        self.sbuf(f"{names}sq", [128, D], F32)
        self.sbuf(f"{names}mom", [128, 8], F32)


def mm_split(ctx, ps_ap, lhsT_ap, rhs_ap, ins, outs, start, stop, nmax=512):
    n = rhs_ap.shape[-1]
    for o in range(0, n, nmax):
        w = min(nmax, n - o)
        ctx.op("tensor", lambda e, o=o, w=w: e.matmul(
            ps_ap[:, o:o + w], lhsT_ap, rhs_ap[:, o:o + w],
            start=start, stop=stop), ins=ins, outs=outs)


def emit_w0prep(ctx, B, maskT_dram, wtp_dram, p4_dram, sel4_dram, w0t_dram):
    sb, bf = B.sb, B.buf
    p4 = B.sbuf("p4", [128, 32, 128], F32)
    sel4 = B.sbuf("sel4", [128, 4, 128], BF16)
    ctx.dma("sync", p4[:], p4_dram.ap.rearrange("a p b -> p a b"),
            bf["p4"], ins=[p4_dram], outs=[bf["p4"]])
    ctx.dma("sync", sel4[:], sel4_dram.ap.rearrange("a p b -> p a b"),
            bf["sel4"], ins=[sel4_dram], outs=[bf["sel4"]])
    for sl in range(2):
        B.sbuf(f"w2m{sl}", [128, D], I32)
        B.sbuf(f"w2mf{sl}", [128, D], F32)
        B.sbuf(f"w2w{sl}", [128, D], F32)
        B.sbuf(f"w2o{sl}", [128, D], BF16)
    bmf = B.sbuf("bmf", [128, D], BF16)
    B.bufonly("pbs0")
    B.bufonly("pbs1")
    B.bufonly("pem0")
    B.bufonly("pem1")
    for Tt in range(32):
        sl = Tt % 2
        mt, mtb = sb[f"w2m{sl}"], bf[f"w2m{sl}"]
        mf, mfb = sb[f"w2mf{sl}"], bf[f"w2mf{sl}"]
        ctx.dma("sync", mt[:], maskT_dram.ap[Tt, :, :], mtb,
                ins=[maskT_dram], outs=[mtb])
        ctx.op("vector", lambda e, mt=mt, mf=mf: e.tensor_copy(mf[:], mt[:]),
               ins=[mtb], outs=[mfb])
        for h in range(2):
            mm_split(ctx, B.ps_ap[:, 2048 * h:2048 * h + 2048], p4[:, Tt, :],
                     mf[:, 2048 * h:2048 * h + 2048],
                     ins=[mfb, bf["p4"]], outs=[bf[f"pbs{h}"]],
                     start=(Tt == 0), stop=(Tt == 31))
    ctx.op("vector", lambda e: e.tensor_scalar(
        bmf[:, 0:2048], B.ps_ap[:, 0:2048], 0.0, None, ALU.is_gt),
        ins=[bf["pbs0"]], outs=[bf["bmf"]])
    ctx.op("vector", lambda e: e.tensor_scalar(
        bmf[:, 2048:4096], B.ps_ap[:, 2048:4096], 0.0, None, ALU.is_gt),
        ins=[bf["pbs1"]], outs=[bf["bmf"]])
    for kk in range(32):
        sl = kk % 2
        wt, wtb = sb[f"w2w{sl}"], bf[f"w2w{sl}"]
        wo, wob = sb[f"w2o{sl}"], bf[f"w2o{sl}"]
        ctx.dma("sync", wt[:], wtp_dram.ap[kk, :, :], wtb,
                ins=[wtp_dram], outs=[wtb])
        b = kk % 4
        for h in range(2):
            pe0 = bf[f"pem{h}"]
            mm_split(ctx, B.ps_ap[:, 2048 * h:2048 * h + 2048],
                     sel4[:, b, :], bmf[:, 2048 * h:2048 * h + 2048],
                     ins=[bf["bmf"], bf["sel4"]], outs=[pe0],
                     start=True, stop=True)
            ctx.op("vector", lambda e, wt=wt, wo=wo, h=h: e.tensor_tensor(
                wo[:, 2048 * h:2048 * h + 2048],
                wt[:, 2048 * h:2048 * h + 2048],
                B.ps_ap[:, 2048 * h:2048 * h + 2048], ALU.mult),
                ins=[wtb, pe0], outs=[wob])
        ctx.dma("sync", w0t_dram.ap[kk, :, :], wo[:], wob,
                ins=[wob], outs=[w0t_dram])


def emit_resident_load(ctx, B, xnp_dram, name="xr"):
    xr = B.sbuf(name, [128, ND, T], BF16)
    ctx.dma("sync", xr[:, :, :],
            xnp_dram.ap.rearrange("k p t -> p k t"), B.buf[name],
            ins=[xnp_dram], outs=[B.buf[name]])
    return xr


def emit_s0(ctx, B, w0t_dram, s0_dram):
    sb, bf = B.sb, B.buf
    xr = sb["xr"]
    B.bufonly("ps30")
    B.bufonly("ps3t0")
    B.bufonly("ps3t1")
    for sl in range(2):
        B.sbuf(f"s3W{sl}", [128, 32, 128], BF16)
    for sl in range(2):
        B.sbuf(f"s3e{sl}", [128, 2048], F32)
    for sl in range(8):
        B.sbuf(f"s3o{sl}", [128, 128], F32)
    for nc_ in range(ND):
        ps = B.ps_ap[:, 0:2048]
        psb = bf["ps30"]
        sl = nc_ % 2
        wt, wtb = sb[f"s3W{sl}"], bf[f"s3W{sl}"]
        ctx.dma("sync", wt[:],
                w0t_dram.ap[:, :, nc_ * 128:(nc_ + 1) * 128].rearrange(
                    "k p c -> p k c"),
                wtb, ins=[w0t_dram], outs=[wtb])
        for kk in range(ND):
            mm_split(ctx, ps, wt[:, kk, :], xr[:, kk, :],
                     ins=[wtb, bf["xr"]], outs=[psb],
                     start=(kk == 0), stop=(kk == 31))
        sl2 = nc_ % 2
        ev, evb = sb[f"s3e{sl2}"], bf[f"s3e{sl2}"]
        ctx.op("vector", lambda e, ev=ev, ps=ps: e.tensor_copy(ev[:], ps),
               ins=[psb], outs=[evb])
        for tc in range(16):
            psl = tc % 2
            sl8 = tc % 8
            pst = B.ps_ap[:, 2048 + 512 * psl:2048 + 512 * psl + 128]
            pstb = bf[f"ps3t{psl}"]
            so, sob = sb[f"s3o{sl8}"], bf[f"s3o{sl8}"]
            ctx.op("tensor", lambda e, ev=ev, tc=tc, pst=pst: e.transpose(
                pst, ev[:, tc * 128:(tc + 1) * 128], sb["identf"][:]),
                ins=[evb, bf["identf"]], outs=[pstb])
            ctx.op("vector", lambda e, so=so, pst=pst: e.tensor_copy(
                so[:], pst), ins=[pstb], outs=[sob])
            ctx.dma("sync",
                    s0_dram.ap[tc * 128:(tc + 1) * 128, nc_ * 128:(nc_ + 1) * 128],
                    so[:], sob, ins=[sob], outs=[s0_dram])


CNTS = [128] * 10 + [96]


def emit_gather(ctx, B, cb_dram, idx_dram, sgu_dram, sd_dram, core,
                dma_eng="sync"):
    sb, bf = B.sb, B.buf
    ctx.gather_streams = getattr(ctx, "gather_streams", set())
    cbsrc = cb_dram.ap
    idxt = B.sbuf("g4idx", [128, 4096], I32)
    for sl in range(2):
        B.sbuf(f"g4d{sl}", [128, 128, 8], F32)
        B.sbuf(f"g4s{sl}", [128, 8, 128], BF16)
    call = 0
    seg = 0
    idx_chunk = -1
    last_reorder = [0, 0]

    def idx_col(k):
        nonlocal idx_chunk
        cc = k // 4096
        if cc != idx_chunk:
            w = min(4096, idx_dram.ap.shape[1] - cc * 4096)
            ctx.dma(dma_eng, idxt[:, 0:w],
                    idx_dram.ap[:, cc * 4096:cc * 4096 + w], bf["g4idx"],
                    ins=[idx_dram], outs=[bf["g4idx"]])
            ctx.gather_streams.add(bf["g4idx"].dstream)
            idx_chunk = cc
        return idxt[:, (k % 4096):(k % 4096) + 1]

    def do_segment(ncalls, npart, store_fn):
        nonlocal call, seg
        sl = seg % 2
        seg += 1
        dstb, dstbb = sb[f"g4d{sl}"], bf[f"g4d{sl}"]
        stg, stgb = sb[f"g4s{sl}"], bf[f"g4s{sl}"]
        ds = ctx.dstream_for(dstbb)
        ctx.gather_streams.add(ds)
        for s in range(ncalls):
            ew = [(ctx.v, last_reorder[sl])] if s == 0 and last_reorder[sl] else ()
            ctx.gather(dstb[0:npart, s, :], cbsrc, idx_col(call + s),
                       key=dstbb, ins=[cb_dram, bf["g4idx"]], outs=(),
                       extra_waits=ew)
        src = dstb[0:128, 0:ncalls, 0:8].rearrange("p s j -> p j s")
        ctx.op("vector", lambda e, src=src, stg=stg, ncalls=ncalls:
               e.tensor_copy(stg[:, :, 0:ncalls], src),
               ins=[], outs=[stgb], extra_waits=[(ds, ds.n)])
        last_reorder[sl] = ctx.v.n
        store_fn(stg, stgb, ncalls)
        call += ncalls

    for m in range(2):
        for cdc in range(11):
            cnt = CNTS[cdc]
            for b in range(4):
                def store_gu(stg, stgb, ncalls, m=m, cdc=cdc, b=b, cnt=cnt):
                    for j in range(8):
                        ctx.dma(dma_eng,
                                sgu_dram.ap[m, cdc, :, b, j, 0:cnt],
                                stg[:, j, 0:cnt], stgb,
                                ins=[stgb], outs=[sgu_dram])
                        ctx.gather_streams.add(stgb.dstream)
                do_segment(cnt, 128, store_gu)
    for dci in range(4):
        for cdc in range(11):
            cnt = CNTS[cdc]
            def store_d(stg, stgb, ncalls, dci=dci, cdc=cdc, cnt=cnt):
                for j in range(8):
                    ctx.dma(dma_eng,
                            sd_dram.ap[dci, 0:cnt, j, cdc, :],
                            stg[0:cnt, j, 0:128], stgb,
                            ins=[stgb], outs=[sd_dram])
                    ctx.gather_streams.add(stgb.dstream)
            do_segment(128, cnt, store_d)


def host_idxcalls(idx_gate, idx_up, idx_down, core):
    cols = []
    jj = core
    for idx_m in (idx_gate, idx_up):
        for cdc in range(11):
            cnt = CNTS[cdc]
            for b in range(4):
                for s in range(cnt):
                    row = 8 * (cdc * 128 + s) + jj
                    cols.append(idx_m[row, 128 * b:128 * b + 128])
    for dci in range(4):
        dc = 4 * core + dci
        for cdc in range(11):
            cnt = CNTS[cdc]
            for s in range(128):
                d = dc * 128 + s
                col = np.zeros(128, dtype=np.int32)
                col[0:cnt] = idx_down[d, 128 * cdc:128 * cdc + cnt]
                cols.append(col)
    arr = np.stack(cols, axis=1).astype(np.int32)
    pad = (-arr.shape[1]) % 4096
    if pad:
        arr = np.concatenate([arr, np.zeros((128, pad), np.int32)], axis=1)
    return np.ascontiguousarray(arr)


def emit_ffn_up(ctx, B, sgu_dram, h_dram, bg=None):
    sb, bf = B.sb, B.buf
    xr = sb["xr"]
    for sl in range(2):
        for m in range(2):
            B.sbuf(f"f5w{m}{sl}", [128, 32, 128], BF16)
        B.sbuf(f"f5h{sl}", [128, T], BF16)
    B.sbuf("f5a", [128, T], F32)
    B.bufonly("ps5g")
    B.bufonly("ps5u")
    for jj in range(8):
        for cdc in range(11):
            if bg is not None:
                bg()
            iic = jj * 11 + cdc
            cnt = CNTS[cdc]
            sl = iic % 2
            for m in range(2):
                w, wb = sb[f"f5w{m}{sl}"], bf[f"f5w{m}{sl}"]
                if len(sgu_dram.ap.shape) == 7:
                    src_ap = sgu_dram.ap[jj, m, cdc].rearrange(
                        "c b j i -> c (b j) i")
                else:
                    src_ap = sgu_dram.ap[m, iic].rearrange(
                        "c b j i -> c (b j) i")
                ctx.dma("sync", w[:], src_ap,
                        wb, ins=[sgu_dram], outs=[wb])
                ps = B.ps_ap[:, 2048 * m:2048 * m + 2048]
                psb = bf["ps5g"] if m == 0 else bf["ps5u"]
                for b in range(4):
                    for j in range(8):
                        kk = j * 4 + b
                        mm_split(ctx, ps[0:cnt, :], w[:, b * 8 + j, 0:cnt],
                                 xr[:, kk, :], ins=[wb, bf["xr"]], outs=[psb],
                                 start=(b == 0 and j == 0),
                                 stop=(b == 3 and j == 7))
            act, actb = sb["f5a"], bf["f5a"]
            hb, hbb = sb[f"f5h{sl}"], bf[f"f5h{sl}"]
            ctx.op("scalar", lambda e, act=act, cnt=cnt: e.activation(
                act[0:cnt, :], B.ps_ap[0:cnt, 0:2048], AF.Silu),
                ins=[bf["ps5g"]], outs=[actb])
            ctx.op("vector", lambda e, act=act, hb=hb, cnt=cnt: e.tensor_tensor(
                hb[0:cnt, :], act[0:cnt, :], B.ps_ap[0:cnt, 2048:4096],
                ALU.mult), ins=[actb, bf["ps5u"]], outs=[hbb])
            ctx.dma("sync", h_dram.ap[iic, 0:cnt, :], hb[0:cnt, :], hbb,
                    ins=[hbb], outs=[h_dram])


def emit_ffn_down(ctx, B, sd_dram, h_dram, x_dram, s0_dram,
                  x1_dram, bg=None, dc_order=None, dc_waits=None):
    sb, bf = B.sb, B.buf
    hq = B.sbuf("f6h", [128, 88, 512], BF16)
    for sl in range(2):
        B.sbuf(f"f6w{sl}", [128, 88, 128], BF16)
        B.sbuf(f"f6g{sl}", [128, 512], F32)
        B.bufonly(f"ps6y{sl}")
        B.bufonly(f"ps6t{sl}")
    for sl in range(8):
        B.sbuf(f"f6x{sl}", [128, 128], F32)
        B.sbuf(f"f6s{sl}", [128, 128], F32)
        B.sbuf(f"f6o{sl}", [128, 128], F32)
        B.sbuf(f"f6t{sl}", [128, 128], F32)
    for tq in range(4):
        ctx.dma("sync", hq[:, :, :],
                h_dram.ap[:, :, tq * 512:(tq + 1) * 512].rearrange(
                    "i p t -> p i t"), bf["f6h"],
                ins=[h_dram], outs=[bf["f6h"]])
        for dci_, dc in enumerate(dc_order or range(32)):
            if bg is not None:
                bg()
            sl = dci_ % 2
            w, wb = sb[f"f6w{sl}"], bf[f"f6w{sl}"]
            if len(sd_dram.ap.shape) == 6:
                src_ap = sd_dram.ap[dc // 4, dc % 4].rearrange(
                    "p j c d -> p (j c) d")
            else:
                src_ap = sd_dram.ap[dc].rearrange("p j c d -> p (j c) d")
            ew = tuple(dc_waits.get(dc, ())) if dc_waits else ()
            ctx.dma("sync", w[:], src_ap,
                    wb, ins=([] if dc_waits else [sd_dram]), outs=[wb],
                    extra_waits=ew)
            ps = B.ps_ap[:, 512 * sl:512 * sl + 512]
            psb = bf[f"ps6y{sl}"]
            for jj in range(8):
                for cdc in range(11):
                    i88 = jj * 11 + cdc
                    cnt = CNTS[cdc]
                    ctx.op("tensor", lambda e, w=w, i88=i88, cnt=cnt, ps=ps:
                           e.matmul(ps, w[0:cnt, i88, :], hq[0:cnt, i88, :],
                                    start=(i88 == 0), stop=(i88 == 87)),
                           ins=[wb, bf["f6h"]], outs=[psb])
            g0, g0b = sb[f"f6g{sl}"], bf[f"f6g{sl}"]
            ctx.op("scalar", lambda e, g0=g0, ps=ps: e.activation(
                g0[:], ps, AF.Silu), ins=[psb], outs=[g0b])
            for ts in range(4):
                sl8 = (dci_ * 4 + ts) % 8
                pst = B.ps_ap[:, 1024 + 512 * sl:1024 + 512 * sl + 128]
                pstb = bf[f"ps6t{sl}"]
                gt, gtb = sb[f"f6t{sl8}"], bf[f"f6t{sl8}"]
                xt, xtb = sb[f"f6x{sl8}"], bf[f"f6x{sl8}"]
                st, stb = sb[f"f6s{sl8}"], bf[f"f6s{sl8}"]
                ot, otb = sb[f"f6o{sl8}"], bf[f"f6o{sl8}"]
                rows = slice((tq * 4 + ts) * 128, (tq * 4 + ts) * 128 + 128)
                cols = slice(dc * 128, dc * 128 + 128)
                ctx.op("tensor", lambda e, g0=g0, ts=ts, pst=pst: e.transpose(
                    pst, g0[:, ts * 128:(ts + 1) * 128], sb["identf"][:]),
                    ins=[g0b, bf["identf"]], outs=[pstb])
                ctx.op("vector", lambda e, gt=gt, pst=pst: e.tensor_copy(
                    gt[:], pst), ins=[pstb], outs=[gtb])
                ctx.dma("sync", xt[:], x_dram.ap[rows, cols], xtb,
                        ins=[x_dram], outs=[xtb])
                ctx.dma("scalar", st[:], s0_dram.ap[rows, cols], stb,
                        ins=[s0_dram], outs=[stb])
                ctx.op("vector", lambda e, gt=gt, st=st: e.tensor_tensor(
                    gt[:], gt[:], st[:], ALU.mult), ins=[gtb, stb], outs=[gtb])
                ctx.op("vector", lambda e, gt=gt, xt=xt, ot=ot: e.tensor_tensor(
                    ot[:], gt[:], xt[:], ALU.add), ins=[gtb, xtb], outs=[otb])
                ctx.dma("sync", x1_dram.ap[rows, cols], ot[:], otb,
                        ins=[otb], outs=[x1_dram])


def emit_gather_fused(ctx, B, cb_dram, idx_dram, rgu, rd, agu, agd,
                      agst, agu_bufs, agd_bufs):
    nc = ctx.nc
    es = ctx.es
    idxt = B.sbuf("gfidx", [128, 2, 1024], I32)
    d0 = B.sbuf("gfd0", [128, 128, 8], F32)
    d1 = B.sbuf("gfd1", [128, 128, 8], F32)
    slots = [d0, d1]
    sg = [Stream(es, nc, "gfg0"), Stream(es, nc, "gfg1")]
    ss = [Stream(es, nc, "gfs0"), Stream(es, nc, "gfs1")]
    si = Stream(es, nc, "gfi")
    cbsrc = cb_dram.ap
    nchunks = idx_dram.ap.shape[1] // 1024
    state = {"call": 0, "chunk": -1, "seg": 0, "si_waited": 0, "loaded": 0}

    def load_chunk(cc):
        def fl(e, cc=cc):
            e.dma_start(out=idxt[:, cc % 2, :],
                        in_=idx_dram.ap[:, cc * 1024:cc * 1024 + 1024]
                        ).then_inc(si.sem, 16)
        ctx.emit("gpsimd", fl)
        si.n += 16
        state["loaded"] = cc + 1

    load_chunk(0)
    load_chunk(1)

    def idx_col(k, npart):
        cc = k // 1024
        if cc != state["chunk"]:
            state["chunk"] = cc
            if state["loaded"] < nchunks and state["loaded"] <= cc + 1:
                load_chunk(state["loaded"])
        return (idxt[0:npart, cc % 2, (k % 1024):(k % 1024) + 1],
                16 * (cc + 1))

    def do_segment(ncalls, npart, dst_ap):
        sl = state["seg"] % 2
        state["seg"] += 1
        dstb = slots[sl]
        base = state["call"]
        for s in range(ncalls):
            icol, sival = idx_col(base + s, npart)
            waits = []
            if sival > state["si_waited"]:
                waits.append((si.sem, sival))
                state["si_waited"] = sival
            if s == 0 and ss[sl].n > 0:
                waits.append((ss[sl].sem, ss[sl].n))

            def fg(e, icol=icol, dst=dstb[0:npart, s, :], waits=tuple(waits),
                   sl=sl):
                for sem, v in waits:
                    e.wait_ge(sem, v)
                e.indirect_dma_start(
                    dst, None, cbsrc,
                    bass.IndirectOffsetOnAxis(icol, 0)).then_inc(
                        sg[sl].sem, 16)
            ctx.emit("gpsimd", fg)
            sg[sl].n += 16
        state["call"] += ncalls

        def fs(e, dst_ap=dst_ap, src=(dstb[:, 0:ncalls, :] if npart == 128
                                      else dstb[0:npart, :, :]),
               gv=sg[sl].n, sl=sl):
            e.wait_ge(sg[sl].sem, gv)
            e.dma_start(out=dst_ap, in_=src).then_inc(ss[sl].sem, 16)
        ctx.emit("gpsimd", fs)
        ss[sl].n += 16

    def emit_ag(in2d, out2d, out_buf):
        w = [(ss[0].sem, ss[0].n), (ss[1].sem, ss[1].n)]
        agst.n += 1

        def fc(e, in2d=in2d, out2d=out2d, w=tuple(w)):
            for sem, v in w:
                e.wait_ge(sem, v)
            e.collective_compute(
                "AllGather", ALU.bypass,
                replica_groups=[list(range(8))],
                ins=[in2d.opt()], outs=[out2d.opt()]).then_inc(agst.sem, 1)
        ctx.emit("gpsimd", fc)
        out_buf.writer = (agst, agst.n)
        out_buf.readers = {}

    for cdc in range(11):
        cnt = CNTS[cdc]
        for m in range(2):
            for b in range(4):
                do_segment(cnt, 128, rgu.ap[cdc, m, b, :, 0:cnt, :])
        agu_h = agu[0] if cdc < 6 else agu[1]
        emit_ag(rgu.ap[cdc].rearrange("m b c s j -> (m b c) (s j)"),
                agu_h.ap[cdc if cdc < 6 else cdc - 6].rearrange(
                    "r u c s j -> (r u c) (s j)"),
                agu_bufs[cdc])
    for dci in range(4):
        for cdc in range(11):
            cnt = CNTS[cdc]
            do_segment(128, cnt, rd.ap[dci, cdc, 0:cnt, :, :])
        emit_ag(rd.ap[dci].rearrange("c p s j -> (c p) (s j)"),
                agd.ap[dci].rearrange("r c p s j -> (r c p) (s j)"),
                agd_bufs[dci])


def host_idxcalls_f(idx_gate, idx_up, idx_down, core):
    cols = []
    jj = core
    for cdc in range(11):
        cnt = CNTS[cdc]
        for idx_m in (idx_gate, idx_up):
            for b in range(4):
                for s in range(cnt):
                    row = 8 * (cdc * 128 + s) + jj
                    cols.append(idx_m[row, 128 * b:128 * b + 128])
    for dci in range(4):
        dc = 4 * core + dci
        for cdc in range(11):
            cnt = CNTS[cdc]
            for s in range(128):
                d = dc * 128 + s
                col = np.zeros(128, dtype=np.int32)
                col[0:cnt] = idx_down[d, 128 * cdc:128 * cdc + cnt]
                cols.append(col)
    arr = np.stack(cols, axis=1).astype(np.int32)
    pad = (-arr.shape[1]) % 2048
    if pad:
        arr = np.concatenate([arr, np.zeros((128, pad), np.int32)], axis=1)
    return np.ascontiguousarray(arr)


def emit_ffn_up_raw(ctx, B, agu, agu_bufs, h_dram):
    sb, bf = B.sb, B.buf
    xr = sb["xr"]
    for sl in range(2):
        B.sbuf(f"u5r{sl}", [128, 128, 8], F32)
        B.sbuf(f"u5w{sl}", [128, 8, 128], BF16)
        B.sbuf(f"f5h{sl}", [128, T], BF16)
    B.sbuf("f5a", [128, T], F32)
    B.bufonly("ps5g")
    B.bufonly("ps5u")
    unit = 0
    nchunk = 0
    for cdc in range(11):
        for jj in range(8):
            iic = jj * 11 + cdc
            cnt = CNTS[cdc]
            hsl = nchunk % 2
            nchunk += 1
            for m in range(2):
                ps = B.ps_ap[:, 2048 * m:2048 * m + 2048]
                psb = bf["ps5g"] if m == 0 else bf["ps5u"]
                for b in range(4):
                    sl = unit % 2
                    unit += 1
                    fr, frb = sb[f"u5r{sl}"], bf[f"u5r{sl}"]
                    w2, w2b = sb[f"u5w{sl}"], bf[f"u5w{sl}"]
                    agu_h = agu[0] if cdc < 6 else agu[1]
                    ctx.dma("sync", fr[:, 0:cnt, :],
                            agu_h.ap[cdc if cdc < 6 else cdc - 6, jj,
                                     m * 4 + b, :, 0:cnt, :], frb,
                            ins=[agu_bufs[cdc]], outs=[frb])
                    ctx.op("vector", lambda e, fr=fr, w2=w2, cnt=cnt:
                           e.tensor_copy(w2[:, :, 0:cnt],
                                         fr[:, 0:cnt, :].rearrange(
                                             "c s j -> c j s")),
                           ins=[frb], outs=[w2b])
                    for j in range(8):
                        kk = j * 4 + b
                        mm_split(ctx, ps[0:cnt, :], w2[:, j, 0:cnt],
                                 xr[:, kk, :], ins=[w2b, bf["xr"]], outs=[psb],
                                 start=(b == 0 and j == 0),
                                 stop=(b == 3 and j == 7))
            act, actb = sb["f5a"], bf["f5a"]
            hb, hbb = sb[f"f5h{hsl}"], bf[f"f5h{hsl}"]
            ctx.op("scalar", lambda e, act=act, cnt=cnt: e.activation(
                act[0:cnt, :], B.ps_ap[0:cnt, 0:2048], AF.Silu),
                ins=[bf["ps5g"]], outs=[actb])
            ctx.op("vector", lambda e, act=act, hb=hb, cnt=cnt: e.tensor_tensor(
                hb[0:cnt, :], act[0:cnt, :], B.ps_ap[0:cnt, 2048:4096],
                ALU.mult), ins=[actb, bf["ps5u"]], outs=[hbb])
            ctx.dma("sync", h_dram.ap[iic, 0:cnt, :], hb[0:cnt, :], hbb,
                    ins=[hbb], outs=[h_dram])


def make_dcast(ctx, B, agd, agd_bufs, sdF, dc_waits):
    sb, bf = B.sb, B.buf
    for sl in range(2):
        B.sbuf(f"dc6r{sl}", [128, 128, 8], F32)
        B.sbuf(f"dc6w{sl}", [128, 8, 128], BF16)
    B.bufonly("dc6st")
    state = {"u": 0}
    stb = bf["dc6st"]

    def group(dci):
        for rank in range(8):
            dc = 4 * rank + dci
            for cdc in range(11):
                cnt = CNTS[cdc]
                sl = state["u"] % 2
                state["u"] += 1
                fr, frb = sb[f"dc6r{sl}"], bf[f"dc6r{sl}"]
                w2, w2b = sb[f"dc6w{sl}"], bf[f"dc6w{sl}"]
                ctx.dma("sync", fr[0:cnt, :, :],
                        agd.ap[dci, rank, cdc, 0:cnt, :, :], frb,
                        ins=[agd_bufs[dci]], outs=[frb])
                ctx.op("vector", lambda e, fr=fr, w2=w2, cnt=cnt:
                       e.tensor_copy(w2[0:cnt, :, :],
                                     fr[0:cnt, :, :].rearrange(
                                         "c d j -> c j d")),
                       ins=[frb], outs=[w2b])
                ctx.dma("sync", sdF.ap[dc, 0:cnt, :, cdc, :],
                        w2[0:cnt, :, :], stb, ins=[w2b], outs=[sdF])
            st = ctx.dstream_for(stb)
            dc_waits[dc] = [(st, st.n)]
    return group


def make_ffn_down(ctx, B, sd_dram, h_dram, x_dram, s0_dram, x1_dram,
                  dc_waits):
    sb, bf = B.sb, B.buf
    hq = B.sbuf("f6h", [128, 88, 512], BF16)
    for sl in range(2):
        B.sbuf(f"f6w{sl}", [128, 88, 128], BF16)
        B.sbuf(f"f6g{sl}", [128, 512], F32)
        B.bufonly(f"ps6y{sl}")
        B.bufonly(f"ps6t{sl}")
    for sl in range(8):
        B.sbuf(f"f6x{sl}", [128, 128], F32)
        B.sbuf(f"f6s{sl}", [128, 128], F32)
        B.sbuf(f"f6o{sl}", [128, 128], F32)
        B.sbuf(f"f6t{sl}", [128, 128], F32)
    state = {"tq": -1, "u": 0}

    def part(tq, dcs):
        if tq != state["tq"]:
            ctx.dma("sync", hq[:, :, :],
                    h_dram.ap[:, :, tq * 512:(tq + 1) * 512].rearrange(
                        "i p t -> p i t"), bf["f6h"],
                    ins=[h_dram], outs=[bf["f6h"]])
            state["tq"] = tq
        for dc in dcs:
            u = state["u"]
            state["u"] += 1
            sl = u % 2
            w, wb = sb[f"f6w{sl}"], bf[f"f6w{sl}"]
            src_ap = sd_dram.ap[dc].rearrange("p j c d -> p (j c) d")
            ew = tuple(dc_waits.get(dc, ()))
            ctx.dma("scalar", w[:], src_ap, wb, ins=[], outs=[wb],
                    extra_waits=ew)
            ps = B.ps_ap[:, 512 * sl:512 * sl + 512]
            psb = bf[f"ps6y{sl}"]
            for jj in range(8):
                for cdc in range(11):
                    i88 = jj * 11 + cdc
                    cnt = CNTS[cdc]
                    ctx.op("tensor", lambda e, w=w, i88=i88, cnt=cnt, ps=ps:
                           e.matmul(ps, w[0:cnt, i88, :], hq[0:cnt, i88, :],
                                    start=(i88 == 0), stop=(i88 == 87)),
                           ins=[wb, bf["f6h"]], outs=[psb])
            g0, g0b = sb[f"f6g{sl}"], bf[f"f6g{sl}"]
            ctx.op("scalar", lambda e, g0=g0, ps=ps: e.activation(
                g0[:], ps, AF.Silu), ins=[psb], outs=[g0b])
            for ts in range(4):
                sl8 = (u * 4 + ts) % 8
                pst = B.ps_ap[:, 1024 + 512 * sl:1024 + 512 * sl + 128]
                pstb = bf[f"ps6t{sl}"]
                gt, gtb = sb[f"f6t{sl8}"], bf[f"f6t{sl8}"]
                xt, xtb = sb[f"f6x{sl8}"], bf[f"f6x{sl8}"]
                st, stb = sb[f"f6s{sl8}"], bf[f"f6s{sl8}"]
                ot, otb = sb[f"f6o{sl8}"], bf[f"f6o{sl8}"]
                rows = slice((tq * 4 + ts) * 128, (tq * 4 + ts) * 128 + 128)
                cols = slice(dc * 128, dc * 128 + 128)
                ctx.op("tensor", lambda e, g0=g0, ts=ts, pst=pst: e.transpose(
                    pst, g0[:, ts * 128:(ts + 1) * 128], sb["identf"][:]),
                    ins=[g0b, bf["identf"]], outs=[pstb])
                ctx.op("vector", lambda e, gt=gt, pst=pst: e.tensor_copy(
                    gt[:], pst), ins=[pstb], outs=[gtb])
                ctx.dma("sync", xt[:], x_dram.ap[rows, cols], xtb,
                        ins=[x_dram], outs=[xtb])
                ctx.dma("scalar", st[:], s0_dram.ap[rows, cols], stb,
                        ins=[s0_dram], outs=[stb])
                ctx.op("vector", lambda e, gt=gt, st=st: e.tensor_tensor(
                    gt[:], gt[:], st[:], ALU.mult), ins=[gtb, stb], outs=[gtb])
                ctx.op("vector", lambda e, gt=gt, xt=xt, ot=ot: e.tensor_tensor(
                    ot[:], gt[:], xt[:], ALU.add), ins=[gtb, xtb], outs=[otb])
                ctx.dma("sync", x1_dram.ap[rows, cols], ot[:], otb,
                        ins=[otb], outs=[x1_dram])
    return part



def make_cast_bg(ctx, B, pairs, shared="cg"):
    for sl in range(2):
        B.sbuf(f"{shared}f{sl}", [128, 1024], F32)
        B.sbuf(f"{shared}b{sl}", [128, 1024], BF16)
    units = []
    for src, dst in pairs:
        n, prt, mcols = src.ap.shape
        for i in range(n):
            for c0 in range(0, mcols, 1024):
                units.append((src, dst, i, c0, min(1024, mcols - c0), prt))
    state = {"k": 0}

    def bg():
        k = state["k"]
        if k >= len(units):
            return False
        state["k"] = k + 1
        src, dst, i, c0, w, prt = units[k]
        sl = k % 2
        f, fb = B.sb[f"{shared}f{sl}"], B.buf[f"{shared}f{sl}"]
        b, bb = B.sb[f"{shared}b{sl}"], B.buf[f"{shared}b{sl}"]
        ctx.dma("sync", f[0:prt, 0:w], src.ap[i, :, c0:c0 + w], fb,
                ins=[src], outs=[fb])
        ctx.op("vector", lambda e, f=f, b=b, prt=prt, w=w:
               e.tensor_copy(b[0:prt, 0:w], f[0:prt, 0:w]),
               ins=[fb], outs=[bb])
        ctx.dma("sync", dst.ap[i, :, c0:c0 + w], b[0:prt, 0:w], bb,
                ins=[bb], outs=[dst])
        return True
    return bg


def emit_cast_bf16(ctx, B, src_dram, dst_dram, name, shared="cast"):
    n, prt, mcols = src_dram.ap.shape
    if f"{shared}f0" not in B.sb:
        for sl in range(2):
            B.sbuf(f"{shared}f{sl}", [128, 4096], F32)
            B.sbuf(f"{shared}b{sl}", [128, 4096], BF16)
    for i in range(n):
        sl = i % 2
        f, fb = B.sb[f"{shared}f{sl}"], B.buf[f"{shared}f{sl}"]
        b, bb = B.sb[f"{shared}b{sl}"], B.buf[f"{shared}b{sl}"]
        ctx.dma("sync", f[0:prt, 0:mcols], src_dram.ap[i], fb,
                ins=[src_dram], outs=[fb])
        ctx.op("vector", lambda e, f=f, b=b, prt=prt, mcols=mcols:
               e.tensor_copy(b[0:prt, 0:mcols], f[0:prt, 0:mcols]),
               ins=[fb], outs=[bb])
        ctx.dma("sync", dst_dram.ap[i], b[0:prt, 0:mcols], bb,
                ins=[bb], outs=[dst_dram])


HT = 1024


def emit_attn_pre(ctx, B, W, xn1_dram):
    sb, bf = B.sb, B.buf
    qhat, khat, vtm = sb["qhat"], sb["khat"], sb["vtm"]
    halo = B.sbuf("a8halo", [128, 12, 2], BF16)
    halo2 = B.sbuf("a8halo2", [128, 10, 2], BF16)
    vsh = B.sbuf("a8vsh", [128, HT], BF16)
    for nm in ("p8a0", "p8a1", "p8b", "p8c", "p8t0", "p8t1"):
        B.bufonly(nm)
    swapm = B.sbuf("swapm_s", [128, 128], F32)
    ctx.dma("sync", swapm[:], W["swapm"].ap[:, :], bf["swapm_s"],
            ins=[W["swapm"]], outs=[bf["swapm_s"]])
    qkv = B.sbuf("a8qkv", [128, 2, HT], BF16)
    qp1 = B.sbuf("a8qp1", [128, 8, HT + 2], BF16)
    kp1 = B.sbuf("a8kp1", [128, 2, HT + 2], BF16)
    qp2 = B.sbuf("a8qp2", [128, 8, HT + 2], BF16)
    kp2 = B.sbuf("a8kp2", [128, 2, HT + 2], BF16)
    kmean = B.sbuf("a8km", [128, HT], BF16)
    qmean = B.sbuf("a8qm", [128, HT], BF16)
    qf = B.sbuf("a8qf", [128, 8, HT], BF16)
    kf = B.sbuf("a8kf", [128, 2, HT], BF16)
    rsq = B.sbuf("a8rsq", [128, HT], F32)
    rsk = B.sbuf("a8rsk", [128, HT], F32)
    rbc = B.sbuf("a8rbc", [128, HT], F32)
    tA = B.sbuf("a8tA", [128, HT], F32)
    tB = B.sbuf("a8tB", [128, HT], F32)
    tC = B.sbuf("a8tC", [128, HT], F32)
    cosf = B.sbuf("a8cos", [128, HT], F32)
    sinf = B.sbuf("a8sin", [128, HT], F32)
    ktl = B.sbuf("a8kt", [128, 1], F32)
    qc1b = B.sbuf("a8q1b", [128, 8], F32)
    qc2b = B.sbuf("a8q2b", [128, 8], F32)
    kc1b = B.sbuf("a8k1b", [128, 2], F32)
    kc2b = B.sbuf("a8k2b", [128, 2], F32)
    mkp = B.sbuf("a8mk", [128, 2, 64], BF16)
    mqp = B.sbuf("a8mq", [128, 8, 64], BF16)
    bqp = B.sbuf("a8bq", [128, 8, 128], BF16)
    bkp = B.sbuf("a8bk", [128, 2, 128], BF16)
    sqp = B.sbuf("a8sq", [128, 8, 16], F32)
    skp = B.sbuf("a8sk", [128, 2, 4], F32)
    rsbq = B.sbuf("a8rq", [128, 8, 128], F32)
    rsbk = B.sbuf("a8rk", [128, 2, 128], F32)
    wt = [B.sbuf(f"a8w{sl}", [128, 128], BF16) for sl in range(4)]
    wq2 = [B.sbuf(f"a8wq{sl}", [128, 32, 128], BF16) for sl in range(2)]
    wtf = [B.sbuf(f"a8wf{sl}", [128, 128], F32) for sl in range(2)]
    xt_ = [B.sbuf(f"a8x{sl}", [128, HT], BF16) for sl in range(4)]
    for nm, src in [("a8kt", "ktemp"), ("a8q1b", "qc1b"), ("a8q2b", "qc2b"),
                    ("a8k1b", "kc1b"), ("a8k2b", "kc2b")]:
        ctx.dma("sync", sb[nm][:], W[src].ap[:, :], bf[nm],
                ins=[W[src]], outs=[bf[nm]])
    for nm, src in [("a8mk", "mk"), ("a8mq", "mq"), ("a8bq", "bq"),
                    ("a8bk", "bk"), ("a8sq", "sq"), ("a8sk", "sk"),
                    ("a8rq", "rsbq"), ("a8rk", "rsbk")]:
        np_ = W[src].ap.shape[1]
        ctx.dma("sync", sb[nm][0:np_, :, :],
                W[src].ap.rearrange("a p b -> p a b"),
                bf[nm], ins=[W[src]], outs=[bf[nm]])

    def PS(idx, width=HT):
        return B.ps_ap[:, 1024 * idx:1024 * idx + width]

    for hf in range(2):
        t0 = hf * HT
        ctx.dma("sync", cosf[:, :], W["cosf"].ap[:, t0:t0 + HT],
                bf["a8cos"], ins=[W["cosf"]], outs=[bf["a8cos"]])
        ctx.dma("sync", sinf[:, :], W["sinf"].ap[:, t0:t0 + HT],
                bf["a8sin"], ins=[W["sinf"]], outs=[bf["a8sin"]])
        if hf == 1:
            ctx.op("vector", lambda e: e.tensor_copy(
                halo[:, 0:8, :], qp1[:, :, HT:HT + 2]), ins=[bf["a8qp1"]],
                outs=[bf["a8halo"]])
            ctx.op("vector", lambda e: e.tensor_copy(
                halo[:, 8:10, :], kp1[:, :, HT:HT + 2]), ins=[bf["a8kp1"]],
                outs=[bf["a8halo"]])
            ctx.op("vector", lambda e: e.tensor_copy(
                halo[:, 10:12, :], qkv[:, 0:2, HT - 2:HT]),
                ins=[bf["a8qkv"]], outs=[bf["a8halo"]])
        for nm in ("a8qp1", "a8kp1", "a8qp2", "a8kp2"):
            if hf == 0:
                ctx.op("vector", lambda e, nm=nm: e.memset(
                    sb[nm][:, :, 0:2], 0.0), ins=[], outs=[bf[nm]])
        if hf == 1:
            ctx.op("vector", lambda e: e.tensor_copy(
                halo2[:, 0:8, :], qp2[:, :, HT:HT + 2]),
                ins=[bf["a8qp2"]], outs=[bf["a8halo2"]])
            ctx.op("vector", lambda e: e.tensor_copy(
                halo2[:, 8:10, :], kp2[:, :, HT:HT + 2]),
                ins=[bf["a8kp2"]], outs=[bf["a8halo2"]])
            ctx.op("vector", lambda e: e.tensor_copy(
                qp1[:, :, 0:2], halo[:, 0:8, :]),
                ins=[bf["a8halo"]], outs=[bf["a8qp1"]])
            ctx.op("vector", lambda e: e.tensor_copy(
                kp1[:, :, 0:2], halo[:, 8:10, :]),
                ins=[bf["a8halo"]], outs=[bf["a8kp1"]])
            ctx.op("vector", lambda e: e.tensor_copy(
                qp2[:, :, 0:2], halo2[:, 0:8, :]),
                ins=[bf["a8halo2"]], outs=[bf["a8qp2"]])
            ctx.op("vector", lambda e: e.tensor_copy(
                kp2[:, :, 0:2], halo2[:, 8:10, :]),
                ins=[bf["a8halo2"]], outs=[bf["a8kp2"]])
        for oc in range(12):
            sl = oc % 2
            ps, psb = PS(sl), bf[f"p8a{sl}"]
            wq, wqb = sb[f"a8wq{sl}"], bf[f"a8wq{sl}"]
            ctx.dma(["sync", "scalar"][oc % 2], wq[:],
                    W["wqkvT_b"].ap[:, :, oc * 128:(oc + 1) * 128].rearrange(
                        "k p c -> p k c"), wqb,
                    ins=[W["wqkvT_b"]], outs=[wqb])
            for kk in range(32):
                xsl = (oc * 32 + kk) % 4
                x_, xb = xt_[xsl], bf[f"a8x{xsl}"]
                ctx.dma(["sync", "scalar", "gpsimd"][kk % 3], x_[:],
                        xn1_dram.ap[kk, :, t0:t0 + HT], xb,
                        ins=[xn1_dram], outs=[xb])
                mm_split(ctx, ps, wq[:, kk, :], x_[:], ins=[wqb, xb],
                         outs=[psb], start=(kk == 0), stop=(kk == 31))
            if oc < 8:
                dst, db = qp1[:, oc, 2:2 + HT], bf["a8qp1"]
            elif oc < 10:
                dst, db = kp1[:, oc - 8, 2:2 + HT], bf["a8kp1"]
            else:
                dst, db = qkv[:, oc - 10, :], bf["a8qkv"]
            ctx.op("vector", lambda e, dst=dst, ps=ps: e.tensor_copy(dst, ps),
                   ins=[psb], outs=[db])
        for c2 in range(2):
            mm_split(ctx, PS(2)[0:64, :], mkp[:, c2, :], kp1[:, c2, 2:2 + HT],
                     ins=[bf["a8mk"], bf["a8kp1"]], outs=[bf["p8b"]],
                     start=(c2 == 0), stop=(c2 == 1))
        ctx.op("vector", lambda e: e.tensor_copy(kmean[0:64, :], PS(2)[0:64, :]),
               ins=[bf["p8b"]], outs=[bf["a8km"]])
        for c8 in range(8):
            mm_split(ctx, PS(2)[0:64, :], mqp[:, c8, :], qp1[:, c8, 2:2 + HT],
                     ins=[bf["a8mq"], bf["a8qp1"]], outs=[bf["p8b"]],
                     start=(c8 == 0), stop=(c8 == 7))
        ctx.op("vector", lambda e: e.tensor_copy(qmean[0:64, :], PS(2)[0:64, :]),
               ins=[bf["p8b"]], outs=[bf["a8qm"]])
        for oc in range(8):
            sl = oc % 2
            ps, psb = PS(sl), bf[f"p8a{sl}"]
            for tap in range(3):
                wsl = (oc * 3 + tap) % 4
                w, wb = wt[wsl], bf[f"a8w{wsl}"]
                ctx.dma("sync", w[:], W["qc1bd_b"].ap[oc * 3 + tap], wb,
                        ins=[W["qc1bd_b"]], outs=[wb])
                mm_split(ctx, ps, w[:], qp1[:, oc, tap:tap + HT],
                         ins=[wb, bf["a8qp1"]], outs=[psb],
                         start=(tap == 0), stop=(tap == 2))
            ctx.op("scalar", lambda e, oc=oc, ps=ps: e.activation(
                qp2[:, oc, 2:2 + HT], ps, AF.Identity, bias=qc1b[:, oc:oc + 1]),
                ins=[psb, bf["a8q1b"]], outs=[bf["a8qp2"]])
        for oc in range(2):
            sl = oc % 2
            ps, psb = PS(sl), bf[f"p8a{sl}"]
            for tap in range(3):
                wsl = (oc * 3 + tap) % 4
                w, wb = wt[wsl], bf[f"a8w{wsl}"]
                ctx.dma("sync", w[:], W["kc1bd_b"].ap[oc * 3 + tap], wb,
                        ins=[W["kc1bd_b"]], outs=[wb])
                mm_split(ctx, ps, w[:], kp1[:, oc, tap:tap + HT],
                         ins=[wb, bf["a8kp1"]], outs=[psb],
                         start=(tap == 0), stop=(tap == 2))
            ctx.op("scalar", lambda e, oc=oc, ps=ps: e.activation(
                kp2[:, oc, 2:2 + HT], ps, AF.Identity, bias=kc1b[:, oc:oc + 1]),
                ins=[psb, bf["a8k1b"]], outs=[bf["a8kp2"]])
        for oc in range(8):
            sl = oc % 2
            ps, psb = PS(sl), bf[f"p8a{sl}"]
            wq, wqb = sb[f"a8wq{sl}"], bf[f"a8wq{sl}"]
            ctx.dma(["sync", "scalar"][oc % 2], wq[:, 0:24, :],
                    W["qc2T_b"].ap[:, :, oc * 128:(oc + 1) * 128].rearrange(
                        "k p c -> p k c"), wqb,
                    ins=[W["qc2T_b"]], outs=[wqb])
            first = True
            for tap in range(3):
                for ic in range(8):
                    mm_split(ctx, ps, wq[:, tap * 8 + ic, :],
                             qp2[:, ic, tap:tap + HT],
                             ins=[wqb, bf["a8qp2"]], outs=[psb],
                             start=first, stop=False)
                    first = False
            mm_split(ctx, ps, bqp[0:64, oc, :], kmean[0:64, :],
                     ins=[bf["a8bq"], bf["a8km"]], outs=[psb],
                     start=False, stop=True)
            ctx.op("scalar", lambda e, oc=oc, ps=ps: e.activation(
                qf[:, oc, :], ps, AF.Identity, bias=qc2b[:, oc:oc + 1]),
                ins=[psb, bf["a8q2b"]], outs=[bf["a8qf"]])
        for oc in range(2):
            sl = oc % 2
            ps, psb = PS(sl), bf[f"p8a{sl}"]
            first = True
            for tap in range(3):
                for ic in range(2):
                    wsl = (tap * 2 + ic) % 4
                    w, wb = wt[wsl], bf[f"a8w{wsl}"]
                    ctx.dma("sync", w[:],
                            W["kc2T"].ap[tap * 2 + ic, :,
                                         oc * 128:(oc + 1) * 128], wb,
                            ins=[W["kc2T"]], outs=[wb])
                    mm_split(ctx, ps, w[:], kp2[:, ic, tap:tap + HT],
                             ins=[wb, bf["a8kp2"]], outs=[psb],
                             start=first, stop=False)
                    first = False
            mm_split(ctx, ps, bkp[0:64, oc, :], qmean[0:64, :],
                     ins=[bf["a8bk"], bf["a8qm"]], outs=[psb],
                     start=False, stop=True)
            ctx.op("scalar", lambda e, oc=oc, ps=ps: e.activation(
                kf[:, oc, :], ps, AF.Identity, bias=kc2b[:, oc:oc + 1]),
                ins=[psb, bf["a8k2b"]], outs=[bf["a8kf"]])
        for (nch, src, srcb, pat, patb, dstn, nr) in [
                (8, qf, bf["a8qf"], sqp, bf["a8sq"], "a8rsq", 16),
                (2, kf, bf["a8kf"], skp, bf["a8sk"], "a8rsk", 4)]:
            for oc in range(nch):
                ctx.op("scalar", lambda e, src=src, oc=oc: e.activation(
                    tA[:], src[:, oc, :], AF.Square), ins=[srcb],
                    outs=[bf["a8tA"]])
                mm_split(ctx, PS(2)[0:nr, :], pat[:, oc, 0:nr], tA[:],
                         ins=[bf["a8tA"], patb], outs=[bf["p8b"]],
                         start=(oc == 0), stop=(oc == nch - 1))
            dst, dstb = sb[dstn], bf[dstn]
            ctx.op("scalar", lambda e, dst=dst, nr=nr: e.activation(
                dst[0:nr, :], PS(2)[0:nr, :], AF.Sqrt),
                ins=[bf["p8b"]], outs=[dstb])
            ctx.op("vector", lambda e, dst=dst, nr=nr: e.tensor_scalar(
                dst[0:nr, :], dst[0:nr, :], 1e-12, None, ALU.max),
                ins=[dstb], outs=[dstb])
            ctx.op("vector", lambda e, dst=dst, nr=nr: e.reciprocal(
                dst[0:nr, :], dst[0:nr, :]), ins=[dstb], outs=[dstb])
        ctx.op("vector", lambda e: e.tensor_scalar(
            rsk[0:4, :], rsk[0:4, :], ktl[0:4, :], None, ALU.mult),
            ins=[bf["a8rsk"], bf["a8kt"]], outs=[bf["a8rsk"]])
        for (nch, srcf, srcb, rpat, rpatb, rsn, nr, dsth, dsthb) in [
                (8, qf, bf["a8qf"], rsbq, bf["a8rq"], "a8rsq", 16, qhat,
                 bf["qhat"]),
                (2, kf, bf["a8kf"], rsbk, bf["a8rk"], "a8rsk", 4, khat,
                 bf["khat"])]:
            rs, rsb2 = sb[rsn], bf[rsn]
            for oc in range(nch):
                mm_split(ctx, PS(2)[:, :], rpat[0:nr, oc, :], rs[0:nr, :],
                         ins=[rpatb, rsb2], outs=[bf["p8b"]],
                         start=True, stop=True)
                ctx.op("vector", lambda e: e.tensor_copy(rbc[:], PS(2)[:, :]),
                       ins=[bf["p8b"]], outs=[bf["a8rbc"]])
                ctx.op("vector", lambda e, srcf=srcf, oc=oc: e.tensor_tensor(
                    tC[:], srcf[:, oc, :], rbc[:], ALU.mult),
                    ins=[srcb, bf["a8rbc"]], outs=[bf["a8tC"]])
                mm_split(ctx, PS(3)[:, :], sb["swapm_s"][:], tC[:],
                         ins=[bf["a8tC"], bf["swapm_s"]], outs=[bf["p8c"]],
                         start=True, stop=True)
                ctx.op("vector", lambda e: e.tensor_tensor(
                    tA[:], tC[:], cosf[:], ALU.mult),
                    ins=[bf["a8tC"], bf["a8cos"]], outs=[bf["a8tA"]])
                ctx.op("vector", lambda e: e.tensor_tensor(
                    tB[:], PS(3)[:, :], sinf[:], ALU.mult),
                    ins=[bf["p8c"], bf["a8sin"]], outs=[bf["a8tB"]])
                ctx.op("vector", lambda e, dsth=dsth, oc=oc, t0=t0:
                       e.tensor_tensor(dsth[:, oc, t0:t0 + HT], tA[:], tB[:],
                                       ALU.add),
                       ins=[bf["a8tA"], bf["a8tB"]], outs=[dsthb])
                if "dbg_qh3" in W and nch == 8 and hf == 0 and oc == 0:
                    ctx.dma("sync", W["dbg_qh3"].ap[:, 0:HT],
                            qhat[:, 0, 0:HT],
                            dsthb, ins=[dsthb], outs=[W["dbg_qh3"]])
                if "dbg_tc" in W and nch == 8 and oc == 0:
                    ctx.dma("sync", W["dbg_tc"].ap[hf, 0], tC[:],
                            bf["a8tC"], ins=[bf["a8tC"]], outs=[W["dbg_tc"]])
                    ctx.dma("sync", W["dbg_tc"].ap[hf, 1], tA[:],
                            bf["a8tA"], ins=[bf["a8tA"]], outs=[W["dbg_tc"]])
                    ctx.dma("sync", W["dbg_tc"].ap[hf, 2], tB[:],
                            bf["a8tB"], ins=[bf["a8tB"]], outs=[W["dbg_tc"]])
                    ctx.dma("sync", W["dbg_tc"].ap[hf, 3], rbc[:],
                            bf["a8rbc"], ins=[bf["a8rbc"]], outs=[W["dbg_tc"]])
        if "dbg_qh2" in W and hf == 1:
            ctx.dma("sync", W["dbg_qh2"].ap[0], qhat[:, :, :],
                    bf["qhat"], ins=[bf["qhat"]], outs=[W["dbg_qh2"]])
        if "dbg_qp1" in W:
            ctx.dma("sync", W["dbg_qp1"].ap[hf], qp1[:, :, :],
                    bf["a8qp1"], ins=[bf["a8qp1"]], outs=[W["dbg_qp1"]])
            ctx.dma("sync", W["dbg_qp2"].ap[hf], qp2[:, :, :],
                    bf["a8qp2"], ins=[bf["a8qp2"]], outs=[W["dbg_qp2"]])
            ctx.dma("sync", W["dbg_qf"].ap[hf], qf[:, :, :],
                    bf["a8qf"], ins=[bf["a8qf"]], outs=[W["dbg_qf"]])
            ctx.dma("sync", W["dbg_rsq"].ap[hf], rsq[:, :],
                    bf["a8rsq"], ins=[bf["a8rsq"]], outs=[W["dbg_rsq"]])
            ctx.dma("sync", W["dbg_km"].ap[hf], kmean[:, :],
                    bf["a8km"], ins=[bf["a8km"]], outs=[W["dbg_km"]])
        for kvh in range(4):
            base = 64 * (kvh % 2)
            kc = kvh // 2
            mm_split(ctx, PS(2)[:, :],
                     sb["dup64"][base:base + 64, :],
                     khat[base:base + 64, kc, t0:t0 + HT],
                     ins=[bf["khat"], bf["dup64"]], outs=[bf["p8b"]],
                     start=True, stop=True)
            ctx.op("vector", lambda e, kvh=kvh, t0=t0: e.tensor_copy(
                sb["khat4"][:, kvh, t0:t0 + HT], PS(2)[:, :]),
                ins=[bf["p8b"]], outs=[bf["khat4"]])
        if hf == 0:
            ctx.op("vector", lambda e: e.memset(vsh[:, 0:1], 0.0),
                   ins=[], outs=[bf["a8vsh"]])
        else:
            ctx.op("vector", lambda e: e.tensor_copy(
                vsh[:, 0:1], halo[:, 11, 1:2]), ins=[bf["a8halo"]],
                outs=[bf["a8vsh"]])
        ctx.op("vector", lambda e: e.tensor_copy(
            vsh[:, 1:HT], qkv[:, 1, 0:HT - 1]), ins=[bf["a8qkv"]],
            outs=[bf["a8vsh"]])
        for kv in range(4):
            base = 64 * (kv % 2)
            if kv < 2:
                def srcap(tt, kv=kv):
                    return qkv[64 * kv:64 * kv + 64, 0,
                               tt * 128:(tt + 1) * 128]
                sb_in = bf["a8qkv"]
            else:
                def srcap(tt, kv=kv):
                    return vsh[64 * (kv - 2):64 * (kv - 2) + 64,
                               tt * 128:(tt + 1) * 128]
                sb_in = bf["a8vsh"]
            for tt in range(HT // 128):
                psl = tt % 2
                pst = B.ps_ap[:, 3072 + 512 * psl:3072 + 512 * psl + 32]\
                    .bitcast(BF16)[:, 0:64]
                pstb = bf[f"p8t{psl}"]
                ctx.op("tensor", lambda e, srcap=srcap, tt=tt, pst=pst,
                       base=base:
                       e.transpose(pst, srcap(tt),
                                   sb["dup64"][base:base + 64, 0:64]),
                       ins=[sb_in, bf["dup64"], bf["p8c"]], outs=[pstb])
                gtt = t0 // 128 + tt
                ctx.op("vector", lambda e, kv=kv, gtt=gtt, pst=pst:
                       e.tensor_copy(vtm[:, kv, gtt, :], pst),
                       ins=[pstb], outs=[bf["vtm"]])


def emit_attn(ctx, B, W, s1_dram):
    sb, bf = B.sb, B.buf
    qhat, khat4, vtm = sb["qhat"], sb["khat4"], sb["vtm"]
    afm = B.sbuf("a9afm", [128, 8, T], BF16)
    cm = B.sbuf("a9cm", [128, 128], F32)
    Pb = [B.sbuf(f"a9P{sl}", [128, T], BF16) for sl in range(2)]
    pt2 = [[B.sbuf(f"a9ptt{S}{psl}", [128, 128], BF16) for psl in range(2)]
           for S in range(2)]
    for S in range(2):
        for psl in range(2):
            B.bufonly(f"a9pt{S}{psl}")
    av2 = [B.sbuf(f"a9avv{S}", [128, 64], BF16) for S in range(2)]
    for S in range(2):
        B.bufonly(f"a9av{S}")
    mx = B.sbuf("a9mx", [128, 8], F32)
    for nm in ("p9A0", "p9A1", "p9t0", "p9t1"):
        B.bufonly(nm)
    ctx.dma("sync", cm[:], W["cmask"].ap[:, :], bf["a9cm"],
            ins=[W["cmask"]], outs=[bf["a9cm"]])

    def head(a, qt):
        kvh = a // 4
        qc, qh = a // 2, a % 2
        ext = (qt + 1) * 128
        S = qt % 2
        base = 2048 * S
        slb = bf[f"p9A{S}"]
        ps = B.ps_ap[:, base:base + ext]
        lhsT = qhat[64 * qh:64 * qh + 64, qc, qt * 128:(qt + 1) * 128]
        rhs = khat4[64 * qh:64 * qh + 64, kvh, 0:ext]
        mm_split(ctx, ps, lhsT, rhs, ins=[bf["qhat"], bf["khat4"]],
                 outs=[slb], start=True, stop=True)
        ctx.op("vector", lambda e, qt=qt, ext=ext, base=base:
               e.tensor_tensor(
                   B.ps_ap[:, base + qt * 128:base + ext],
                   B.ps_ap[:, base + qt * 128:base + ext], cm[:],
                   ALU.add), ins=[slb, bf["a9cm"]], outs=[slb])
        ctx.op("vector", lambda e, ext=ext, base=base, S=S: e.reduce_max(
            mx[:, 4 * S:4 * S + 1], B.ps_ap[:, base:base + ext], AX.X),
            ins=[slb], outs=[bf["a9mx"]])
        ctx.op("vector", lambda e, S=S: e.tensor_scalar(
            mx[:, 4 * S + 1:4 * S + 2], mx[:, 4 * S:4 * S + 1], -1.0,
            None, ALU.mult), ins=[bf["a9mx"]], outs=[bf["a9mx"]])
        P_, Pbb = Pb[S], bf[f"a9P{S}"]
        ctx.op("scalar", lambda e, P_=P_, ext=ext, base=base, S=S:
               e.activation(P_[:, 0:ext], B.ps_ap[:, base:base + ext],
                            AF.Exp, bias=mx[:, 4 * S + 1:4 * S + 2]),
               ins=[slb, bf["a9mx"]], outs=[Pbb])
        ctx.op("vector", lambda e, P_=P_, ext=ext, S=S: e.reduce_sum(
            mx[:, 4 * S + 2:4 * S + 3], P_[:, 0:ext], AX.X),
            ins=[Pbb], outs=[bf["a9mx"]])
        ctx.op("vector", lambda e, S=S: e.reciprocal(
            mx[:, 4 * S + 3:4 * S + 4], mx[:, 4 * S + 2:4 * S + 3]),
            ins=[bf["a9mx"]], outs=[bf["a9mx"]])

    def tail(a, qt):
        kvh = a // 4
        qc, qh = a // 2, a % 2
        S = qt % 2
        base = 2048 * S
        slb = bf[f"p9A{S}"]
        P_, Pbb = Pb[S], bf[f"a9P{S}"]
        for kt in range(qt + 1):
            psl = kt % 2
            pstp = B.ps_ap[:, base + 512 * psl:base + 512 * psl + 64]\
                .bitcast(BF16)[:, 0:128]
            ctx.op("tensor", lambda e, P_=P_, kt=kt, pstp=pstp:
                   e.transpose(pstp, P_[:, kt * 128:(kt + 1) * 128],
                               sb["ident"][:]),
                   ins=[Pbb, bf["ident"]], outs=[slb])
            ctx.op("vector", lambda e, psl=psl, pstp=pstp, S=S:
                   e.tensor_copy(pt2[S][psl][:], pstp),
                   ins=[slb], outs=[bf[f"a9pt{S}{psl}"]])
            ctx.op("tensor", lambda e, psl=psl, kvh=kvh, kt=kt, qt=qt,
                   base=base, S=S:
                   e.matmul(B.ps_ap[:, base + 1024:base + 1088],
                            pt2[S][psl][:], vtm[:, kvh, kt, :],
                            start=(kt == 0), stop=(kt == qt),
                            skip_group_check=True),
                   ins=[bf[f"a9pt{S}{psl}"], bf["vtm"]], outs=[slb])
        ctx.op("vector", lambda e, base=base, S=S: e.tensor_scalar(
            av2[S][:], B.ps_ap[:, base + 1024:base + 1088],
            mx[:, 4 * S + 3:4 * S + 4], None, ALU.mult),
            ins=[slb, bf["a9mx"]], outs=[bf[f"a9av{S}"]])
        pso = B.ps_ap[:, base + 1536:base + 1600]\
            .bitcast(BF16)[0:64, 0:128]
        ctx.op("tensor", lambda e, pso=pso, S=S: e.transpose(
            pso, av2[S][:], sb["ident"][:]),
            ins=[bf[f"a9av{S}"], bf["ident"]], outs=[slb])
        ctx.op("vector", lambda e, qh=qh, qc=qc, qt=qt, pso=pso:
               e.tensor_copy(
                   afm[64 * qh:64 * qh + 64, qc, qt * 128:(qt + 1) * 128],
                   pso), ins=[slb], outs=[bf["a9afm"]])

    iters = [(a, qt) for a in range(16) for qt in range(NT)]
    for i, (a, qt) in enumerate(iters):
        head(a, qt)
        if i > 0:
            tail(*iters[i - 1])
    tail(*iters[-1])
    wo = [B.sbuf(f"a9wo{sl}", [128, 8, 128], BF16) for sl in range(2)]
    ev = B.sbuf("a9ev", [128, T], F32)
    so = [B.sbuf(f"a9so{sl}", [128, 128], F32) for sl in range(8)]
    B.bufonly("p9wo")
    for nc_ in range(ND):
        ps = B.ps_ap[:, 0:2048]
        wsl = nc_ % 2
        w, wb = wo[wsl], bf[f"a9wo{wsl}"]
        ctx.dma(["sync", "scalar"][nc_ % 2], w[:],
                W["woT_b"].ap[:, :, nc_ * 128:(nc_ + 1) * 128].rearrange(
                    "l p c -> p l c"), wb,
                ins=[W["woT_b"]], outs=[wb])
        for lc in range(8):
            mm_split(ctx, ps, w[:, lc, :], afm[:, lc, :], ins=[wb, bf["a9afm"]],
                     outs=[bf["p9wo"]], start=(lc == 0), stop=(lc == 7))
        ctx.op("vector", lambda e, ps=ps: e.tensor_copy(ev[:], ps),
               ins=[bf["p9wo"]], outs=[bf["a9ev"]])
        for tc in range(16):
            psl = tc % 2
            sl8 = tc % 8
            pst = B.ps_ap[:, 2048 + 512 * psl:2048 + 512 * psl + 128]
            s_, s_b = so[sl8], bf[f"a9so{sl8}"]
            ctx.op("tensor", lambda e, tc=tc, pst=pst: e.transpose(
                pst, ev[:, tc * 128:(tc + 1) * 128], sb["identf"][:]),
                ins=[bf["a9ev"], bf["identf"]], outs=[bf[f"p9t{psl}"]])
            ctx.op("vector", lambda e, s_=s_, pst=pst: e.tensor_copy(
                s_[:], pst), ins=[bf[f"p9t{psl}"]], outs=[s_b])
            ctx.dma("sync",
                    s1_dram.ap[tc * 128:(tc + 1) * 128,
                               nc_ * 128:(nc_ + 1) * 128],
                    s_[:], s_b, ins=[s_b], outs=[s1_dram])


def emit_ffn1(ctx, B, W, s1_dram, x1_dram, out_dram):
    sb, bf = B.sb, B.buf
    xr = sb["xr1"]
    rsum, rsumb = sb["rsum"], bf["rsum"]
    rwt = B.sbuf("aArwt", [128, 32, 8], F32)
    rbs = B.sbuf("aArb", [128, 8], F32)
    e8 = B.sbuf("aAe8", [128, 128], F32)
    qr = B.sbuf("aAqr", [128, 32], F32)
    rt = B.sbuf("aArt", [128, 8], F32)
    rsc = B.sbuf("aArsc", [128, 1], F32)
    tl = B.sbuf("aAtl", [128, T], BF16)
    g1 = B.sbuf("aAg1", [128, T], F32)
    wt = [B.sbuf(f"aAw{sl}", [128, 128], BF16) for sl in range(4)]
    xt_ = [B.sbuf(f"aAx{sl}", [128, 128], F32) for sl in range(8)]
    st_ = [B.sbuf(f"aAs{sl}", [128, 128], F32) for sl in range(8)]
    gt_ = [B.sbuf(f"aAgt{sl}", [128, 128], F32) for sl in range(8)]
    ot_ = [B.sbuf(f"aAo{sl}", [128, 128], F32) for sl in range(8)]
    b1s = B.sbuf("aAb1", [128, 32], F32)
    for nm in ("pA0", "pA1", "pAt0", "pAt1", "pAr"):
        B.bufonly(nm)
    ctx.dma("sync", rwt[:], W["rwt"].ap.rearrange("a p b -> p a b"),
            bf["aArwt"], ins=[W["rwt"]], outs=[bf["aArwt"]])
    ctx.dma("sync", rbs[0:1, :], W["rb"].ap[:, :], bf["aArb"],
            ins=[W["rb"]], outs=[bf["aArb"]])
    ctx.dma("sync", e8[0:8, :], W["e8"].ap[:, :], bf["aAe8"],
            ins=[W["e8"]], outs=[bf["aAe8"]])
    ctx.dma("sync", b1s[:], W["b1s"].ap[:, :], bf["aAb1"],
            ins=[W["b1s"]], outs=[bf["aAb1"]])
    ctx.op("vector", lambda e: e.tensor_scalar(
        qr[:], rsum[:, 0:32], 1.0 / 1436.0, None, ALU.mult),
        ins=[rsumb], outs=[bf["aAqr"]])
    for kk in range(32):
        ctx.op("tensor", lambda e, kk=kk: e.matmul(
            B.ps_ap[0:1, 0:8], qr[:, kk:kk + 1], rwt[:, kk, :],
            start=(kk == 0), stop=(kk == 31)),
            ins=[bf["aAqr"], bf["aArwt"]], outs=[bf["pAr"]])
    ctx.op("vector", lambda e: e.tensor_tensor(
        rt[0:1, :], B.ps_ap[0:1, 0:8], rbs[0:1, :], ALU.add),
        ins=[bf["pAr"], bf["aArb"]], outs=[bf["aArt"]])
    ctx.op("vector", lambda e: e.reduce_max(
        rsc[0:1, 0:1], rt[0:1, :], AX.X), ins=[bf["aArt"]], outs=[bf["aArsc"]])
    ctx.op("vector", lambda e: e.tensor_scalar(
        rsc[0:1, 0:1], rsc[0:1, 0:1], -1.0, None, ALU.mult),
        ins=[bf["aArsc"]], outs=[bf["aArsc"]])
    ctx.op("scalar", lambda e: e.activation(
        rt[0:1, :], rt[0:1, :], AF.Exp, bias=rsc[0:1, 0:1]),
        ins=[bf["aArt"], bf["aArsc"]], outs=[bf["aArt"]])
    ctx.op("vector", lambda e: e.reduce_sum(
        rsc[0:1, 0:1], rt[0:1, :], AX.X), ins=[bf["aArt"]], outs=[bf["aArsc"]])
    ctx.op("vector", lambda e: e.reciprocal(
        rsc[0:1, 0:1], rsc[0:1, 0:1]), ins=[bf["aArsc"]], outs=[bf["aArsc"]])
    ctx.op("vector", lambda e: e.tensor_scalar(
        rt[0:1, :], rt[0:1, :], rsc[0:1, 0:1], None, ALU.mult),
        ins=[bf["aArt"], bf["aArsc"]], outs=[bf["aArt"]])
    ctx.op("tensor", lambda e: e.transpose(
        B.ps_ap[0:8, 512:513], rt[0:1, 0:8], sb["identf"][0:1, 0:1]),
        ins=[bf["aArt"], bf["identf"]], outs=[bf["pAt0"]])
    ctx.op("vector", lambda e: e.tensor_copy(
        rt[0:8, 0:1], B.ps_ap[0:8, 512:513]), ins=[bf["pAt0"]],
        outs=[bf["aArt"]])
    ctx.op("tensor", lambda e: e.matmul(
        B.ps_ap[:, 512:513], e8[0:8, :], rt[0:8, 0:1], start=True, stop=True),
        ins=[bf["aAe8"], bf["aArt"]], outs=[bf["pAt0"]])
    ctx.op("vector", lambda e: e.tensor_copy(
        rsc[:, 0:1], B.ps_ap[:, 512:513]), ins=[bf["pAt0"]],
        outs=[bf["aArsc"]])
    w32 = [B.sbuf(f"aAW{sl}", [128, 32, 128], BF16) for sl in range(2)]
    wB, wBb = w32[0], bf["aAW0"]
    ctx.dma("sync", wB[:], W["atp_b"].ap.rearrange("k p c -> p k c"),
            wBb, ins=[W["atp_b"]], outs=[wBb])
    for kk in range(32):
        mm_split(ctx, B.ps_ap[:, 0:2048], wB[:, kk, :], xr[:, kk, :],
                 ins=[wBb, bf["xr1"]], outs=[bf["pA0"]],
                 start=(kk == 0), stop=(kk == 31))
    ctx.op("vector", lambda e: e.tensor_scalar(
        tl[:], B.ps_ap[:, 0:2048], rsc[:, 0:1], None, ALU.mult),
        ins=[bf["pA0"], bf["aArsc"]], outs=[bf["aAtl"]])
    for oc in range(ND):
        ps = B.ps_ap[:, 0:2048]
        wsl = oc % 2
        w1, w1b = w32[wsl], bf[f"aAW{wsl}"]
        ctx.dma(["sync", "scalar"][oc % 2], w1[:],
                W["w1t_b"].ap[:, :, oc * 128:(oc + 1) * 128].rearrange(
                    "k p c -> p k c"),
                w1b, ins=[W["w1t_b"]], outs=[w1b])
        for kk in range(32):
            mm_split(ctx, ps, w1[:, kk, :], xr[:, kk, :], ins=[w1b, bf["xr1"]],
                     outs=[bf["pA0"]], start=(kk == 0), stop=False)
        w, wb = wt[(32) % 4], bf[f"aAw{0}"]
        ctx.dma("sync", w[:], W["btp_b"].ap[0, :, oc * 128:(oc + 1) * 128],
                wb, ins=[W["btp_b"]], outs=[wb])
        mm_split(ctx, ps, w[:], tl[:], ins=[wb, bf["aAtl"]],
                 outs=[bf["pA0"]], start=False, stop=True)
        ctx.op("scalar", lambda e, oc=oc, ps=ps: e.activation(
            g1[:], ps, AF.Silu, bias=b1s[:, oc:oc + 1]),
            ins=[bf["pA0"], bf["aAb1"]], outs=[bf["aAg1"]])
        for tc in range(16):
            psl = tc % 2
            sl8 = tc % 8
            pst = B.ps_ap[:, 2048 + 512 * psl:2048 + 512 * psl + 128]
            gt, gtb = gt_[sl8], bf[f"aAgt{sl8}"]
            xt, xtb = xt_[sl8], bf[f"aAx{sl8}"]
            st, stb = st_[sl8], bf[f"aAs{sl8}"]
            ot, otb = ot_[sl8], bf[f"aAo{sl8}"]
            rows = slice(tc * 128, tc * 128 + 128)
            cols = slice(oc * 128, oc * 128 + 128)
            ctx.op("tensor", lambda e, tc=tc, pst=pst: e.transpose(
                pst, g1[:, tc * 128:(tc + 1) * 128], sb["identf"][:]),
                ins=[bf["aAg1"], bf["identf"]], outs=[bf[f"pAt{psl}"]])
            ctx.op("vector", lambda e, gt=gt, pst=pst: e.tensor_copy(
                gt[:], pst), ins=[bf[f"pAt{psl}"]], outs=[gtb])
            ctx.dma("sync", xt[:], x1_dram.ap[rows, cols], xtb,
                    ins=[x1_dram], outs=[xtb])
            ctx.dma("sync", st[:], s1_dram.ap[rows, cols], stb,
                    ins=[s1_dram], outs=[stb])
            ctx.op("vector", lambda e, gt=gt, st=st: e.tensor_tensor(
                gt[:], gt[:], st[:], ALU.mult), ins=[gtb, stb], outs=[gtb])
            ctx.op("vector", lambda e, gt=gt, xt=xt, ot=ot: e.tensor_tensor(
                ot[:], gt[:], xt[:], ALU.add), ins=[gtb, xtb], outs=[otb])
            ctx.dma("sync", out_dram.ap[rows, cols], ot[:], otb,
                    ins=[otb], outs=[out_dram])


def _pd():
    pd = np.zeros(64, dtype=np.int64)
    for p in range(32):
        pd[p] = 2 * p
        pd[32 + p] = 2 * p + 1
    return pd


def host_l1(inp):
    pd = _pd()
    W = {}
    W["wqkvT"] = np.ascontiguousarray(
        inp["L1_attn_w_qkv"].T).reshape(32, 128, 1536).astype(np.float32)
    qc1 = np.asarray(inp["L1_attn_qc1_w"], dtype=np.float32)
    bd = np.zeros((24, 128, 128), np.float32)
    for oc in range(8):
        for tap in range(3):
            for o_rel in range(128):
                o = oc * 128 + o_rel
                g = o // 64
                for i in range(64):
                    r_in = g * 64 + i - oc * 128
                    bd[oc * 3 + tap, r_in, o_rel] = qc1[o, i, tap]
    W["qc1bd"] = bd
    W["qc1b"] = np.ascontiguousarray(
        np.asarray(inp["L1_attn_qc1_b"], np.float32).reshape(8, 128).T)
    qc2 = np.asarray(inp["L1_attn_qc2_w"], dtype=np.float32)
    q2 = np.zeros((24, 128, 1024), np.float32)
    operm = np.concatenate([(o // 64) * 64 + pd[o % 64]
                            for o in range(1024)], axis=None) \
        if False else np.array([(o // 64) * 64 + pd[o % 64]
                                for o in range(1024)])
    for tap in range(3):
        for ic in range(8):
            q2[tap * 8 + ic] = qc2[operm, ic * 128:(ic + 1) * 128, tap].T
    W["qc2T"] = q2
    W["qc2b"] = np.ascontiguousarray(
        np.asarray(inp["L1_attn_qc2_b"], np.float32)[operm].reshape(8, 128).T)
    kc1 = np.asarray(inp["L1_attn_kc1_w"], dtype=np.float32)
    bdk = np.zeros((6, 128, 128), np.float32)
    for oc in range(2):
        for tap in range(3):
            for o_rel in range(128):
                o = oc * 128 + o_rel
                g = o // 64
                for i in range(64):
                    r_in = g * 64 + i - oc * 128
                    bdk[oc * 3 + tap, r_in, o_rel] = kc1[o, i, tap]
    W["kc1bd"] = bdk
    W["kc1b"] = np.ascontiguousarray(
        np.asarray(inp["L1_attn_kc1_b"], np.float32).reshape(2, 128).T)
    kc2 = np.asarray(inp["L1_attn_kc2_w"], dtype=np.float32)
    kperm = np.array([(o // 64) * 64 + pd[o % 64] for o in range(256)])
    k2 = np.zeros((6, 128, 256), np.float32)
    for tap in range(3):
        for ic in range(2):
            k2[tap * 2 + ic] = kc2[kperm, ic * 128:(ic + 1) * 128, tap].T
    W["kc2T"] = k2
    W["kc2b"] = np.ascontiguousarray(
        np.asarray(inp["L1_attn_kc2_b"], np.float32)[kperm].reshape(2, 128).T)
    mk_ = np.zeros((2, 128, 64), np.float32)
    for c2 in range(2):
        for r in range(128):
            mk_[c2, r, (c2 * 128 + r) % 64] = 0.25
    W["mk"] = mk_
    mq = np.zeros((8, 128, 64), np.float32)
    for c8 in range(8):
        for r in range(128):
            mq[c8, r, r % 64] = 1.0 / 16.0
    W["mq"] = mq
    bq = np.zeros((8, 64, 128), np.float32)
    for oc in range(8):
        for o_rel in range(128):
            bq[oc, pd[(oc * 128 + o_rel) % 64], o_rel] = 0.5
    W["bq"] = bq
    bk = np.zeros((2, 64, 128), np.float32)
    for oc in range(2):
        for o_rel in range(128):
            bk[oc, pd[(oc * 128 + o_rel) % 64], o_rel] = 0.5
    W["bk"] = bk
    sq = np.zeros((8, 128, 16), np.float32)
    for oc in range(8):
        for r in range(128):
            sq[oc, r, (oc * 128 + r) // 64] = 1.0
    W["sq"] = sq
    sk = np.zeros((2, 128, 4), np.float32)
    for oc in range(2):
        for r in range(128):
            sk[oc, r, (oc * 128 + r) // 64] = 1.0
    W["sk"] = sk
    rq = np.zeros((8, 16, 128), np.float32)
    for oc in range(8):
        for o_rel in range(128):
            rq[oc, (oc * 128 + o_rel) // 64, o_rel] = 0.125
    W["rsbq"] = rq
    rk = np.zeros((2, 4, 128), np.float32)
    for oc in range(2):
        for o_rel in range(128):
            rk[oc, (oc * 128 + o_rel) // 64, o_rel] = 1.0
    W["rsbk"] = rk
    inv = 1.0 / 10000.0 ** (np.arange(0, 64, 2, dtype=np.float64) / 64.0)
    ang = np.arange(T, dtype=np.float64)[:, None] * inv[None, :]
    W["cosf"] = np.ascontiguousarray(
        np.tile(np.cos(ang).T.astype(np.float32), (4, 1)))
    sins = np.tile(np.sin(ang).T.astype(np.float32), (4, 1))
    sins[0:32] *= -1.0
    sins[64:96] *= -1.0
    W["sinf"] = np.ascontiguousarray(sins)
    swapm = np.zeros((128, 128), np.float32)
    for p in range(128):
        blk, off = p // 64, p % 64
        sw = blk * 64 + (off + 32) % 64
        swapm[sw, p] = 1.0
    W["swapm"] = swapm
    W["ktemp"] = np.broadcast_to(
        np.asarray(inp["L1_attn_key_temp"], np.float32).reshape(1, 1),
        (128, 1)).copy()
    cmask = np.zeros((128, 128), np.float32)
    for qq in range(128):
        cmask[qq, qq + 1:] = -1e30
    W["cmask"] = cmask
    W["woT"] = np.ascontiguousarray(
        np.asarray(inp["L1_attn_w_o"], np.float32).T).reshape(8, 128, 4096)
    W["W1T"] = np.ascontiguousarray(
        np.asarray(inp["L1_ffn_W"], np.float32).T).reshape(32, 128, 4096)
    A = np.asarray(inp["L1_ffn_A"], np.float32).reshape(128, 4096)
    W["ATp"] = np.ascontiguousarray(A.T).reshape(32, 128, 128)
    Bm = np.asarray(inp["L1_ffn_B"], np.float32)
    BT = np.ascontiguousarray(Bm.transpose(0, 2, 1).reshape(128, 4096))
    W["BTf"] = BT.reshape(1, 128, 4096)
    W["rwt"] = np.ascontiguousarray(
        np.asarray(inp["L1_ffn_router_W"], np.float32).T).reshape(32, 128, 8)
    W["rb"] = np.asarray(inp["L1_ffn_router_b"], np.float32).reshape(1, 8)
    e8 = np.zeros((8, 128), np.float32)
    for p in range(128):
        e8[p // 16, p] = 2.0
    W["e8"] = e8
    W["b1s"] = np.ascontiguousarray(
        np.asarray(inp["L1_ffn_b"], np.float32).reshape(32, 128).T)
    return W


def emit_allgather(ctx, B, shard_dram, ag_dram, rows, cols, key_buf):
    stream = ctx.dstream_for(key_buf)
    ctx.gather_streams.add(stream)
    waits = ctx._edges(stream, [shard_dram], [ag_dram])
    stream.n += 16

    def f(e, waits=waits, stream=stream):
        for st, v in waits:
            e.wait_ge(st.sem, v)
        e.collective_compute(
            "AllGather", ALU.bypass,
            replica_groups=[list(range(8))],
            ins=[shard_dram.ap.rearrange("a b c d e f -> (a b c d e) f")
                 if len(shard_dram.ap.shape) == 6 else
                 shard_dram.ap.rearrange("a b c d e -> (a b c d) e")],
            outs=[ag_dram.ap]).then_inc(stream.sem, 16)
    ctx.emit("gpsimd", f)
    ctx._post(stream, [shard_dram], [ag_dram])
    return stream.n



"""Driver section for kernel.py: per-launch builders + kernel()."""
import numpy as np
import ml_dtypes
from contextlib import ExitStack
import concourse.bass as bass
import concourse.mybir as mybir
from concourse.bass_utils import run_bass_kernel_spmd

WNAMES_F32 = {
    "wqkvT": (32, 128, 1536), "qc1bd": (24, 128, 128), "qc1b": (128, 8),
    "qc2T": (24, 128, 1024), "qc2b": (128, 8), "kc1bd": (6, 128, 128),
    "kc1b": (128, 2), "kc2T": (6, 128, 256), "kc2b": (128, 2),
    "mk": (2, 128, 64), "mq": (8, 128, 64), "bq": (8, 64, 128),
    "bk": (2, 64, 128), "sq": (8, 128, 16), "sk": (2, 128, 4),
    "rsbq": (8, 16, 128), "rsbk": (2, 4, 128), "cosf": (128, T),
    "sinf": (128, T), "ktemp": (128, 1), "cmask": (128, 128),
    "woT": (8, 128, 4096), "W1T": (32, 128, 4096), "ATp": (32, 128, 128),
    "BTf": (1, 128, 4096), "rwt": (32, 128, 8), "rb": (1, 8),
    "e8": (8, 128), "b1s": (128, 32), "swapm": (128, 128),
}
CASTS = [("wqkvT", "wqkvT_b"), ("qc2T", "qc2T_b"), ("qc1bd", "qc1bd_b"),
         ("kc1bd", "kc1bd_b"), ("woT", "woT_b"), ("W1T", "w1t_b"),
         ("ATp", "atp_b"), ("BTf", "btp_b"), ("bq", "bq_b"), ("mk", "mk_b"),
         ("mq", "mq_b"), ("kc2T", "kc2T_b"), ("bk", "bk_b")]


def host_consts():
    p4 = np.zeros((32, 128, 128), dtype=np.float32)
    for Tt in range(32):
        for k in range(128):
            p4[Tt, k, 4 * Tt + k // 32] = 1.0
    sel4 = np.zeros((4, 128, 128), dtype=np.float32)
    for b in range(4):
        for r in range(128):
            sel4[b, 32 * b + r // 4, r] = 1.0
    dup = np.zeros((128, 128), np.float32)
    for r in range(128):
        dup[r, r % 64] = 1.0
        dup[r, 64 + r % 64] = 1.0
    dupT = np.zeros((128, 128), np.float32)
    for r in range(128):
        for c in range(128):
            if c % 64 == r % 64:
                dupT[r, c] = 1.0
    return (p4, sel4.astype(BF), np.eye(128, dtype=np.float32).astype(BF),
            np.eye(128, dtype=np.float32), dupT.astype(BF))


def build_A():
    nc = bass.Bass("TRN2", target_bir_lowering=False)
    es = ExitStack()
    with es:
        ctx = Ctx(es, nc)
        B = Builder(es, nc, ctx)
        x = B.dram("x", [T, D], F32, "ExternalInput")
        ident_in = B.dram("ident_in", [128, 128], BF16, "ExternalInput")
        identf_in = B.dram("identf_in", [128, 128], F32, "ExternalInput")
        maskT = B.dram("maskT", [32, 128, D], I32, "ExternalInput")
        wtp = B.dram("wtp", [32, 128, D], F32, "ExternalInput")
        p4d = B.dram("p4d", [32, 128, 128], F32, "ExternalInput")
        sel4d = B.dram("sel4d", [4, 128, 128], BF16, "ExternalInput")
        xnp = B.dram("xnp", [ND, 128, T], BF16, "ExternalOutput")
        w0t = B.dram("w0t", [ND, 128, D], BF16, "Internal")
        s0 = B.dram("s0", [T, D], F32, "ExternalOutput")
        B.psum_main(4096)
        B.sbuf("ident", [128, 128], BF16)
        B.sbuf("identf", [128, 128], F32)
        ctx.dma("sync", B.sb["ident"][:], ident_in.ap[:, :], B.buf["ident"],
                ins=[ident_in], outs=[B.buf["ident"]])
        ctx.dma("sync", B.sb["identf"][:], identf_in.ap[:, :], B.buf["identf"],
                ins=[identf_in], outs=[B.buf["identf"]])
        B.push_scope()
        B.ln_bufs("l0")
        emit_ln(ctx, B, x, xnp, "l0", permuted=True)
        B.pop_scope()
        ctx.fence()
        B.push_scope()
        emit_w0prep(ctx, B, maskT, wtp, p4d, sel4d, w0t)
        B.pop_scope()
        ctx.fence()
        emit_resident_load(ctx, B, xnp, "xr")
        B.push_scope()
        emit_s0(ctx, B, w0t, s0)
        B.pop_scope()
        ctx.final_waits()
        ctx.run_block()
    mybir.codegen_inst_isa_subclasses(nc)
    return nc


def build_B(core):
    nc = bass.Bass("TRN2", target_bir_lowering=False)
    es = ExitStack()
    with es:
        ctx = Ctx(es, nc)
        B = Builder(es, nc, ctx)
        cb = B.dram("cb", [65536, 8], F32, "ExternalInput")
        idxc = B.dram("idxc", [128, 20480], I32, "ExternalInput")
        sgu = B.dram("sgu", [2, 11, 128, 4, 8, 128], BF16, "ExternalOutput")
        sd = B.dram("sd", [4, 128, 8, 11, 128], BF16, "ExternalOutput")
        B.push_scope()
        emit_gather(ctx, B, cb, idxc, sgu, sd, core)
        B.pop_scope()
        ctx.final_waits()
        ctx.run_block()
    mybir.codegen_inst_isa_subclasses(nc)
    return nc


def build_C():
    nc = bass.Bass("TRN2", target_bir_lowering=False)
    es = ExitStack()
    with es:
        ctx = Ctx(es, nc)
        B = Builder(es, nc, ctx)
        xnp = B.dram("xnp", [ND, 128, T], BF16, "ExternalInput")
        sgu = B.dram("sgu", [2, 88, 128, 4, 8, 128], BF16, "ExternalInput")
        sd = B.dram("sd", [32, 128, 8, 11, 128], BF16, "ExternalInput")
        x = B.dram("x", [T, D], F32, "ExternalInput")
        s0 = B.dram("s0", [T, D], F32, "ExternalInput")
        identf_in = B.dram("identf_in", [128, 128], F32, "ExternalInput")
        hd = B.dram("hd", [88, 128, T], BF16, "Internal")
        x1 = B.dram("x1", [T, D], F32, "ExternalOutput")
        B.psum_main(4096)
        B.sbuf("identf", [128, 128], F32)
        ctx.dma("sync", B.sb["identf"][:], identf_in.ap[:, :], B.buf["identf"],
                ins=[identf_in], outs=[B.buf["identf"]])
        B.push_scope()
        emit_resident_load(ctx, B, xnp, "xr")
        B.push_scope()
        emit_ffn_up(ctx, B, sgu, hd)
        B.pop_scope()
        B.pop_scope()
        ctx.fence()
        B.push_scope()
        emit_ffn_down(ctx, B, sd, hd, x, s0, x1)
        B.pop_scope()
        ctx.final_waits()
        ctx.run_block()
    mybir.codegen_inst_isa_subclasses(nc)
    return nc


def build_D():
    nc = bass.Bass("TRN2", target_bir_lowering=False)
    es = ExitStack()
    with es:
        ctx = Ctx(es, nc)
        B = Builder(es, nc, ctx)
        drams = {}
        drams["x1"] = B.dram("x1", [T, D], F32, "ExternalInput")
        for nm, shp in WNAMES_F32.items():
            drams[nm] = B.dram(nm, list(shp), F32, "ExternalInput")
        drams["ident_in"] = B.dram("ident_in", [128, 128], BF16,
                                   "ExternalInput")
        drams["identf_in"] = B.dram("identf_in", [128, 128], F32,
                                    "ExternalInput")
        drams["dup64_in"] = B.dram("dup64_in", [128, 128], BF16,
                                   "ExternalInput")
        drams["xn1"] = B.dram("xn1", [ND, 128, T], BF16, "Internal")
        drams["s1"] = B.dram("s1", [T, D], F32, "Internal")
        drams["out"] = B.dram("out", [T, D], F32, "ExternalOutput")
        for src, dst in CASTS:
            drams[dst] = B.dram(dst, list(WNAMES_F32[src]), BF16, "Internal")
        B.psum_main(4096)
        B.sbuf("ident", [128, 128], BF16)
        B.sbuf("identf", [128, 128], F32)
        ctx.dma("sync", B.sb["ident"][:], drams["ident_in"].ap[:, :],
                B.buf["ident"], ins=[drams["ident_in"]], outs=[B.buf["ident"]])
        ctx.dma("sync", B.sb["identf"][:], drams["identf_in"].ap[:, :],
                B.buf["identf"], ins=[drams["identf_in"]],
                outs=[B.buf["identf"]])
        B.sbuf("rsum", [128, 64], F32)
        B.push_scope()
        B.ln_bufs("l1")
        emit_ln(ctx, B, drams["x1"], drams["xn1"], "l1", permuted=False,
                rsum="rsum", rsum_lo=611, rsum_hi=2047)
        B.pop_scope()
        ctx.fence()
        B.push_scope()
        for i, (src, dst) in enumerate(CASTS):
            emit_cast_bf16(ctx, B, drams[src], drams[dst], f"c{i}")
        B.pop_scope()
        ctx.fence()
        B.push_scope()
        B.sbuf("qhat", [128, 8, T], BF16)
        B.sbuf("khat", [128, 2, T], BF16)
        B.sbuf("khat4", [128, 4, T], BF16)
        B.sbuf("vtm", [128, 4, 16, 64], BF16)
        B.sbuf("dup64", [128, 128], BF16)
        ctx.dma("sync", B.sb["dup64"][:], drams["dup64_in"].ap[:, :],
                B.buf["dup64"], ins=[drams["dup64_in"]],
                outs=[B.buf["dup64"]])
        B.push_scope()
        Wd = dict(drams)
        for src, dst in CASTS:
            Wd[src if src not in ("wqkvT", "qc2T", "qc1bd", "kc1bd", "woT")
               else src] = drams[dst]
        Wd["wqkvT_b"] = drams["wqkvT_b"]
        Wd["qc2T_b"] = drams["qc2T_b"]
        Wd["qc1bd_b"] = drams["qc1bd_b"]
        Wd["kc1bd_b"] = drams["kc1bd_b"]
        Wd["bq"] = drams["bq_b"]
        Wd["mk"] = drams["mk_b"]
        Wd["mq"] = drams["mq_b"]
        Wd["kc2T"] = drams["kc2T_b"]
        Wd["bk"] = drams["bk_b"]
        emit_attn_pre(ctx, B, Wd, drams["xn1"])
        B.pop_scope()
        ctx.fence()
        B.push_scope()
        Wd2 = dict(drams)
        Wd2["woT_b"] = drams["woT_b"]
        emit_attn(ctx, B, Wd2, drams["s1"])
        B.pop_scope()
        B.pop_scope()
        ctx.fence()
        B.push_scope()
        emit_resident_load(ctx, B, drams["xn1"], "xr1")
        Wd3 = dict(drams)
        Wd3["atp_b"] = drams["atp_b"]
        Wd3["w1t_b"] = drams["w1t_b"]
        Wd3["btp_b"] = drams["btp_b"]
        emit_ffn1(ctx, B, Wd3, drams["s1"], drams["x1"], drams["out"])
        B.pop_scope()
        ctx.final_waits()
        ctx.run_block()
    mybir.codegen_inst_isa_subclasses(nc)
    return nc



def build_F(with_gather=True):
    nc = bass.Bass("TRN2", target_bir_lowering=False)
    es = ExitStack()
    with es:
        ctx = Ctx(es, nc)
        B = Builder(es, nc, ctx)
        dr = {}
        dr["x"] = B.dram("x", [T, D], F32, "ExternalInput")
        dr["maskT"] = B.dram("maskT", [32, 128, D], I32, "ExternalInput")
        dr["wtp"] = B.dram("wtp", [32, 128, D], F32, "ExternalInput")
        dr["cb"] = B.dram("cb", [65536, 8], F32, "ExternalInput")
        dr["idxc"] = B.dram("idxc", [128, 20480], I32, "ExternalInput")
        dr["p4d"] = B.dram("p4d", [32, 128, 128], F32, "ExternalInput")
        dr["sel4d"] = B.dram("sel4d", [4, 128, 128], BF16, "ExternalInput")
        dr["ident_in"] = B.dram("ident_in", [128, 128], BF16, "ExternalInput")
        dr["identf_in"] = B.dram("identf_in", [128, 128], F32,
                                 "ExternalInput")
        dr["dup64_in"] = B.dram("dup64_in", [128, 128], BF16, "ExternalInput")
        for nm, shp in WNAMES_F32.items():
            dr[nm] = B.dram(nm, list(shp), F32, "ExternalInput")
        dr["xnp"] = B.dram("xnp", [ND, 128, T], BF16, "Internal")
        dr["w0t"] = B.dram("w0t", [ND, 128, D], BF16, "Internal")
        dr["s0"] = B.dram("s0", [T, D], F32, "Internal")
        if with_gather:
            dr["sgu_sh"] = B.dram("sgu_sh", [2, 11, 128, 4, 8, 128], BF16,
                                  "Internal")
            dr["sd_sh"] = B.dram("sd_sh", [4, 128, 8, 11, 128], BF16,
                                 "Internal")
            dr["ag_gu"] = B.dram("ag_gu", [8, 2, 11, 128, 4, 8, 128], BF16,
                                 "Internal")
            dr["ag_sd"] = B.dram("ag_sd", [8, 4, 128, 8, 11, 128], BF16,
                                 "Internal")
        else:
            dr["ag_gu"] = B.dram("sguF", [2, 88, 128, 4, 8, 128], BF16,
                                 "ExternalInput")
            dr["ag_sd"] = B.dram("sdF", [32, 128, 8, 11, 128], BF16,
                                 "ExternalInput")
        dr["hd"] = B.dram("hd", [88, 128, T], BF16, "Internal")
        dr["x1"] = B.dram("x1", [T, D], F32, "Internal")
        dr["xn1"] = B.dram("xn1", [ND, 128, T], BF16, "Internal")
        dr["s1"] = B.dram("s1", [T, D], F32, "Internal")
        dr["out"] = B.dram("out", [T, D], F32, "ExternalOutput")
        for src, dst in CASTS:
            dr[dst] = B.dram(dst, list(WNAMES_F32[src]), BF16, "Internal")

        B.psum_main(4096)
        B.sbuf("ident", [128, 128], BF16)
        B.sbuf("identf", [128, 128], F32)
        ctx.dma("sync", B.sb["ident"][:], dr["ident_in"].ap[:, :],
                B.buf["ident"], ins=[dr["ident_in"]], outs=[B.buf["ident"]])
        ctx.dma("sync", B.sb["identf"][:], dr["identf_in"].ap[:, :],
                B.buf["identf"], ins=[dr["identf_in"]],
                outs=[B.buf["identf"]])

        if with_gather:
            B.push_scope()
            emit_gather(ctx, B, dr["cb"], dr["idxc"], dr["sgu_sh"],
                        dr["sd_sh"], core=0, dma_eng="gpsimd")
            emit_allgather(ctx, B, dr["sgu_sh"], dr["ag_gu"],
                           2 * 11 * 4 * 8 * 128, 128, B.bufonly("agk1"))
            emit_allgather(ctx, B, dr["sd_sh"], dr["ag_sd"],
                           4 * 8 * 11 * 128, 128, B.bufonly("agk2"))
        gs = set(getattr(ctx, "gather_streams", ()))
        skipE = ("gpsimd",) if with_gather else ()

        B.push_scope()
        B.ln_bufs("l0")
        emit_ln(ctx, B, dr["x"], dr["xnp"], "l0", permuted=True)
        B.pop_scope()
        ctx.fence(skip_streams=gs, skip_engines=skipE)
        B.push_scope()
        emit_w0prep(ctx, B, dr["maskT"], dr["wtp"], dr["p4d"], dr["sel4d"],
                    dr["w0t"])
        B.pop_scope()
        ctx.fence(skip_streams=gs, skip_engines=skipE)
        ctx.fence()
        if with_gather:
            B.pop_scope()
        B.push_scope()
        cast_pairs = [(dr[src], dr[dst]) for src, dst in CASTS]
        bg = make_cast_bg(ctx, B, cast_pairs)
        B.push_scope()
        emit_resident_load(ctx, B, dr["xnp"], "xr")
        B.push_scope()
        emit_s0(ctx, B, dr["w0t"], dr["s0"])
        B.pop_scope()
        B.push_scope()
        emit_ffn_up(ctx, B, dr["ag_gu"], dr["hd"], bg=None)
        B.pop_scope()
        B.pop_scope()
        ctx.fence()
        B.push_scope()
        emit_ffn_down(ctx, B, dr["ag_sd"], dr["hd"], dr["x"], dr["s0"],
                      dr["x1"], bg=None)
        B.pop_scope()
        while bg():
            pass
        B.pop_scope()
        ctx.fence()

        B.sbuf("rsum", [128, 64], F32)
        B.push_scope()
        B.ln_bufs("l1")
        emit_ln(ctx, B, dr["x1"], dr["xn1"], "l1", permuted=False,
                rsum="rsum", rsum_lo=611, rsum_hi=2047)
        B.pop_scope()
        ctx.fence()
        B.push_scope()
        B.sbuf("qhat", [128, 8, T], BF16)
        B.sbuf("khat", [128, 2, T], BF16)
        B.sbuf("khat4", [128, 4, T], BF16)
        B.sbuf("vtm", [128, 4, 16, 64], BF16)
        B.sbuf("dup64", [128, 128], BF16)
        ctx.dma("sync", B.sb["dup64"][:], dr["dup64_in"].ap[:, :],
                B.buf["dup64"], ins=[dr["dup64_in"]], outs=[B.buf["dup64"]])
        B.push_scope()
        Wd = dict(dr)
        Wd["bq"] = dr["bq_b"]
        Wd["mk"] = dr["mk_b"]
        Wd["mq"] = dr["mq_b"]
        Wd["kc2T"] = dr["kc2T_b"]
        Wd["bk"] = dr["bk_b"]
        emit_attn_pre(ctx, B, Wd, dr["xn1"])
        B.pop_scope()
        ctx.fence()
        B.push_scope()
        emit_attn(ctx, B, dr, dr["s1"])
        B.pop_scope()
        B.pop_scope()
        ctx.fence()
        B.push_scope()
        emit_resident_load(ctx, B, dr["xn1"], "xr1")
        emit_ffn1(ctx, B, dr, dr["s1"], dr["x1"], dr["out"])
        B.pop_scope()
        ctx.final_waits()
        ctx.run_block()
    mybir.codegen_inst_isa_subclasses(nc)
    return nc, ctx.nops


def build_G():
    nc = bass.Bass("TRN2", target_bir_lowering=False)
    es = ExitStack()
    with es:
        ctx = Ctx(es, nc)
        B = Builder(es, nc, ctx)
        dr = {}
        dr["x"] = B.dram("x", [T, D], F32, "ExternalInput")
        dr["maskT"] = B.dram("maskT", [32, 128, D], I32, "ExternalInput")
        dr["wtp"] = B.dram("wtp", [32, 128, D], F32, "ExternalInput")
        dr["cb"] = B.dram("cb", [65536, 8], F32, "ExternalInput")
        dr["idxc"] = B.dram("idxc", [128, 18432], I32, "ExternalInput")
        dr["p4d"] = B.dram("p4d", [32, 128, 128], F32, "ExternalInput")
        dr["sel4d"] = B.dram("sel4d", [4, 128, 128], BF16, "ExternalInput")
        dr["ident_in"] = B.dram("ident_in", [128, 128], BF16, "ExternalInput")
        dr["identf_in"] = B.dram("identf_in", [128, 128], F32,
                                 "ExternalInput")
        dr["dup64_in"] = B.dram("dup64_in", [128, 128], BF16, "ExternalInput")
        for nm, shp in WNAMES_F32.items():
            dr[nm] = B.dram(nm, list(shp), F32, "ExternalInput")
        dr["rgu"] = B.dram("rgu", [11, 2, 4, 128, 128, 8], F32, "Internal")
        dr["rd"] = B.dram("rd", [4, 11, 128, 128, 8], F32, "Internal")
        dr["agu0"] = B.dram("agu0", [6, 8, 8, 128, 128, 8], F32, "Internal")
        dr["agu1"] = B.dram("agu1", [5, 8, 8, 128, 128, 8], F32, "Internal")
        dr["agd"] = B.dram("agd", [4, 8, 11, 128, 128, 8], F32, "Internal")
        dr["sdF"] = B.dram("sdF", [32, 128, 8, 11, 128], BF16, "Internal")
        dr["xnp"] = B.dram("xnp", [ND, 128, T], BF16, "Internal")
        dr["w0t"] = B.dram("w0t", [ND, 128, D], BF16, "Internal")
        dr["s0"] = B.dram("s0", [T, D], F32, "Internal")
        dr["hd"] = B.dram("hd", [88, 128, T], BF16, "Internal")
        dr["x1"] = B.dram("x1", [T, D], F32, "Internal")
        dr["xn1"] = B.dram("xn1", [ND, 128, T], BF16, "Internal")
        dr["s1"] = B.dram("s1", [T, D], F32, "Internal")
        dr["out"] = B.dram("out", [T, D], F32, "ExternalOutput")
        for src, dst in CASTS:
            dr[dst] = B.dram(dst, list(WNAMES_F32[src]), BF16, "Internal")

        B.psum_main(4096)
        B.sbuf("ident", [128, 128], BF16)
        B.sbuf("identf", [128, 128], F32)
        ctx.dma("sync", B.sb["ident"][:], dr["ident_in"].ap[:, :],
                B.buf["ident"], ins=[dr["ident_in"]], outs=[B.buf["ident"]])
        ctx.dma("sync", B.sb["identf"][:], dr["identf_in"].ap[:, :],
                B.buf["identf"], ins=[dr["identf_in"]],
                outs=[B.buf["identf"]])

        B.push_scope()
        agst = Stream(es, nc, "agc")
        agu_bufs = [Buf(f"agu{c}") for c in range(11)]
        agd_bufs = [Buf(f"agd{c}") for c in range(4)]
        agu_sel = (dr["agu0"], dr["agu1"])
        emit_gather_fused(ctx, B, dr["cb"], dr["idxc"], dr["rgu"], dr["rd"],
                          agu_sel, dr["agd"], agst, agu_bufs, agd_bufs)

        skipE = ("gpsimd",)
        B.push_scope()
        B.ln_bufs("l0")
        emit_ln(ctx, B, dr["x"], dr["xnp"], "l0", permuted=True)
        B.pop_scope()
        ctx.fence()
        B.push_scope()
        emit_w0prep(ctx, B, dr["maskT"], dr["wtp"], dr["p4d"], dr["sel4d"],
                    dr["w0t"])
        B.pop_scope()
        ctx.fence()
        B.push_scope()
        cast_pairs = [(dr[src], dr[dst]) for src, dst in CASTS]
        bg = make_cast_bg(ctx, B, cast_pairs)
        B.push_scope()
        emit_resident_load(ctx, B, dr["xnp"], "xr")
        B.push_scope()
        emit_s0(ctx, B, dr["w0t"], dr["s0"])
        B.pop_scope()
        ctx.fence()
        B.push_scope()
        emit_ffn_up_raw(ctx, B, agu_sel, agu_bufs, dr["hd"])
        B.pop_scope()
        B.pop_scope()
        ctx.fence()
        B.push_scope()
        dc_waits = {}
        dgroup = make_dcast(ctx, B, dr["agd"], agd_bufs, dr["sdF"], dc_waits)
        fdpart = make_ffn_down(ctx, B, dr["sdF"], dr["hd"], dr["x"],
                               dr["s0"], dr["x1"], dc_waits)
        for dci in range(4):
            dgroup(dci)
            fdpart(0, [4 * r + dci for r in range(8)])
        dc_all = [4 * r + dci for dci in range(4) for r in range(8)]
        for tq in (1, 2, 3):
            fdpart(tq, dc_all)
        B.pop_scope()
        while bg():
            pass
        B.pop_scope()
        B.pop_scope()
        ctx.fence()

        B.sbuf("rsum", [128, 64], F32)
        B.push_scope()
        B.ln_bufs("l1")
        emit_ln(ctx, B, dr["x1"], dr["xn1"], "l1", permuted=False,
                rsum="rsum", rsum_lo=611, rsum_hi=2047)
        B.pop_scope()
        ctx.fence()
        B.push_scope()
        B.sbuf("qhat", [128, 8, T], BF16)
        B.sbuf("khat", [128, 2, T], BF16)
        B.sbuf("khat4", [128, 4, T], BF16)
        B.sbuf("vtm", [128, 4, 16, 64], BF16)
        B.sbuf("dup64", [128, 128], BF16)
        ctx.dma("sync", B.sb["dup64"][:], dr["dup64_in"].ap[:, :],
                B.buf["dup64"], ins=[dr["dup64_in"]], outs=[B.buf["dup64"]])
        B.push_scope()
        Wd = dict(dr)
        Wd["bq"] = dr["bq_b"]
        Wd["mk"] = dr["mk_b"]
        Wd["mq"] = dr["mq_b"]
        Wd["kc2T"] = dr["kc2T_b"]
        Wd["bk"] = dr["bk_b"]
        emit_attn_pre(ctx, B, Wd, dr["xn1"])
        B.pop_scope()
        ctx.fence()
        B.push_scope()
        emit_attn(ctx, B, dr, dr["s1"])
        B.pop_scope()
        B.pop_scope()
        ctx.fence()
        B.push_scope()
        emit_resident_load(ctx, B, dr["xn1"], "xr1")
        emit_ffn1(ctx, B, dr, dr["s1"], dr["x1"], dr["out"])
        B.pop_scope()
        ctx.final_waits()
        ctx.run_block()
    mybir.codegen_inst_isa_subclasses(nc)
    return nc


def kernel(**inputs):
    inp = {k: np.asarray(v) for k, v in inputs.items()}
    x_all = inp["L0_attn_data"].astype(np.float32)
    weight = inp["L0_attn_weight"].astype(np.float32)
    mask = inp["L0_attn_mask"].astype(np.int32)
    cb = inp["L0_ffn_codebook"].astype(np.float32)
    p4, sel4, identb, identf, dup = host_consts()
    wtp = np.ascontiguousarray(weight.T[PERM_H]).reshape(32, 128, D)
    maskT = np.ascontiguousarray(mask.T).reshape(32, 128, D)
    W1 = host_l1(inp)
    nc = build_G()
    ims = []
    for c in range(8):
        im = {"x": np.ascontiguousarray(x_all[c]), "maskT": maskT,
              "wtp": wtp, "cb": cb,
              "idxc": host_idxcalls_f(inp["L0_ffn_idx_gate"],
                                      inp["L0_ffn_idx_up"],
                                      inp["L0_ffn_idx_down"], c),
              "p4d": p4, "sel4d": sel4, "ident_in": identb,
              "identf_in": identf, "dup64_in": dup}
        for nm in WNAMES_F32:
            im[nm] = np.ascontiguousarray(W1[nm]).astype(np.float32)
        ims.append(im)
    res = run_bass_kernel_spmd(nc, ims, core_ids=list(range(8)))
    out = np.stack([r["out"] for r in res.results], axis=0)
    return out.astype(np.float32)


# version 17 (final)
# speedup vs baseline: 1.1825x; 1.0018x over previous
import os
import traceback
import numpy as np
import ml_dtypes
from contextlib import ExitStack

DEBUG_EMIT = os.environ.get("MK_DEBUG_EMIT", "0") == "1"
import concourse.bass as bass
import concourse.mybir as mybir
from concourse.bass_utils import run_bass_kernel_spmd

F32 = mybir.dt.float32
BF16 = mybir.dt.bfloat16
I32 = mybir.dt.int32
ALU = mybir.AluOpType
AF = mybir.ActivationFunctionType
AX = mybir.AxisListType
BF = ml_dtypes.bfloat16

T = 2048
D = 4096
I_FFN = 11008
NT = T // 128
ND = D // 128

PERM_H = np.array([1024 * (kk % 4) + 8 * r + (kk // 4)
                   for kk in range(32) for r in range(128)])
PERM_I = np.array([8 * (ip % 1376) + ip // 1376 for ip in range(I_FFN)])


class Stream:
    def __init__(self, es, nc, name):
        self.sem = es.enter_context(nc.semaphore(name))
        self.name = name
        self.n = 0


class Buf:
    __slots__ = ("writer", "readers", "name", "dstream")

    def __init__(self, name=""):
        self.writer = None
        self.readers = {}
        self.name = name
        self.dstream = None


class Ctx:
    ENGS = ["sync", "tensor", "vector", "scalar", "gpsimd"]

    def __init__(self, es, nc):
        self.nc = nc
        self.es = es
        self.ops = {e: [] for e in self.ENGS}
        self.t = Stream(es, nc, "t")
        self.v = Stream(es, nc, "v")
        self.s = Stream(es, nc, "s")
        self.p = Stream(es, nc, "p")
        self.waited = {}
        self.nops = 0
        self.dstreams = []
        self.free_dstreams = []
        self.nsem = 0

    def dstream_for(self, key):
        if key.dstream is None:
            if self.free_dstreams:
                key.dstream = self.free_dstreams.pop()
            else:
                key.dstream = Stream(self.es, self.nc, f"d{self.nsem}")
                self.nsem += 1
                self.dstreams.append(key.dstream)
        return key.dstream

    def emit(self, eng, fn):
        if DEBUG_EMIT:
            fn._tb = traceback.format_stack()[-4:-1]
        self.ops[eng].append(fn)
        self.nops += 1

    def _edges(self, stream, ins, outs):
        need = {}
        for b in ins:
            if b.writer:
                s, v = b.writer
                if not (s is stream and s.name == "t"):
                    need[s] = max(need.get(s, 0), v)
        for b in outs:
            if b.writer:
                s, v = b.writer
                if not (s is stream and s.name == "t"):
                    need[s] = max(need.get(s, 0), v)
            for s, v in b.readers.items():
                if not (s is stream and s.name == "t"):
                    need[s] = max(need.get(s, 0), v)
        waits = []
        for s, v in need.items():
            key = (stream, s)
            if self.waited.get(key, 0) >= v:
                continue
            self.waited[key] = v
            waits.append((s, v))
        return waits

    def _post(self, stream, ins, outs):
        for b in outs:
            b.writer = (stream, stream.n)
            b.readers = {}
        for b in ins:
            b.readers[stream] = stream.n

    def op(self, eng, fn, ins=(), outs=(), extra_waits=()):
        stream = {"tensor": self.t, "vector": self.v, "scalar": self.s,
                  "gpsimd": self.p}[eng]
        waits = self._edges(stream, ins, outs) + list(extra_waits)
        stream.n += 1

        def f(e, fn=fn, waits=waits, stream=stream):
            for s, v in waits:
                e.wait_ge(s.sem, v)
            fn(e).then_inc(stream.sem, 1)
        self.emit(eng, f)
        self._post(stream, ins, outs)
        return stream.n

    def dma(self, eng, out_ap, in_ap, key, ins=(), outs=(), extra_waits=(),
            **kw):
        stream = self.dstream_for(key)
        waits = self._edges(stream, ins, outs) + list(extra_waits)
        stream.n += 16

        def f(e, out_ap=out_ap, in_ap=in_ap, waits=waits, stream=stream, kw=kw):
            for s, v in waits:
                e.wait_ge(s.sem, v)
            e.dma_start(out=out_ap, in_=in_ap, **kw).then_inc(stream.sem, 16)
        self.emit(eng, f)
        self._post(stream, ins, outs)
        return stream.n

    def gather(self, out_ap, in_ap, idx_ap, key, ins=(), outs=(),
               extra_waits=()):
        stream = self.dstream_for(key)
        waits = self._edges(stream, ins, outs) + list(extra_waits)
        stream.n += 16

        def f(e, out_ap=out_ap, in_ap=in_ap, idx_ap=idx_ap, waits=waits):
            for s, v in waits:
                e.wait_ge(s.sem, v)
            e.indirect_dma_start(
                out_ap, None, in_ap,
                bass.IndirectOffsetOnAxis(idx_ap, 0)).then_inc(stream.sem, 16)
        self.emit("gpsimd", f)
        self._post(stream, ins, outs)
        return stream.n

    def barrier_wait(self, eng, stream, val):
        def f(e, stream=stream, val=val):
            e.wait_ge(stream.sem, val)
        self.emit(eng, f)

    def fence(self, skip_streams=(), skip_engines=()):
        skip = set(skip_streams)
        snap = [(st, st.n) for st in
                [self.t, self.v, self.s, self.p] + self.dstreams
                if st.n > 0 and st not in skip]
        engs = {"sync": None, "tensor": self.t, "vector": self.v,
                "scalar": self.s, "gpsimd": self.p}
        for eng, own in engs.items():
            if eng in skip_engines:
                continue
            for (st, v) in snap:
                if st is own:
                    continue
                self.barrier_wait(eng, st, v)
        for dst_st in [self.t, self.v, self.s, self.p] + self.dstreams:
            if dst_st in skip:
                continue
            for (st, v) in snap:
                self.waited[(dst_st, st)] = max(
                    self.waited.get((dst_st, st), 0), v)

    def final_waits(self, eng="gpsimd"):
        for st in [self.t, self.v, self.s, self.p] + self.dstreams:
            if st.n > 0:
                self.barrier_wait(eng, st, st.n)

    def run_block(self):
        nc = self.nc
        def _runall(e, fns):
            for f in fns:
                try:
                    f(e)
                except Exception:
                    if hasattr(f, "_tb"):
                        print("EMIT SITE:\n" + "".join(f._tb))
                    raise
        with nc.Block() as block:
            @block.sync
            def _(e):
                _runall(e, self.ops["sync"])

            @block.tensor
            def _(e):
                _runall(e, self.ops["tensor"])

            @block.vector
            def _(e):
                _runall(e, self.ops["vector"])

            @block.scalar
            def _(e):
                _runall(e, self.ops["scalar"])

            @block.gpsimd
            def _(e):
                _runall(e, self.ops["gpsimd"])


def emit_ln(ctx, B, x_dram, xnp_dram, names, *, permuted, rsum=None,
            rsum_lo=0, rsum_hi=0):
    nc = ctx.nc
    eps = 1e-5
    sb, bf = B.sb, B.buf
    idn = sb["ident"][:]
    if rsum is not None:
        B.bufonly(rsum + "_part")
        ctx.op("vector", lambda e: e.memset(sb[rsum][:], 0.0),
               ins=[], outs=[bf[rsum]])
    for tt in range(NT):
        sl = tt % 2
        xt, xtb = sb[f"{names}xt{sl}"], bf[f"{names}xt{sl}"]
        xnb, xnbb = sb[f"{names}xnb{sl}"], bf[f"{names}xnb{sl}"]
        sq, sqb = sb[f"{names}sq"], bf[f"{names}sq"]
        mom, momb = sb[f"{names}mom"], bf[f"{names}mom"]
        ctx.dma("sync", xt[:], x_dram.ap[tt * 128:(tt + 1) * 128, :], xtb,
                ins=[x_dram], outs=[xtb])
        ctx.op("vector", lambda e, xt=xt, mom=mom: e.reduce_sum(
            mom[:, 0:1], xt[:], AX.X), ins=[xtb], outs=[momb])
        ctx.op("scalar", lambda e, xt=xt, sq=sq: e.activation(
            sq[:], xt[:], AF.Square), ins=[xtb], outs=[sqb])
        ctx.op("vector", lambda e, sq=sq, mom=mom: e.reduce_sum(
            mom[:, 1:2], sq[:], AX.X), ins=[sqb], outs=[momb])
        ctx.op("vector", lambda e, mom=mom: e.tensor_scalar(
            mom[:, 0:1], mom[:, 0:1], 1.0 / D, None, ALU.mult),
            ins=[momb], outs=[momb])
        ctx.op("vector", lambda e, mom=mom: e.tensor_scalar(
            mom[:, 1:2], mom[:, 1:2], 1.0 / D, None, ALU.mult),
            ins=[momb], outs=[momb])
        ctx.op("vector", lambda e, mom=mom: e.tensor_tensor(
            mom[:, 2:3], mom[:, 0:1], mom[:, 0:1], ALU.mult),
            ins=[momb], outs=[momb])
        ctx.op("vector", lambda e, mom=mom: e.tensor_tensor(
            mom[:, 1:2], mom[:, 1:2], mom[:, 2:3], ALU.subtract),
            ins=[momb], outs=[momb])
        ctx.op("vector", lambda e, mom=mom: e.tensor_scalar(
            mom[:, 1:2], mom[:, 1:2], float(eps), None, ALU.add),
            ins=[momb], outs=[momb])
        ctx.op("scalar", lambda e, mom=mom: e.activation(
            mom[:, 2:3], mom[:, 1:2], AF.Sqrt),
            ins=[momb], outs=[momb])
        ctx.op("vector", lambda e, mom=mom: e.reciprocal(
            mom[:, 1:2], mom[:, 2:3]), ins=[momb], outs=[momb])
        ctx.op("vector", lambda e, mom=mom: e.tensor_tensor(
            mom[:, 3:4], mom[:, 0:1], mom[:, 1:2], ALU.mult),
            ins=[momb], outs=[momb])
        ctx.op("vector", lambda e, mom=mom: e.tensor_scalar(
            mom[:, 3:4], mom[:, 3:4], -1.0, None, ALU.mult),
            ins=[momb], outs=[momb])
        ctx.op("scalar", lambda e, xt=xt, xnb=xnb, mom=mom: e.activation(
            xnb[:], xt[:], AF.Identity, bias=mom[:, 3:4], scale=mom[:, 1:2]),
            ins=[xtb, momb], outs=[xnbb])
        for kk in range(ND):
            psl = (tt * ND + kk) % 2
            sl8 = (tt * ND + kk) % 8
            pst = B.ps_ap[:, 512 * psl:512 * psl + 64].bitcast(BF16)
            pstb = bf[f"pst{psl}"]
            tout, toutb = sb[f"{names}to{sl8}"], bf[f"{names}to{sl8}"]
            if permuted:
                j, b = kk // 4, kk % 4
                src = xnb[:, 1024 * b + j: 1024 * b + j + 1017: 8]
            else:
                src = xnb[:, kk * 128:(kk + 1) * 128]
            ctx.op("tensor", lambda e, src=src, pst=pst: e.transpose(
                pst, src, idn), ins=[xnbb, bf["ident"]], outs=[pstb])
            ctx.op("vector", lambda e, pst=pst, tout=tout: e.tensor_copy(
                tout[:], pst), ins=[pstb], outs=[toutb])
            if rsum is not None:
                lo = max(rsum_lo, tt * 128) - tt * 128
                hi = min(rsum_hi, (tt + 1) * 128) - tt * 128
                if hi > lo:
                    rs, rsb = sb[rsum], bf[rsum]
                    ctx.op("vector",
                           lambda e, tout=tout, rs=rs, kk=kk, lo=lo, hi=hi:
                           e.tensor_reduce(rs[:, 32 + kk:33 + kk],
                                           tout[:, lo:hi], AX.X, ALU.add),
                           ins=[toutb], outs=[bf[rsum + "_part"]])
                    ctx.op("vector",
                           lambda e, rs=rs, kk=kk: e.tensor_tensor(
                               rs[:, kk:kk + 1], rs[:, kk:kk + 1],
                               rs[:, 32 + kk:33 + kk], ALU.add),
                           ins=[bf[rsum + "_part"], rsb], outs=[rsb])
            ctx.dma("sync",
                    xnp_dram.ap[kk, :, tt * 128:(tt + 1) * 128], tout[:],
                    toutb, ins=[toutb], outs=[xnp_dram])


class DramBuf(Buf):
    __slots__ = ("ap",)

    def __init__(self, name, ap):
        super().__init__(name)
        self.ap = ap


class Builder:
    def __init__(self, es, nc, ctx=None):
        self.nc, self.es = nc, es
        self.sb = {}
        self.buf = {}
        self.ps_ap = None
        self.scopes = []
        self.scope_bufs = [[]]
        self.ctx = ctx

    def push_scope(self):
        st = ExitStack()
        self.scopes.append(st)
        self.scope_bufs.append([])
        return st

    def pop_scope(self):
        self.scopes.pop().close()
        for b in self.scope_bufs.pop():
            if b.dstream is not None and self.ctx is not None:
                self.ctx.free_dstreams.append(b.dstream)
                b.dstream = None

    def sbuf(self, name, shape, dtype):
        stack = self.scopes[-1] if self.scopes else self.es
        t = stack.enter_context(self.nc.sbuf_tensor(name, shape, dtype))
        self.sb[name] = t
        b = Buf(name)
        self.buf[name] = b
        self.scope_bufs[-1].append(b)
        return t

    def bufonly(self, name):
        self.buf[name] = Buf(name)
        return self.buf[name]

    def psum_main(self, cols=2048):
        t = self.es.enter_context(self.nc.psum_tensor("psmain", [128, cols], F32))
        self.ps_ap = t[:]
        self.buf["pst0"] = Buf("pst0")
        self.buf["pst1"] = Buf("pst1")
        return t

    def dram(self, name, shape, dtype, kind, **kw):
        t = self.nc.dram_tensor(name, list(shape), dtype, kind=kind, **kw)
        return DramBuf(name, t.ap())

    def ln_bufs(self, names):
        for sl in range(2):
            self.sbuf(f"{names}xt{sl}", [128, D], F32)
            self.sbuf(f"{names}xnb{sl}", [128, D], BF16)
        for sl in range(8):
            self.sbuf(f"{names}to{sl}", [128, 128], BF16)
        self.sbuf(f"{names}sq", [128, D], F32)
        self.sbuf(f"{names}mom", [128, 8], F32)


def mm_split(ctx, ps_ap, lhsT_ap, rhs_ap, ins, outs, start, stop, nmax=512):
    n = rhs_ap.shape[-1]
    for o in range(0, n, nmax):
        w = min(nmax, n - o)
        ctx.op("tensor", lambda e, o=o, w=w: e.matmul(
            ps_ap[:, o:o + w], lhsT_ap, rhs_ap[:, o:o + w],
            start=start, stop=stop), ins=ins, outs=outs)


def emit_w0prep(ctx, B, maskT_dram, wtp_dram, p4_dram, sel4_dram, w0t_dram):
    sb, bf = B.sb, B.buf
    p4 = B.sbuf("p4", [128, 32, 128], F32)
    sel4 = B.sbuf("sel4", [128, 4, 128], BF16)
    ctx.dma("sync", p4[:], p4_dram.ap.rearrange("a p b -> p a b"),
            bf["p4"], ins=[p4_dram], outs=[bf["p4"]])
    ctx.dma("sync", sel4[:], sel4_dram.ap.rearrange("a p b -> p a b"),
            bf["sel4"], ins=[sel4_dram], outs=[bf["sel4"]])
    for sl in range(2):
        B.sbuf(f"w2m{sl}", [128, D], I32)
        B.sbuf(f"w2mf{sl}", [128, D], F32)
        B.sbuf(f"w2w{sl}", [128, D], F32)
        B.sbuf(f"w2o{sl}", [128, D], BF16)
    bmf = B.sbuf("bmf", [128, D], BF16)
    B.bufonly("pbs0")
    B.bufonly("pbs1")
    B.bufonly("pem0")
    B.bufonly("pem1")
    for Tt in range(32):
        sl = Tt % 2
        mt, mtb = sb[f"w2m{sl}"], bf[f"w2m{sl}"]
        mf, mfb = sb[f"w2mf{sl}"], bf[f"w2mf{sl}"]
        ctx.dma("sync", mt[:], maskT_dram.ap[Tt, :, :], mtb,
                ins=[maskT_dram], outs=[mtb])
        ctx.op("vector", lambda e, mt=mt, mf=mf: e.tensor_copy(mf[:], mt[:]),
               ins=[mtb], outs=[mfb])
        for h in range(2):
            mm_split(ctx, B.ps_ap[:, 2048 * h:2048 * h + 2048], p4[:, Tt, :],
                     mf[:, 2048 * h:2048 * h + 2048],
                     ins=[mfb, bf["p4"]], outs=[bf[f"pbs{h}"]],
                     start=(Tt == 0), stop=(Tt == 31))
    ctx.op("vector", lambda e: e.tensor_scalar(
        bmf[:, 0:2048], B.ps_ap[:, 0:2048], 0.0, None, ALU.is_gt),
        ins=[bf["pbs0"]], outs=[bf["bmf"]])
    ctx.op("vector", lambda e: e.tensor_scalar(
        bmf[:, 2048:4096], B.ps_ap[:, 2048:4096], 0.0, None, ALU.is_gt),
        ins=[bf["pbs1"]], outs=[bf["bmf"]])
    for kk in range(32):
        sl = kk % 2
        wt, wtb = sb[f"w2w{sl}"], bf[f"w2w{sl}"]
        wo, wob = sb[f"w2o{sl}"], bf[f"w2o{sl}"]
        ctx.dma("sync", wt[:], wtp_dram.ap[kk, :, :], wtb,
                ins=[wtp_dram], outs=[wtb])
        b = kk % 4
        for h in range(2):
            pe0 = bf[f"pem{h}"]
            mm_split(ctx, B.ps_ap[:, 2048 * h:2048 * h + 2048],
                     sel4[:, b, :], bmf[:, 2048 * h:2048 * h + 2048],
                     ins=[bf["bmf"], bf["sel4"]], outs=[pe0],
                     start=True, stop=True)
            ctx.op("vector", lambda e, wt=wt, wo=wo, h=h: e.tensor_tensor(
                wo[:, 2048 * h:2048 * h + 2048],
                wt[:, 2048 * h:2048 * h + 2048],
                B.ps_ap[:, 2048 * h:2048 * h + 2048], ALU.mult),
                ins=[wtb, pe0], outs=[wob])
        ctx.dma("sync", w0t_dram.ap[kk, :, :], wo[:], wob,
                ins=[wob], outs=[w0t_dram])


def emit_resident_load(ctx, B, xnp_dram, name="xr"):
    xr = B.sbuf(name, [128, ND, T], BF16)
    ctx.dma("sync", xr[:, :, :],
            xnp_dram.ap.rearrange("k p t -> p k t"), B.buf[name],
            ins=[xnp_dram], outs=[B.buf[name]])
    return xr


def emit_s0(ctx, B, w0t_dram, s0_dram):
    sb, bf = B.sb, B.buf
    xr = sb["xr"]
    B.bufonly("ps30")
    B.bufonly("ps3t0")
    B.bufonly("ps3t1")
    for sl in range(2):
        B.sbuf(f"s3W{sl}", [128, 32, 128], BF16)
    for sl in range(2):
        B.sbuf(f"s3e{sl}", [128, 2048], F32)
    for sl in range(8):
        B.sbuf(f"s3o{sl}", [128, 128], F32)
    for nc_ in range(ND):
        ps = B.ps_ap[:, 0:2048]
        psb = bf["ps30"]
        sl = nc_ % 2
        wt, wtb = sb[f"s3W{sl}"], bf[f"s3W{sl}"]
        ctx.dma("sync", wt[:],
                w0t_dram.ap[:, :, nc_ * 128:(nc_ + 1) * 128].rearrange(
                    "k p c -> p k c"),
                wtb, ins=[w0t_dram], outs=[wtb])
        for kk in range(ND):
            mm_split(ctx, ps, wt[:, kk, :], xr[:, kk, :],
                     ins=[wtb, bf["xr"]], outs=[psb],
                     start=(kk == 0), stop=(kk == 31))
        sl2 = nc_ % 2
        ev, evb = sb[f"s3e{sl2}"], bf[f"s3e{sl2}"]
        ctx.op("vector", lambda e, ev=ev, ps=ps: e.tensor_copy(ev[:], ps),
               ins=[psb], outs=[evb])
        for tc in range(16):
            psl = tc % 2
            sl8 = tc % 8
            pst = B.ps_ap[:, 2048 + 512 * psl:2048 + 512 * psl + 128]
            pstb = bf[f"ps3t{psl}"]
            so, sob = sb[f"s3o{sl8}"], bf[f"s3o{sl8}"]
            ctx.op("tensor", lambda e, ev=ev, tc=tc, pst=pst: e.transpose(
                pst, ev[:, tc * 128:(tc + 1) * 128], sb["identf"][:]),
                ins=[evb, bf["identf"]], outs=[pstb])
            ctx.op("vector", lambda e, so=so, pst=pst: e.tensor_copy(
                so[:], pst), ins=[pstb], outs=[sob])
            ctx.dma("sync",
                    s0_dram.ap[tc * 128:(tc + 1) * 128, nc_ * 128:(nc_ + 1) * 128],
                    so[:], sob, ins=[sob], outs=[s0_dram])


CNTS = [128] * 10 + [96]


def emit_gather(ctx, B, cb_dram, idx_dram, sgu_dram, sd_dram, core,
                dma_eng="sync"):
    sb, bf = B.sb, B.buf
    ctx.gather_streams = getattr(ctx, "gather_streams", set())
    cbsrc = cb_dram.ap
    idxt = B.sbuf("g4idx", [128, 4096], I32)
    for sl in range(2):
        B.sbuf(f"g4d{sl}", [128, 128, 8], F32)
        B.sbuf(f"g4s{sl}", [128, 8, 128], BF16)
    call = 0
    seg = 0
    idx_chunk = -1
    last_reorder = [0, 0]

    def idx_col(k):
        nonlocal idx_chunk
        cc = k // 4096
        if cc != idx_chunk:
            w = min(4096, idx_dram.ap.shape[1] - cc * 4096)
            ctx.dma(dma_eng, idxt[:, 0:w],
                    idx_dram.ap[:, cc * 4096:cc * 4096 + w], bf["g4idx"],
                    ins=[idx_dram], outs=[bf["g4idx"]])
            ctx.gather_streams.add(bf["g4idx"].dstream)
            idx_chunk = cc
        return idxt[:, (k % 4096):(k % 4096) + 1]

    def do_segment(ncalls, npart, store_fn):
        nonlocal call, seg
        sl = seg % 2
        seg += 1
        dstb, dstbb = sb[f"g4d{sl}"], bf[f"g4d{sl}"]
        stg, stgb = sb[f"g4s{sl}"], bf[f"g4s{sl}"]
        ds = ctx.dstream_for(dstbb)
        ctx.gather_streams.add(ds)
        for s in range(ncalls):
            ew = [(ctx.v, last_reorder[sl])] if s == 0 and last_reorder[sl] else ()
            ctx.gather(dstb[0:npart, s, :], cbsrc, idx_col(call + s),
                       key=dstbb, ins=[cb_dram, bf["g4idx"]], outs=(),
                       extra_waits=ew)
        src = dstb[0:128, 0:ncalls, 0:8].rearrange("p s j -> p j s")
        ctx.op("vector", lambda e, src=src, stg=stg, ncalls=ncalls:
               e.tensor_copy(stg[:, :, 0:ncalls], src),
               ins=[], outs=[stgb], extra_waits=[(ds, ds.n)])
        last_reorder[sl] = ctx.v.n
        store_fn(stg, stgb, ncalls)
        call += ncalls

    for m in range(2):
        for cdc in range(11):
            cnt = CNTS[cdc]
            for b in range(4):
                def store_gu(stg, stgb, ncalls, m=m, cdc=cdc, b=b, cnt=cnt):
                    for j in range(8):
                        ctx.dma(dma_eng,
                                sgu_dram.ap[m, cdc, :, b, j, 0:cnt],
                                stg[:, j, 0:cnt], stgb,
                                ins=[stgb], outs=[sgu_dram])
                        ctx.gather_streams.add(stgb.dstream)
                do_segment(cnt, 128, store_gu)
    for dci in range(4):
        for cdc in range(11):
            cnt = CNTS[cdc]
            def store_d(stg, stgb, ncalls, dci=dci, cdc=cdc, cnt=cnt):
                for j in range(8):
                    ctx.dma(dma_eng,
                            sd_dram.ap[dci, 0:cnt, j, cdc, :],
                            stg[0:cnt, j, 0:128], stgb,
                            ins=[stgb], outs=[sd_dram])
                    ctx.gather_streams.add(stgb.dstream)
            do_segment(128, cnt, store_d)


def host_idxcalls(idx_gate, idx_up, idx_down, core):
    cols = []
    jj = core
    for idx_m in (idx_gate, idx_up):
        for cdc in range(11):
            cnt = CNTS[cdc]
            for b in range(4):
                for s in range(cnt):
                    row = 8 * (cdc * 128 + s) + jj
                    cols.append(idx_m[row, 128 * b:128 * b + 128])
    for dci in range(4):
        dc = 4 * core + dci
        for cdc in range(11):
            cnt = CNTS[cdc]
            for s in range(128):
                d = dc * 128 + s
                col = np.zeros(128, dtype=np.int32)
                col[0:cnt] = idx_down[d, 128 * cdc:128 * cdc + cnt]
                cols.append(col)
    arr = np.stack(cols, axis=1).astype(np.int32)
    pad = (-arr.shape[1]) % 4096
    if pad:
        arr = np.concatenate([arr, np.zeros((128, pad), np.int32)], axis=1)
    return np.ascontiguousarray(arr)


def emit_ffn_up(ctx, B, sgu_dram, h_dram, bg=None):
    sb, bf = B.sb, B.buf
    xr = sb["xr"]
    for sl in range(2):
        for m in range(2):
            B.sbuf(f"f5w{m}{sl}", [128, 32, 128], BF16)
        B.sbuf(f"f5h{sl}", [128, T], BF16)
    B.sbuf("f5a", [128, T], F32)
    B.bufonly("ps5g")
    B.bufonly("ps5u")
    for jj in range(8):
        for cdc in range(11):
            if bg is not None:
                bg()
            iic = jj * 11 + cdc
            cnt = CNTS[cdc]
            sl = iic % 2
            for m in range(2):
                w, wb = sb[f"f5w{m}{sl}"], bf[f"f5w{m}{sl}"]
                if len(sgu_dram.ap.shape) == 7:
                    src_ap = sgu_dram.ap[jj, m, cdc].rearrange(
                        "c b j i -> c (b j) i")
                else:
                    src_ap = sgu_dram.ap[m, iic].rearrange(
                        "c b j i -> c (b j) i")
                ctx.dma("sync", w[:], src_ap,
                        wb, ins=[sgu_dram], outs=[wb])
                ps = B.ps_ap[:, 2048 * m:2048 * m + 2048]
                psb = bf["ps5g"] if m == 0 else bf["ps5u"]
                for b in range(4):
                    for j in range(8):
                        kk = j * 4 + b
                        mm_split(ctx, ps[0:cnt, :], w[:, b * 8 + j, 0:cnt],
                                 xr[:, kk, :], ins=[wb, bf["xr"]], outs=[psb],
                                 start=(b == 0 and j == 0),
                                 stop=(b == 3 and j == 7))
            act, actb = sb["f5a"], bf["f5a"]
            hb, hbb = sb[f"f5h{sl}"], bf[f"f5h{sl}"]
            ctx.op("scalar", lambda e, act=act, cnt=cnt: e.activation(
                act[0:cnt, :], B.ps_ap[0:cnt, 0:2048], AF.Silu),
                ins=[bf["ps5g"]], outs=[actb])
            ctx.op("vector", lambda e, act=act, hb=hb, cnt=cnt: e.tensor_tensor(
                hb[0:cnt, :], act[0:cnt, :], B.ps_ap[0:cnt, 2048:4096],
                ALU.mult), ins=[actb, bf["ps5u"]], outs=[hbb])
            ctx.dma("sync", h_dram.ap[iic, 0:cnt, :], hb[0:cnt, :], hbb,
                    ins=[hbb], outs=[h_dram])


def emit_ffn_down(ctx, B, sd_dram, h_dram, x_dram, s0_dram,
                  x1_dram, bg=None, dc_order=None, dc_waits=None):
    sb, bf = B.sb, B.buf
    hq = B.sbuf("f6h", [128, 88, 512], BF16)
    for sl in range(2):
        B.sbuf(f"f6w{sl}", [128, 88, 128], BF16)
        B.sbuf(f"f6g{sl}", [128, 512], F32)
        B.bufonly(f"ps6y{sl}")
        B.bufonly(f"ps6t{sl}")
    for sl in range(8):
        B.sbuf(f"f6x{sl}", [128, 128], F32)
        B.sbuf(f"f6s{sl}", [128, 128], F32)
        B.sbuf(f"f6o{sl}", [128, 128], F32)
        B.sbuf(f"f6t{sl}", [128, 128], F32)
    for tq in range(4):
        ctx.dma("sync", hq[:, :, :],
                h_dram.ap[:, :, tq * 512:(tq + 1) * 512].rearrange(
                    "i p t -> p i t"), bf["f6h"],
                ins=[h_dram], outs=[bf["f6h"]])
        for dci_, dc in enumerate(dc_order or range(32)):
            if bg is not None:
                bg()
            sl = dci_ % 2
            w, wb = sb[f"f6w{sl}"], bf[f"f6w{sl}"]
            if len(sd_dram.ap.shape) == 6:
                src_ap = sd_dram.ap[dc // 4, dc % 4].rearrange(
                    "p j c d -> p (j c) d")
            else:
                src_ap = sd_dram.ap[dc].rearrange("p j c d -> p (j c) d")
            ew = tuple(dc_waits.get(dc, ())) if dc_waits else ()
            ctx.dma("sync", w[:], src_ap,
                    wb, ins=([] if dc_waits else [sd_dram]), outs=[wb],
                    extra_waits=ew)
            ps = B.ps_ap[:, 512 * sl:512 * sl + 512]
            psb = bf[f"ps6y{sl}"]
            for jj in range(8):
                for cdc in range(11):
                    i88 = jj * 11 + cdc
                    cnt = CNTS[cdc]
                    ctx.op("tensor", lambda e, w=w, i88=i88, cnt=cnt, ps=ps:
                           e.matmul(ps, w[0:cnt, i88, :], hq[0:cnt, i88, :],
                                    start=(i88 == 0), stop=(i88 == 87)),
                           ins=[wb, bf["f6h"]], outs=[psb])
            g0, g0b = sb[f"f6g{sl}"], bf[f"f6g{sl}"]
            ctx.op("scalar", lambda e, g0=g0, ps=ps: e.activation(
                g0[:], ps, AF.Silu), ins=[psb], outs=[g0b])
            for ts in range(4):
                sl8 = (dci_ * 4 + ts) % 8
                pst = B.ps_ap[:, 1024 + 512 * sl:1024 + 512 * sl + 128]
                pstb = bf[f"ps6t{sl}"]
                gt, gtb = sb[f"f6t{sl8}"], bf[f"f6t{sl8}"]
                xt, xtb = sb[f"f6x{sl8}"], bf[f"f6x{sl8}"]
                st, stb = sb[f"f6s{sl8}"], bf[f"f6s{sl8}"]
                ot, otb = sb[f"f6o{sl8}"], bf[f"f6o{sl8}"]
                rows = slice((tq * 4 + ts) * 128, (tq * 4 + ts) * 128 + 128)
                cols = slice(dc * 128, dc * 128 + 128)
                ctx.op("tensor", lambda e, g0=g0, ts=ts, pst=pst: e.transpose(
                    pst, g0[:, ts * 128:(ts + 1) * 128], sb["identf"][:]),
                    ins=[g0b, bf["identf"]], outs=[pstb])
                ctx.op("vector", lambda e, gt=gt, pst=pst: e.tensor_copy(
                    gt[:], pst), ins=[pstb], outs=[gtb])
                ctx.dma("sync", xt[:], x_dram.ap[rows, cols], xtb,
                        ins=[x_dram], outs=[xtb])
                ctx.dma("scalar", st[:], s0_dram.ap[rows, cols], stb,
                        ins=[s0_dram], outs=[stb])
                ctx.op("vector", lambda e, gt=gt, st=st: e.tensor_tensor(
                    gt[:], gt[:], st[:], ALU.mult), ins=[gtb, stb], outs=[gtb])
                ctx.op("vector", lambda e, gt=gt, xt=xt, ot=ot: e.tensor_tensor(
                    ot[:], gt[:], xt[:], ALU.add), ins=[gtb, xtb], outs=[otb])
                ctx.dma("sync", x1_dram.ap[rows, cols], ot[:], otb,
                        ins=[otb], outs=[x1_dram])


def emit_gather_fused(ctx, B, cb_dram, idx_dram, rgu, rd, agu, agd,
                      agst, agu_bufs, agd_bufs):
    nc = ctx.nc
    es = ctx.es
    idxt = B.sbuf("gfidx", [128, 2, 1024], I32)
    d0 = B.sbuf("gfd0", [128, 128, 8], F32)
    d1 = B.sbuf("gfd1", [128, 128, 8], F32)
    slots = [d0, d1]
    sg = [Stream(es, nc, "gfg0"), Stream(es, nc, "gfg1")]
    ss = [Stream(es, nc, "gfs0"), Stream(es, nc, "gfs1")]
    si = Stream(es, nc, "gfi")
    cbsrc = cb_dram.ap
    nchunks = idx_dram.ap.shape[1] // 1024
    state = {"call": 0, "chunk": -1, "seg": 0, "si_waited": 0, "loaded": 0}

    def load_chunk(cc):
        def fl(e, cc=cc):
            e.dma_start(out=idxt[:, cc % 2, :],
                        in_=idx_dram.ap[:, cc * 1024:cc * 1024 + 1024]
                        ).then_inc(si.sem, 16)
        ctx.emit("gpsimd", fl)
        si.n += 16
        state["loaded"] = cc + 1

    load_chunk(0)
    load_chunk(1)

    def idx_col(k, npart):
        cc = k // 1024
        if cc != state["chunk"]:
            state["chunk"] = cc
            if state["loaded"] < nchunks and state["loaded"] <= cc + 1:
                load_chunk(state["loaded"])
        return (idxt[0:npart, cc % 2, (k % 1024):(k % 1024) + 1],
                16 * (cc + 1))

    def do_segment(ncalls, npart, dst_ap):
        sl = state["seg"] % 2
        state["seg"] += 1
        dstb = slots[sl]
        base = state["call"]
        for s in range(ncalls):
            icol, sival = idx_col(base + s, npart)
            waits = []
            if sival > state["si_waited"]:
                waits.append((si.sem, sival))
                state["si_waited"] = sival
            if s == 0 and ss[sl].n > 0:
                waits.append((ss[sl].sem, ss[sl].n))

            def fg(e, icol=icol, dst=dstb[0:npart, s, :], waits=tuple(waits),
                   sl=sl):
                for sem, v in waits:
                    e.wait_ge(sem, v)
                e.indirect_dma_start(
                    dst, None, cbsrc,
                    bass.IndirectOffsetOnAxis(icol, 0)).then_inc(
                        sg[sl].sem, 16)
            ctx.emit("gpsimd", fg)
            sg[sl].n += 16
        state["call"] += ncalls

        def fs(e, dst_ap=dst_ap, src=(dstb[:, 0:ncalls, :] if npart == 128
                                      else dstb[0:npart, :, :]),
               gv=sg[sl].n, sl=sl):
            e.wait_ge(sg[sl].sem, gv)
            e.dma_start(out=dst_ap, in_=src).then_inc(ss[sl].sem, 16)
        ctx.emit("gpsimd", fs)
        ss[sl].n += 16

    def emit_ag(in2d, out2d, out_buf):
        w = [(ss[0].sem, ss[0].n), (ss[1].sem, ss[1].n)]
        agst.n += 1

        def fc(e, in2d=in2d, out2d=out2d, w=tuple(w)):
            for sem, v in w:
                e.wait_ge(sem, v)
            e.collective_compute(
                "AllGather", ALU.bypass,
                replica_groups=[list(range(8))],
                ins=[in2d.opt()], outs=[out2d.opt()]).then_inc(agst.sem, 1)
        ctx.emit("gpsimd", fc)
        out_buf.writer = (agst, agst.n)
        out_buf.readers = {}

    for cdc in range(11):
        cnt = CNTS[cdc]
        for m in range(2):
            for b in range(4):
                do_segment(cnt, 128, rgu.ap[cdc, m, b, :, 0:cnt, :])
        agu_h = agu[0] if cdc < 6 else agu[1]
        emit_ag(rgu.ap[cdc].rearrange("m b c s j -> (m b c) (s j)"),
                agu_h.ap[cdc if cdc < 6 else cdc - 6].rearrange(
                    "r u c s j -> (r u c) (s j)"),
                agu_bufs[cdc])
    for dci in range(4):
        for cdc in range(11):
            cnt = CNTS[cdc]
            do_segment(128, cnt, rd.ap[dci, cdc, 0:cnt, :, :])
        emit_ag(rd.ap[dci].rearrange("c p s j -> (c p) (s j)"),
                agd.ap[dci].rearrange("r c p s j -> (r c p) (s j)"),
                agd_bufs[dci])


def host_idxcalls_f(idx_gate, idx_up, idx_down, core):
    cols = []
    jj = core
    for cdc in range(11):
        cnt = CNTS[cdc]
        for idx_m in (idx_gate, idx_up):
            for b in range(4):
                for s in range(cnt):
                    row = 8 * (cdc * 128 + s) + jj
                    cols.append(idx_m[row, 128 * b:128 * b + 128])
    for dci in range(4):
        dc = 4 * core + dci
        for cdc in range(11):
            cnt = CNTS[cdc]
            for s in range(128):
                d = dc * 128 + s
                col = np.zeros(128, dtype=np.int32)
                col[0:cnt] = idx_down[d, 128 * cdc:128 * cdc + cnt]
                cols.append(col)
    arr = np.stack(cols, axis=1).astype(np.int32)
    pad = (-arr.shape[1]) % 2048
    if pad:
        arr = np.concatenate([arr, np.zeros((128, pad), np.int32)], axis=1)
    return np.ascontiguousarray(arr)


def emit_ffn_up_raw(ctx, B, agu, agu_bufs, h_dram):
    sb, bf = B.sb, B.buf
    xr = sb["xr"]
    for sl in range(2):
        B.sbuf(f"u5r{sl}", [128, 128, 8], F32)
        B.sbuf(f"u5w{sl}", [128, 8, 128], BF16)
        B.sbuf(f"f5h{sl}", [128, T], BF16)
    B.sbuf("f5a", [128, T], F32)
    B.bufonly("ps5g")
    B.bufonly("ps5u")
    unit = 0
    nchunk = 0
    for cdc in range(11):
        for jj in range(8):
            iic = jj * 11 + cdc
            cnt = CNTS[cdc]
            hsl = nchunk % 2
            nchunk += 1
            for m in range(2):
                ps = B.ps_ap[:, 2048 * m:2048 * m + 2048]
                psb = bf["ps5g"] if m == 0 else bf["ps5u"]
                for b in range(4):
                    sl = unit % 2
                    unit += 1
                    fr, frb = sb[f"u5r{sl}"], bf[f"u5r{sl}"]
                    w2, w2b = sb[f"u5w{sl}"], bf[f"u5w{sl}"]
                    agu_h = agu[0] if cdc < 6 else agu[1]
                    ctx.dma("sync", fr[:, 0:cnt, :],
                            agu_h.ap[cdc if cdc < 6 else cdc - 6, jj,
                                     m * 4 + b, :, 0:cnt, :], frb,
                            ins=[agu_bufs[cdc]], outs=[frb])
                    ctx.op("vector", lambda e, fr=fr, w2=w2, cnt=cnt:
                           e.tensor_copy(w2[:, :, 0:cnt],
                                         fr[:, 0:cnt, :].rearrange(
                                             "c s j -> c j s")),
                           ins=[frb], outs=[w2b])
                    for j in range(8):
                        kk = j * 4 + b
                        mm_split(ctx, ps[0:cnt, :], w2[:, j, 0:cnt],
                                 xr[:, kk, :], ins=[w2b, bf["xr"]], outs=[psb],
                                 start=(b == 0 and j == 0),
                                 stop=(b == 3 and j == 7))
            act, actb = sb["f5a"], bf["f5a"]
            hb, hbb = sb[f"f5h{hsl}"], bf[f"f5h{hsl}"]
            ctx.op("scalar", lambda e, act=act, cnt=cnt: e.activation(
                act[0:cnt, :], B.ps_ap[0:cnt, 0:2048], AF.Silu),
                ins=[bf["ps5g"]], outs=[actb])
            ctx.op("vector", lambda e, act=act, hb=hb, cnt=cnt: e.tensor_tensor(
                hb[0:cnt, :], act[0:cnt, :], B.ps_ap[0:cnt, 2048:4096],
                ALU.mult), ins=[actb, bf["ps5u"]], outs=[hbb])
            ctx.dma("sync", h_dram.ap[iic, 0:cnt, :], hb[0:cnt, :], hbb,
                    ins=[hbb], outs=[h_dram])


def make_dcast(ctx, B, agd, agd_bufs, sdF, dc_waits):
    sb, bf = B.sb, B.buf
    for sl in range(2):
        B.sbuf(f"dc6r{sl}", [128, 128, 8], F32)
        B.sbuf(f"dc6w{sl}", [128, 8, 128], BF16)
    B.bufonly("dc6st")
    state = {"u": 0}
    stb = bf["dc6st"]

    def group(dci):
        for rank in range(8):
            dc = 4 * rank + dci
            for cdc in range(11):
                cnt = CNTS[cdc]
                sl = state["u"] % 2
                state["u"] += 1
                fr, frb = sb[f"dc6r{sl}"], bf[f"dc6r{sl}"]
                w2, w2b = sb[f"dc6w{sl}"], bf[f"dc6w{sl}"]
                ctx.dma("sync", fr[0:cnt, :, :],
                        agd.ap[dci, rank, cdc, 0:cnt, :, :], frb,
                        ins=[agd_bufs[dci]], outs=[frb])
                ctx.op("vector", lambda e, fr=fr, w2=w2, cnt=cnt:
                       e.tensor_copy(w2[0:cnt, :, :],
                                     fr[0:cnt, :, :].rearrange(
                                         "c d j -> c j d")),
                       ins=[frb], outs=[w2b])
                ctx.dma("sync", sdF.ap[dc, 0:cnt, :, cdc, :],
                        w2[0:cnt, :, :], stb, ins=[w2b], outs=[sdF])
            st = ctx.dstream_for(stb)
            dc_waits[dc] = [(st, st.n)]
    return group


def make_ffn_down(ctx, B, sd_dram, h_dram, x_dram, s0_dram, x1_dram,
                  dc_waits):
    sb, bf = B.sb, B.buf
    hq = B.sbuf("f6h", [128, 88, 512], BF16)
    for sl in range(2):
        B.sbuf(f"f6w{sl}", [128, 88, 128], BF16)
        B.sbuf(f"f6g{sl}", [128, 512], F32)
        B.bufonly(f"ps6y{sl}")
        B.bufonly(f"ps6t{sl}")
    for sl in range(8):
        B.sbuf(f"f6x{sl}", [128, 128], F32)
        B.sbuf(f"f6s{sl}", [128, 128], F32)
        B.sbuf(f"f6o{sl}", [128, 128], F32)
        B.sbuf(f"f6t{sl}", [128, 128], F32)
    state = {"tq": -1, "u": 0}

    def part(tq, dcs):
        if tq != state["tq"]:
            ctx.dma("sync", hq[:, :, :],
                    h_dram.ap[:, :, tq * 512:(tq + 1) * 512].rearrange(
                        "i p t -> p i t"), bf["f6h"],
                    ins=[h_dram], outs=[bf["f6h"]])
            state["tq"] = tq
        for dc in dcs:
            u = state["u"]
            state["u"] += 1
            sl = u % 2
            w, wb = sb[f"f6w{sl}"], bf[f"f6w{sl}"]
            src_ap = sd_dram.ap[dc].rearrange("p j c d -> p (j c) d")
            ew = tuple(dc_waits.get(dc, ()))
            ctx.dma("scalar", w[:], src_ap, wb, ins=[], outs=[wb],
                    extra_waits=ew)
            ps = B.ps_ap[:, 512 * sl:512 * sl + 512]
            psb = bf[f"ps6y{sl}"]
            for jj in range(8):
                for cdc in range(11):
                    i88 = jj * 11 + cdc
                    cnt = CNTS[cdc]
                    ctx.op("tensor", lambda e, w=w, i88=i88, cnt=cnt, ps=ps:
                           e.matmul(ps, w[0:cnt, i88, :], hq[0:cnt, i88, :],
                                    start=(i88 == 0), stop=(i88 == 87)),
                           ins=[wb, bf["f6h"]], outs=[psb])
            g0, g0b = sb[f"f6g{sl}"], bf[f"f6g{sl}"]
            ctx.op("scalar", lambda e, g0=g0, ps=ps: e.activation(
                g0[:], ps, AF.Silu), ins=[psb], outs=[g0b])
            for ts in range(4):
                sl8 = (u * 4 + ts) % 8
                pst = B.ps_ap[:, 1024 + 512 * sl:1024 + 512 * sl + 128]
                pstb = bf[f"ps6t{sl}"]
                gt, gtb = sb[f"f6t{sl8}"], bf[f"f6t{sl8}"]
                xt, xtb = sb[f"f6x{sl8}"], bf[f"f6x{sl8}"]
                st, stb = sb[f"f6s{sl8}"], bf[f"f6s{sl8}"]
                ot, otb = sb[f"f6o{sl8}"], bf[f"f6o{sl8}"]
                rows = slice((tq * 4 + ts) * 128, (tq * 4 + ts) * 128 + 128)
                cols = slice(dc * 128, dc * 128 + 128)
                ctx.op("tensor", lambda e, g0=g0, ts=ts, pst=pst: e.transpose(
                    pst, g0[:, ts * 128:(ts + 1) * 128], sb["identf"][:]),
                    ins=[g0b, bf["identf"]], outs=[pstb])
                ctx.op("vector", lambda e, gt=gt, pst=pst: e.tensor_copy(
                    gt[:], pst), ins=[pstb], outs=[gtb])
                ctx.dma("sync", xt[:], x_dram.ap[rows, cols], xtb,
                        ins=[x_dram], outs=[xtb])
                ctx.dma("scalar", st[:], s0_dram.ap[rows, cols], stb,
                        ins=[s0_dram], outs=[stb])
                ctx.op("vector", lambda e, gt=gt, st=st: e.tensor_tensor(
                    gt[:], gt[:], st[:], ALU.mult), ins=[gtb, stb], outs=[gtb])
                ctx.op("vector", lambda e, gt=gt, xt=xt, ot=ot: e.tensor_tensor(
                    ot[:], gt[:], xt[:], ALU.add), ins=[gtb, xtb], outs=[otb])
                ctx.dma("sync", x1_dram.ap[rows, cols], ot[:], otb,
                        ins=[otb], outs=[x1_dram])
    return part



def make_cast_bg(ctx, B, pairs, shared="cg"):
    for sl in range(2):
        B.sbuf(f"{shared}f{sl}", [128, 1024], F32)
        B.sbuf(f"{shared}b{sl}", [128, 1024], BF16)
    units = []
    for src, dst in pairs:
        n, prt, mcols = src.ap.shape
        for i in range(n):
            for c0 in range(0, mcols, 1024):
                units.append((src, dst, i, c0, min(1024, mcols - c0), prt))
    state = {"k": 0}

    def bg():
        k = state["k"]
        if k >= len(units):
            return False
        state["k"] = k + 1
        src, dst, i, c0, w, prt = units[k]
        sl = k % 2
        f, fb = B.sb[f"{shared}f{sl}"], B.buf[f"{shared}f{sl}"]
        b, bb = B.sb[f"{shared}b{sl}"], B.buf[f"{shared}b{sl}"]
        ctx.dma("sync", f[0:prt, 0:w], src.ap[i, :, c0:c0 + w], fb,
                ins=[src], outs=[fb])
        ctx.op("vector", lambda e, f=f, b=b, prt=prt, w=w:
               e.tensor_copy(b[0:prt, 0:w], f[0:prt, 0:w]),
               ins=[fb], outs=[bb])
        ctx.dma("sync", dst.ap[i, :, c0:c0 + w], b[0:prt, 0:w], bb,
                ins=[bb], outs=[dst])
        return True
    return bg


def emit_cast_bf16(ctx, B, src_dram, dst_dram, name, shared="cast"):
    n, prt, mcols = src_dram.ap.shape
    if f"{shared}f0" not in B.sb:
        for sl in range(2):
            B.sbuf(f"{shared}f{sl}", [128, 4096], F32)
            B.sbuf(f"{shared}b{sl}", [128, 4096], BF16)
    for i in range(n):
        sl = i % 2
        f, fb = B.sb[f"{shared}f{sl}"], B.buf[f"{shared}f{sl}"]
        b, bb = B.sb[f"{shared}b{sl}"], B.buf[f"{shared}b{sl}"]
        ctx.dma("sync", f[0:prt, 0:mcols], src_dram.ap[i], fb,
                ins=[src_dram], outs=[fb])
        ctx.op("vector", lambda e, f=f, b=b, prt=prt, mcols=mcols:
               e.tensor_copy(b[0:prt, 0:mcols], f[0:prt, 0:mcols]),
               ins=[fb], outs=[bb])
        ctx.dma("sync", dst_dram.ap[i], b[0:prt, 0:mcols], bb,
                ins=[bb], outs=[dst_dram])


HT = 1024


def emit_attn_pre(ctx, B, W, xn1_dram):
    sb, bf = B.sb, B.buf
    qhat, khat, vtm = sb["qhat"], sb["khat"], sb["vtm"]
    halo = B.sbuf("a8halo", [128, 12, 2], BF16)
    halo2 = B.sbuf("a8halo2", [128, 10, 2], BF16)
    vsh = B.sbuf("a8vsh", [128, HT], BF16)
    for nm in ("p8a0", "p8a1", "p8b", "p8c", "p8t0", "p8t1"):
        B.bufonly(nm)
    swapm = B.sbuf("swapm_s", [128, 128], F32)
    ctx.dma("sync", swapm[:], W["swapm"].ap[:, :], bf["swapm_s"],
            ins=[W["swapm"]], outs=[bf["swapm_s"]])
    qkv = B.sbuf("a8qkv", [128, 2, HT], BF16)
    qp1 = B.sbuf("a8qp1", [128, 8, HT + 2], BF16)
    kp1 = B.sbuf("a8kp1", [128, 2, HT + 2], BF16)
    qp2 = B.sbuf("a8qp2", [128, 8, HT + 2], BF16)
    kp2 = B.sbuf("a8kp2", [128, 2, HT + 2], BF16)
    kmean = B.sbuf("a8km", [128, HT], BF16)
    qmean = B.sbuf("a8qm", [128, HT], BF16)
    qf = B.sbuf("a8qf", [128, 8, HT], BF16)
    kf = B.sbuf("a8kf", [128, 2, HT], BF16)
    rsq = B.sbuf("a8rsq", [128, HT], F32)
    rsk = B.sbuf("a8rsk", [128, HT], F32)
    rbc = B.sbuf("a8rbc", [128, HT], F32)
    tA = B.sbuf("a8tA", [128, HT], F32)
    tB = B.sbuf("a8tB", [128, HT], F32)
    tC = B.sbuf("a8tC", [128, HT], F32)
    cosf = B.sbuf("a8cos", [128, HT], F32)
    sinf = B.sbuf("a8sin", [128, HT], F32)
    ktl = B.sbuf("a8kt", [128, 1], F32)
    qc1b = B.sbuf("a8q1b", [128, 8], F32)
    qc2b = B.sbuf("a8q2b", [128, 8], F32)
    kc1b = B.sbuf("a8k1b", [128, 2], F32)
    kc2b = B.sbuf("a8k2b", [128, 2], F32)
    mkp = B.sbuf("a8mk", [128, 2, 64], BF16)
    mqp = B.sbuf("a8mq", [128, 8, 64], BF16)
    bqp = B.sbuf("a8bq", [128, 8, 128], BF16)
    bkp = B.sbuf("a8bk", [128, 2, 128], BF16)
    sqp = B.sbuf("a8sq", [128, 8, 16], F32)
    skp = B.sbuf("a8sk", [128, 2, 4], F32)
    rsbq = B.sbuf("a8rq", [128, 8, 128], F32)
    rsbk = B.sbuf("a8rk", [128, 2, 128], F32)
    wt = [B.sbuf(f"a8w{sl}", [128, 128], BF16) for sl in range(4)]
    wq2 = [B.sbuf(f"a8wq{sl}", [128, 32, 128], BF16) for sl in range(2)]
    wtf = [B.sbuf(f"a8wf{sl}", [128, 128], F32) for sl in range(2)]
    xt_ = [B.sbuf(f"a8x{sl}", [128, HT], BF16) for sl in range(4)]
    for nm, src in [("a8kt", "ktemp"), ("a8q1b", "qc1b"), ("a8q2b", "qc2b"),
                    ("a8k1b", "kc1b"), ("a8k2b", "kc2b")]:
        ctx.dma("sync", sb[nm][:], W[src].ap[:, :], bf[nm],
                ins=[W[src]], outs=[bf[nm]])
    for nm, src in [("a8mk", "mk"), ("a8mq", "mq"), ("a8bq", "bq"),
                    ("a8bk", "bk"), ("a8sq", "sq"), ("a8sk", "sk"),
                    ("a8rq", "rsbq"), ("a8rk", "rsbk")]:
        np_ = W[src].ap.shape[1]
        ctx.dma("sync", sb[nm][0:np_, :, :],
                W[src].ap.rearrange("a p b -> p a b"),
                bf[nm], ins=[W[src]], outs=[bf[nm]])

    def PS(idx, width=HT):
        return B.ps_ap[:, 1024 * idx:1024 * idx + width]

    for hf in range(2):
        t0 = hf * HT
        ctx.dma("sync", cosf[:, :], W["cosf"].ap[:, t0:t0 + HT],
                bf["a8cos"], ins=[W["cosf"]], outs=[bf["a8cos"]])
        ctx.dma("sync", sinf[:, :], W["sinf"].ap[:, t0:t0 + HT],
                bf["a8sin"], ins=[W["sinf"]], outs=[bf["a8sin"]])
        if hf == 1:
            ctx.op("vector", lambda e: e.tensor_copy(
                halo[:, 0:8, :], qp1[:, :, HT:HT + 2]), ins=[bf["a8qp1"]],
                outs=[bf["a8halo"]])
            ctx.op("vector", lambda e: e.tensor_copy(
                halo[:, 8:10, :], kp1[:, :, HT:HT + 2]), ins=[bf["a8kp1"]],
                outs=[bf["a8halo"]])
            ctx.op("vector", lambda e: e.tensor_copy(
                halo[:, 10:12, :], qkv[:, 0:2, HT - 2:HT]),
                ins=[bf["a8qkv"]], outs=[bf["a8halo"]])
        for nm in ("a8qp1", "a8kp1", "a8qp2", "a8kp2"):
            if hf == 0:
                ctx.op("vector", lambda e, nm=nm: e.memset(
                    sb[nm][:, :, 0:2], 0.0), ins=[], outs=[bf[nm]])
        if hf == 1:
            ctx.op("vector", lambda e: e.tensor_copy(
                halo2[:, 0:8, :], qp2[:, :, HT:HT + 2]),
                ins=[bf["a8qp2"]], outs=[bf["a8halo2"]])
            ctx.op("vector", lambda e: e.tensor_copy(
                halo2[:, 8:10, :], kp2[:, :, HT:HT + 2]),
                ins=[bf["a8kp2"]], outs=[bf["a8halo2"]])
            ctx.op("vector", lambda e: e.tensor_copy(
                qp1[:, :, 0:2], halo[:, 0:8, :]),
                ins=[bf["a8halo"]], outs=[bf["a8qp1"]])
            ctx.op("vector", lambda e: e.tensor_copy(
                kp1[:, :, 0:2], halo[:, 8:10, :]),
                ins=[bf["a8halo"]], outs=[bf["a8kp1"]])
            ctx.op("vector", lambda e: e.tensor_copy(
                qp2[:, :, 0:2], halo2[:, 0:8, :]),
                ins=[bf["a8halo2"]], outs=[bf["a8qp2"]])
            ctx.op("vector", lambda e: e.tensor_copy(
                kp2[:, :, 0:2], halo2[:, 8:10, :]),
                ins=[bf["a8halo2"]], outs=[bf["a8kp2"]])
        for oc in range(12):
            sl = oc % 2
            ps, psb = PS(sl), bf[f"p8a{sl}"]
            wq, wqb = sb[f"a8wq{sl}"], bf[f"a8wq{sl}"]
            ctx.dma(["sync", "scalar"][oc % 2], wq[:],
                    W["wqkvT_b"].ap[:, :, oc * 128:(oc + 1) * 128].rearrange(
                        "k p c -> p k c"), wqb,
                    ins=[W["wqkvT_b"]], outs=[wqb])
            for kk in range(32):
                xsl = (oc * 32 + kk) % 4
                x_, xb = xt_[xsl], bf[f"a8x{xsl}"]
                ctx.dma(["sync", "scalar", "gpsimd"][kk % 3], x_[:],
                        xn1_dram.ap[kk, :, t0:t0 + HT], xb,
                        ins=[xn1_dram], outs=[xb])
                mm_split(ctx, ps, wq[:, kk, :], x_[:], ins=[wqb, xb],
                         outs=[psb], start=(kk == 0), stop=(kk == 31))
            if oc < 8:
                dst, db = qp1[:, oc, 2:2 + HT], bf["a8qp1"]
            elif oc < 10:
                dst, db = kp1[:, oc - 8, 2:2 + HT], bf["a8kp1"]
            else:
                dst, db = qkv[:, oc - 10, :], bf["a8qkv"]
            ctx.op("vector", lambda e, dst=dst, ps=ps: e.tensor_copy(dst, ps),
                   ins=[psb], outs=[db])
        for c2 in range(2):
            mm_split(ctx, PS(2)[0:64, :], mkp[:, c2, :], kp1[:, c2, 2:2 + HT],
                     ins=[bf["a8mk"], bf["a8kp1"]], outs=[bf["p8b"]],
                     start=(c2 == 0), stop=(c2 == 1))
        ctx.op("vector", lambda e: e.tensor_copy(kmean[0:64, :], PS(2)[0:64, :]),
               ins=[bf["p8b"]], outs=[bf["a8km"]])
        for c8 in range(8):
            mm_split(ctx, PS(2)[0:64, :], mqp[:, c8, :], qp1[:, c8, 2:2 + HT],
                     ins=[bf["a8mq"], bf["a8qp1"]], outs=[bf["p8b"]],
                     start=(c8 == 0), stop=(c8 == 7))
        ctx.op("vector", lambda e: e.tensor_copy(qmean[0:64, :], PS(2)[0:64, :]),
               ins=[bf["p8b"]], outs=[bf["a8qm"]])
        for oc in range(8):
            sl = oc % 2
            ps, psb = PS(sl), bf[f"p8a{sl}"]
            for tap in range(3):
                wsl = (oc * 3 + tap) % 4
                w, wb = wt[wsl], bf[f"a8w{wsl}"]
                ctx.dma("sync", w[:], W["qc1bd_b"].ap[oc * 3 + tap], wb,
                        ins=[W["qc1bd_b"]], outs=[wb])
                mm_split(ctx, ps, w[:], qp1[:, oc, tap:tap + HT],
                         ins=[wb, bf["a8qp1"]], outs=[psb],
                         start=(tap == 0), stop=(tap == 2))
            ctx.op("scalar", lambda e, oc=oc, ps=ps: e.activation(
                qp2[:, oc, 2:2 + HT], ps, AF.Identity, bias=qc1b[:, oc:oc + 1]),
                ins=[psb, bf["a8q1b"]], outs=[bf["a8qp2"]])
        for oc in range(2):
            sl = oc % 2
            ps, psb = PS(sl), bf[f"p8a{sl}"]
            for tap in range(3):
                wsl = (oc * 3 + tap) % 4
                w, wb = wt[wsl], bf[f"a8w{wsl}"]
                ctx.dma("sync", w[:], W["kc1bd_b"].ap[oc * 3 + tap], wb,
                        ins=[W["kc1bd_b"]], outs=[wb])
                mm_split(ctx, ps, w[:], kp1[:, oc, tap:tap + HT],
                         ins=[wb, bf["a8kp1"]], outs=[psb],
                         start=(tap == 0), stop=(tap == 2))
            ctx.op("scalar", lambda e, oc=oc, ps=ps: e.activation(
                kp2[:, oc, 2:2 + HT], ps, AF.Identity, bias=kc1b[:, oc:oc + 1]),
                ins=[psb, bf["a8k1b"]], outs=[bf["a8kp2"]])
        for oc in range(8):
            sl = oc % 2
            ps, psb = PS(sl), bf[f"p8a{sl}"]
            wq, wqb = sb[f"a8wq{sl}"], bf[f"a8wq{sl}"]
            ctx.dma(["sync", "scalar"][oc % 2], wq[:, 0:24, :],
                    W["qc2T_b"].ap[:, :, oc * 128:(oc + 1) * 128].rearrange(
                        "k p c -> p k c"), wqb,
                    ins=[W["qc2T_b"]], outs=[wqb])
            first = True
            for tap in range(3):
                for ic in range(8):
                    mm_split(ctx, ps, wq[:, tap * 8 + ic, :],
                             qp2[:, ic, tap:tap + HT],
                             ins=[wqb, bf["a8qp2"]], outs=[psb],
                             start=first, stop=False)
                    first = False
            mm_split(ctx, ps, bqp[0:64, oc, :], kmean[0:64, :],
                     ins=[bf["a8bq"], bf["a8km"]], outs=[psb],
                     start=False, stop=True)
            ctx.op("scalar", lambda e, oc=oc, ps=ps: e.activation(
                qf[:, oc, :], ps, AF.Identity, bias=qc2b[:, oc:oc + 1]),
                ins=[psb, bf["a8q2b"]], outs=[bf["a8qf"]])
        for oc in range(2):
            sl = oc % 2
            ps, psb = PS(sl), bf[f"p8a{sl}"]
            first = True
            for tap in range(3):
                for ic in range(2):
                    wsl = (tap * 2 + ic) % 4
                    w, wb = wt[wsl], bf[f"a8w{wsl}"]
                    ctx.dma("sync", w[:],
                            W["kc2T"].ap[tap * 2 + ic, :,
                                         oc * 128:(oc + 1) * 128], wb,
                            ins=[W["kc2T"]], outs=[wb])
                    mm_split(ctx, ps, w[:], kp2[:, ic, tap:tap + HT],
                             ins=[wb, bf["a8kp2"]], outs=[psb],
                             start=first, stop=False)
                    first = False
            mm_split(ctx, ps, bkp[0:64, oc, :], qmean[0:64, :],
                     ins=[bf["a8bk"], bf["a8qm"]], outs=[psb],
                     start=False, stop=True)
            ctx.op("scalar", lambda e, oc=oc, ps=ps: e.activation(
                kf[:, oc, :], ps, AF.Identity, bias=kc2b[:, oc:oc + 1]),
                ins=[psb, bf["a8k2b"]], outs=[bf["a8kf"]])
        for (nch, src, srcb, pat, patb, dstn, nr) in [
                (8, qf, bf["a8qf"], sqp, bf["a8sq"], "a8rsq", 16),
                (2, kf, bf["a8kf"], skp, bf["a8sk"], "a8rsk", 4)]:
            for oc in range(nch):
                ctx.op("scalar", lambda e, src=src, oc=oc: e.activation(
                    tA[:], src[:, oc, :], AF.Square), ins=[srcb],
                    outs=[bf["a8tA"]])
                mm_split(ctx, PS(2)[0:nr, :], pat[:, oc, 0:nr], tA[:],
                         ins=[bf["a8tA"], patb], outs=[bf["p8b"]],
                         start=(oc == 0), stop=(oc == nch - 1))
            dst, dstb = sb[dstn], bf[dstn]
            ctx.op("scalar", lambda e, dst=dst, nr=nr: e.activation(
                dst[0:nr, :], PS(2)[0:nr, :], AF.Sqrt),
                ins=[bf["p8b"]], outs=[dstb])
            ctx.op("vector", lambda e, dst=dst, nr=nr: e.tensor_scalar(
                dst[0:nr, :], dst[0:nr, :], 1e-12, None, ALU.max),
                ins=[dstb], outs=[dstb])
            ctx.op("vector", lambda e, dst=dst, nr=nr: e.reciprocal(
                dst[0:nr, :], dst[0:nr, :]), ins=[dstb], outs=[dstb])
        ctx.op("vector", lambda e: e.tensor_scalar(
            rsk[0:4, :], rsk[0:4, :], ktl[0:4, :], None, ALU.mult),
            ins=[bf["a8rsk"], bf["a8kt"]], outs=[bf["a8rsk"]])
        for (nch, srcf, srcb, rpat, rpatb, rsn, nr, dsth, dsthb) in [
                (8, qf, bf["a8qf"], rsbq, bf["a8rq"], "a8rsq", 16, qhat,
                 bf["qhat"]),
                (2, kf, bf["a8kf"], rsbk, bf["a8rk"], "a8rsk", 4, khat,
                 bf["khat"])]:
            rs, rsb2 = sb[rsn], bf[rsn]
            for oc in range(nch):
                mm_split(ctx, PS(2)[:, :], rpat[0:nr, oc, :], rs[0:nr, :],
                         ins=[rpatb, rsb2], outs=[bf["p8b"]],
                         start=True, stop=True)
                ctx.op("vector", lambda e: e.tensor_copy(rbc[:], PS(2)[:, :]),
                       ins=[bf["p8b"]], outs=[bf["a8rbc"]])
                ctx.op("vector", lambda e, srcf=srcf, oc=oc: e.tensor_tensor(
                    tC[:], srcf[:, oc, :], rbc[:], ALU.mult),
                    ins=[srcb, bf["a8rbc"]], outs=[bf["a8tC"]])
                mm_split(ctx, PS(3)[:, :], sb["swapm_s"][:], tC[:],
                         ins=[bf["a8tC"], bf["swapm_s"]], outs=[bf["p8c"]],
                         start=True, stop=True)
                ctx.op("vector", lambda e: e.tensor_tensor(
                    tA[:], tC[:], cosf[:], ALU.mult),
                    ins=[bf["a8tC"], bf["a8cos"]], outs=[bf["a8tA"]])
                ctx.op("vector", lambda e: e.tensor_tensor(
                    tB[:], PS(3)[:, :], sinf[:], ALU.mult),
                    ins=[bf["p8c"], bf["a8sin"]], outs=[bf["a8tB"]])
                ctx.op("vector", lambda e, dsth=dsth, oc=oc, t0=t0:
                       e.tensor_tensor(dsth[:, oc, t0:t0 + HT], tA[:], tB[:],
                                       ALU.add),
                       ins=[bf["a8tA"], bf["a8tB"]], outs=[dsthb])
                if "dbg_qh3" in W and nch == 8 and hf == 0 and oc == 0:
                    ctx.dma("sync", W["dbg_qh3"].ap[:, 0:HT],
                            qhat[:, 0, 0:HT],
                            dsthb, ins=[dsthb], outs=[W["dbg_qh3"]])
                if "dbg_tc" in W and nch == 8 and oc == 0:
                    ctx.dma("sync", W["dbg_tc"].ap[hf, 0], tC[:],
                            bf["a8tC"], ins=[bf["a8tC"]], outs=[W["dbg_tc"]])
                    ctx.dma("sync", W["dbg_tc"].ap[hf, 1], tA[:],
                            bf["a8tA"], ins=[bf["a8tA"]], outs=[W["dbg_tc"]])
                    ctx.dma("sync", W["dbg_tc"].ap[hf, 2], tB[:],
                            bf["a8tB"], ins=[bf["a8tB"]], outs=[W["dbg_tc"]])
                    ctx.dma("sync", W["dbg_tc"].ap[hf, 3], rbc[:],
                            bf["a8rbc"], ins=[bf["a8rbc"]], outs=[W["dbg_tc"]])
        if "dbg_qh2" in W and hf == 1:
            ctx.dma("sync", W["dbg_qh2"].ap[0], qhat[:, :, :],
                    bf["qhat"], ins=[bf["qhat"]], outs=[W["dbg_qh2"]])
        if "dbg_qp1" in W:
            ctx.dma("sync", W["dbg_qp1"].ap[hf], qp1[:, :, :],
                    bf["a8qp1"], ins=[bf["a8qp1"]], outs=[W["dbg_qp1"]])
            ctx.dma("sync", W["dbg_qp2"].ap[hf], qp2[:, :, :],
                    bf["a8qp2"], ins=[bf["a8qp2"]], outs=[W["dbg_qp2"]])
            ctx.dma("sync", W["dbg_qf"].ap[hf], qf[:, :, :],
                    bf["a8qf"], ins=[bf["a8qf"]], outs=[W["dbg_qf"]])
            ctx.dma("sync", W["dbg_rsq"].ap[hf], rsq[:, :],
                    bf["a8rsq"], ins=[bf["a8rsq"]], outs=[W["dbg_rsq"]])
            ctx.dma("sync", W["dbg_km"].ap[hf], kmean[:, :],
                    bf["a8km"], ins=[bf["a8km"]], outs=[W["dbg_km"]])
        for kvh in range(4):
            base = 64 * (kvh % 2)
            kc = kvh // 2
            mm_split(ctx, PS(2)[:, :],
                     sb["dup64"][base:base + 64, :],
                     khat[base:base + 64, kc, t0:t0 + HT],
                     ins=[bf["khat"], bf["dup64"]], outs=[bf["p8b"]],
                     start=True, stop=True)
            ctx.op("vector", lambda e, kvh=kvh, t0=t0: e.tensor_copy(
                sb["khat4"][:, kvh, t0:t0 + HT], PS(2)[:, :]),
                ins=[bf["p8b"]], outs=[bf["khat4"]])
        if hf == 0:
            ctx.op("vector", lambda e: e.memset(vsh[:, 0:1], 0.0),
                   ins=[], outs=[bf["a8vsh"]])
        else:
            ctx.op("vector", lambda e: e.tensor_copy(
                vsh[:, 0:1], halo[:, 11, 1:2]), ins=[bf["a8halo"]],
                outs=[bf["a8vsh"]])
        ctx.op("vector", lambda e: e.tensor_copy(
            vsh[:, 1:HT], qkv[:, 1, 0:HT - 1]), ins=[bf["a8qkv"]],
            outs=[bf["a8vsh"]])
        for kv in range(4):
            base = 64 * (kv % 2)
            if kv < 2:
                def srcap(tt, kv=kv):
                    return qkv[64 * kv:64 * kv + 64, 0,
                               tt * 128:(tt + 1) * 128]
                sb_in = bf["a8qkv"]
            else:
                def srcap(tt, kv=kv):
                    return vsh[64 * (kv - 2):64 * (kv - 2) + 64,
                               tt * 128:(tt + 1) * 128]
                sb_in = bf["a8vsh"]
            for tt in range(HT // 128):
                psl = tt % 2
                pst = B.ps_ap[:, 3072 + 512 * psl:3072 + 512 * psl + 32]\
                    .bitcast(BF16)[:, 0:64]
                pstb = bf[f"p8t{psl}"]
                ctx.op("tensor", lambda e, srcap=srcap, tt=tt, pst=pst,
                       base=base:
                       e.transpose(pst, srcap(tt),
                                   sb["dup64"][base:base + 64, 0:64]),
                       ins=[sb_in, bf["dup64"], bf["p8c"]], outs=[pstb])
                gtt = t0 // 128 + tt
                ctx.op("vector", lambda e, kv=kv, gtt=gtt, pst=pst:
                       e.tensor_copy(vtm[:, kv, gtt, :], pst),
                       ins=[pstb], outs=[bf["vtm"]])


def emit_attn(ctx, B, W, s1_dram):
    sb, bf = B.sb, B.buf
    qhat, khat4, vtm = sb["qhat"], sb["khat4"], sb["vtm"]
    afm = B.sbuf("a9afm", [128, 8, T], BF16)
    cm = B.sbuf("a9cm", [128, 128], F32)
    Pb = [B.sbuf(f"a9P{sl}", [128, T], BF16) for sl in range(2)]
    pt2 = [[B.sbuf(f"a9ptt{S}{psl}", [128, 128], BF16) for psl in range(2)]
           for S in range(2)]
    for S in range(2):
        for psl in range(2):
            B.bufonly(f"a9pt{S}{psl}")
    av2 = [B.sbuf(f"a9avv{S}", [128, 64], BF16) for S in range(2)]
    for S in range(2):
        B.bufonly(f"a9av{S}")
    mx = B.sbuf("a9mx", [128, 8], F32)
    for nm in ("p9A0", "p9A1", "p9t0", "p9t1"):
        B.bufonly(nm)
    ctx.dma("sync", cm[:], W["cmask"].ap[:, :], bf["a9cm"],
            ins=[W["cmask"]], outs=[bf["a9cm"]])

    def head(a, qt):
        kvh = a // 4
        qc, qh = a // 2, a % 2
        ext = (qt + 1) * 128
        S = qt % 2
        base = 2048 * S
        slb = bf[f"p9A{S}"]
        ps = B.ps_ap[:, base:base + ext]
        lhsT = qhat[64 * qh:64 * qh + 64, qc, qt * 128:(qt + 1) * 128]
        rhs = khat4[64 * qh:64 * qh + 64, kvh, 0:ext]
        mm_split(ctx, ps, lhsT, rhs, ins=[bf["qhat"], bf["khat4"]],
                 outs=[slb], start=True, stop=True)
        ctx.op("vector", lambda e, qt=qt, ext=ext, base=base:
               e.tensor_tensor(
                   B.ps_ap[:, base + qt * 128:base + ext],
                   B.ps_ap[:, base + qt * 128:base + ext], cm[:],
                   ALU.add), ins=[slb, bf["a9cm"]], outs=[slb])
        ctx.op("vector", lambda e, ext=ext, base=base, S=S: e.reduce_max(
            mx[:, 4 * S:4 * S + 1], B.ps_ap[:, base:base + ext], AX.X),
            ins=[slb], outs=[bf["a9mx"]])
        ctx.op("vector", lambda e, S=S: e.tensor_scalar(
            mx[:, 4 * S + 1:4 * S + 2], mx[:, 4 * S:4 * S + 1], -1.0,
            None, ALU.mult), ins=[bf["a9mx"]], outs=[bf["a9mx"]])
        P_, Pbb = Pb[S], bf[f"a9P{S}"]
        ctx.op("scalar", lambda e, P_=P_, ext=ext, base=base, S=S:
               e.activation(P_[:, 0:ext], B.ps_ap[:, base:base + ext],
                            AF.Exp, bias=mx[:, 4 * S + 1:4 * S + 2]),
               ins=[slb, bf["a9mx"]], outs=[Pbb])
        ctx.op("vector", lambda e, P_=P_, ext=ext, S=S: e.reduce_sum(
            mx[:, 4 * S + 2:4 * S + 3], P_[:, 0:ext], AX.X),
            ins=[Pbb], outs=[bf["a9mx"]])
        ctx.op("vector", lambda e, S=S: e.reciprocal(
            mx[:, 4 * S + 3:4 * S + 4], mx[:, 4 * S + 2:4 * S + 3]),
            ins=[bf["a9mx"]], outs=[bf["a9mx"]])

    def tail(a, qt):
        kvh = a // 4
        qc, qh = a // 2, a % 2
        S = qt % 2
        base = 2048 * S
        slb = bf[f"p9A{S}"]
        P_, Pbb = Pb[S], bf[f"a9P{S}"]
        for kt in range(qt + 1):
            psl = kt % 2
            pstp = B.ps_ap[:, base + 512 * psl:base + 512 * psl + 64]\
                .bitcast(BF16)[:, 0:128]
            ctx.op("tensor", lambda e, P_=P_, kt=kt, pstp=pstp:
                   e.transpose(pstp, P_[:, kt * 128:(kt + 1) * 128],
                               sb["ident"][:]),
                   ins=[Pbb, bf["ident"]], outs=[slb])
            ctx.op("vector", lambda e, psl=psl, pstp=pstp, S=S:
                   e.tensor_copy(pt2[S][psl][:], pstp),
                   ins=[slb], outs=[bf[f"a9pt{S}{psl}"]])
            ctx.op("tensor", lambda e, psl=psl, kvh=kvh, kt=kt, qt=qt,
                   base=base, S=S:
                   e.matmul(B.ps_ap[:, base + 1024:base + 1088],
                            pt2[S][psl][:], vtm[:, kvh, kt, :],
                            start=(kt == 0), stop=(kt == qt),
                            skip_group_check=True),
                   ins=[bf[f"a9pt{S}{psl}"], bf["vtm"]], outs=[slb])
        ctx.op("vector", lambda e, base=base, S=S: e.tensor_scalar(
            av2[S][:], B.ps_ap[:, base + 1024:base + 1088],
            mx[:, 4 * S + 3:4 * S + 4], None, ALU.mult),
            ins=[slb, bf["a9mx"]], outs=[bf[f"a9av{S}"]])
        pso = B.ps_ap[:, base + 1536:base + 1600]\
            .bitcast(BF16)[0:64, 0:128]
        ctx.op("tensor", lambda e, pso=pso, S=S: e.transpose(
            pso, av2[S][:], sb["ident"][:]),
            ins=[bf[f"a9av{S}"], bf["ident"]], outs=[slb])
        ctx.op("vector", lambda e, qh=qh, qc=qc, qt=qt, pso=pso:
               e.tensor_copy(
                   afm[64 * qh:64 * qh + 64, qc, qt * 128:(qt + 1) * 128],
                   pso), ins=[slb], outs=[bf["a9afm"]])

    iters = [(a, qt) for a in range(16) for qt in range(NT)]
    for i, (a, qt) in enumerate(iters):
        head(a, qt)
        if i > 0:
            tail(*iters[i - 1])
    tail(*iters[-1])
    wo = [B.sbuf(f"a9wo{sl}", [128, 8, 128], BF16) for sl in range(2)]
    ev = B.sbuf("a9ev", [128, T], F32)
    so = [B.sbuf(f"a9so{sl}", [128, 128], F32) for sl in range(8)]
    B.bufonly("p9wo")
    for nc_ in range(ND):
        ps = B.ps_ap[:, 0:2048]
        wsl = nc_ % 2
        w, wb = wo[wsl], bf[f"a9wo{wsl}"]
        ctx.dma(["sync", "scalar"][nc_ % 2], w[:],
                W["woT_b"].ap[:, :, nc_ * 128:(nc_ + 1) * 128].rearrange(
                    "l p c -> p l c"), wb,
                ins=[W["woT_b"]], outs=[wb])
        for lc in range(8):
            mm_split(ctx, ps, w[:, lc, :], afm[:, lc, :], ins=[wb, bf["a9afm"]],
                     outs=[bf["p9wo"]], start=(lc == 0), stop=(lc == 7))
        ctx.op("vector", lambda e, ps=ps: e.tensor_copy(ev[:], ps),
               ins=[bf["p9wo"]], outs=[bf["a9ev"]])
        for tc in range(16):
            psl = tc % 2
            sl8 = tc % 8
            pst = B.ps_ap[:, 2048 + 512 * psl:2048 + 512 * psl + 128]
            s_, s_b = so[sl8], bf[f"a9so{sl8}"]
            ctx.op("tensor", lambda e, tc=tc, pst=pst: e.transpose(
                pst, ev[:, tc * 128:(tc + 1) * 128], sb["identf"][:]),
                ins=[bf["a9ev"], bf["identf"]], outs=[bf[f"p9t{psl}"]])
            ctx.op("vector", lambda e, s_=s_, pst=pst: e.tensor_copy(
                s_[:], pst), ins=[bf[f"p9t{psl}"]], outs=[s_b])
            ctx.dma("sync",
                    s1_dram.ap[tc * 128:(tc + 1) * 128,
                               nc_ * 128:(nc_ + 1) * 128],
                    s_[:], s_b, ins=[s_b], outs=[s1_dram])


def emit_ffn1(ctx, B, W, s1_dram, x1_dram, out_dram):
    sb, bf = B.sb, B.buf
    xr = sb["xr1"]
    rsum, rsumb = sb["rsum"], bf["rsum"]
    rwt = B.sbuf("aArwt", [128, 32, 8], F32)
    rbs = B.sbuf("aArb", [128, 8], F32)
    e8 = B.sbuf("aAe8", [128, 128], F32)
    qr = B.sbuf("aAqr", [128, 32], F32)
    rt = B.sbuf("aArt", [128, 8], F32)
    rsc = B.sbuf("aArsc", [128, 1], F32)
    tl = B.sbuf("aAtl", [128, T], BF16)
    g1 = B.sbuf("aAg1", [128, T], F32)
    wt = [B.sbuf(f"aAw{sl}", [128, 128], BF16) for sl in range(4)]
    xt_ = [B.sbuf(f"aAx{sl}", [128, 128], F32) for sl in range(8)]
    st_ = [B.sbuf(f"aAs{sl}", [128, 128], F32) for sl in range(8)]
    gt_ = [B.sbuf(f"aAgt{sl}", [128, 128], F32) for sl in range(8)]
    ot_ = [B.sbuf(f"aAo{sl}", [128, 128], F32) for sl in range(8)]
    b1s = B.sbuf("aAb1", [128, 32], F32)
    for nm in ("pA0", "pA1", "pAt0", "pAt1", "pAr"):
        B.bufonly(nm)
    ctx.dma("sync", rwt[:], W["rwt"].ap.rearrange("a p b -> p a b"),
            bf["aArwt"], ins=[W["rwt"]], outs=[bf["aArwt"]])
    ctx.dma("sync", rbs[0:1, :], W["rb"].ap[:, :], bf["aArb"],
            ins=[W["rb"]], outs=[bf["aArb"]])
    ctx.dma("sync", e8[0:8, :], W["e8"].ap[:, :], bf["aAe8"],
            ins=[W["e8"]], outs=[bf["aAe8"]])
    ctx.dma("sync", b1s[:], W["b1s"].ap[:, :], bf["aAb1"],
            ins=[W["b1s"]], outs=[bf["aAb1"]])
    ctx.op("vector", lambda e: e.tensor_scalar(
        qr[:], rsum[:, 0:32], 1.0 / 1436.0, None, ALU.mult),
        ins=[rsumb], outs=[bf["aAqr"]])
    for kk in range(32):
        ctx.op("tensor", lambda e, kk=kk: e.matmul(
            B.ps_ap[0:1, 0:8], qr[:, kk:kk + 1], rwt[:, kk, :],
            start=(kk == 0), stop=(kk == 31)),
            ins=[bf["aAqr"], bf["aArwt"]], outs=[bf["pAr"]])
    ctx.op("vector", lambda e: e.tensor_tensor(
        rt[0:1, :], B.ps_ap[0:1, 0:8], rbs[0:1, :], ALU.add),
        ins=[bf["pAr"], bf["aArb"]], outs=[bf["aArt"]])
    ctx.op("vector", lambda e: e.reduce_max(
        rsc[0:1, 0:1], rt[0:1, :], AX.X), ins=[bf["aArt"]], outs=[bf["aArsc"]])
    ctx.op("vector", lambda e: e.tensor_scalar(
        rsc[0:1, 0:1], rsc[0:1, 0:1], -1.0, None, ALU.mult),
        ins=[bf["aArsc"]], outs=[bf["aArsc"]])
    ctx.op("scalar", lambda e: e.activation(
        rt[0:1, :], rt[0:1, :], AF.Exp, bias=rsc[0:1, 0:1]),
        ins=[bf["aArt"], bf["aArsc"]], outs=[bf["aArt"]])
    ctx.op("vector", lambda e: e.reduce_sum(
        rsc[0:1, 0:1], rt[0:1, :], AX.X), ins=[bf["aArt"]], outs=[bf["aArsc"]])
    ctx.op("vector", lambda e: e.reciprocal(
        rsc[0:1, 0:1], rsc[0:1, 0:1]), ins=[bf["aArsc"]], outs=[bf["aArsc"]])
    ctx.op("vector", lambda e: e.tensor_scalar(
        rt[0:1, :], rt[0:1, :], rsc[0:1, 0:1], None, ALU.mult),
        ins=[bf["aArt"], bf["aArsc"]], outs=[bf["aArt"]])
    ctx.op("tensor", lambda e: e.transpose(
        B.ps_ap[0:8, 512:513], rt[0:1, 0:8], sb["identf"][0:1, 0:1]),
        ins=[bf["aArt"], bf["identf"]], outs=[bf["pAt0"]])
    ctx.op("vector", lambda e: e.tensor_copy(
        rt[0:8, 0:1], B.ps_ap[0:8, 512:513]), ins=[bf["pAt0"]],
        outs=[bf["aArt"]])
    ctx.op("tensor", lambda e: e.matmul(
        B.ps_ap[:, 512:513], e8[0:8, :], rt[0:8, 0:1], start=True, stop=True),
        ins=[bf["aAe8"], bf["aArt"]], outs=[bf["pAt0"]])
    ctx.op("vector", lambda e: e.tensor_copy(
        rsc[:, 0:1], B.ps_ap[:, 512:513]), ins=[bf["pAt0"]],
        outs=[bf["aArsc"]])
    w32 = [B.sbuf(f"aAW{sl}", [128, 32, 128], BF16) for sl in range(2)]
    wB, wBb = w32[0], bf["aAW0"]
    ctx.dma("sync", wB[:], W["atp_b"].ap.rearrange("k p c -> p k c"),
            wBb, ins=[W["atp_b"]], outs=[wBb])
    for kk in range(32):
        mm_split(ctx, B.ps_ap[:, 0:2048], wB[:, kk, :], xr[:, kk, :],
                 ins=[wBb, bf["xr1"]], outs=[bf["pA0"]],
                 start=(kk == 0), stop=(kk == 31))
    ctx.op("vector", lambda e: e.tensor_scalar(
        tl[:], B.ps_ap[:, 0:2048], rsc[:, 0:1], None, ALU.mult),
        ins=[bf["pA0"], bf["aArsc"]], outs=[bf["aAtl"]])
    for oc in range(ND):
        ps = B.ps_ap[:, 0:2048]
        wsl = oc % 2
        w1, w1b = w32[wsl], bf[f"aAW{wsl}"]
        ctx.dma(["sync", "scalar"][oc % 2], w1[:],
                W["w1t_b"].ap[:, :, oc * 128:(oc + 1) * 128].rearrange(
                    "k p c -> p k c"),
                w1b, ins=[W["w1t_b"]], outs=[w1b])
        for kk in range(32):
            mm_split(ctx, ps, w1[:, kk, :], xr[:, kk, :], ins=[w1b, bf["xr1"]],
                     outs=[bf["pA0"]], start=(kk == 0), stop=False)
        w, wb = wt[(32) % 4], bf[f"aAw{0}"]
        ctx.dma("sync", w[:], W["btp_b"].ap[0, :, oc * 128:(oc + 1) * 128],
                wb, ins=[W["btp_b"]], outs=[wb])
        mm_split(ctx, ps, w[:], tl[:], ins=[wb, bf["aAtl"]],
                 outs=[bf["pA0"]], start=False, stop=True)
        ctx.op("scalar", lambda e, oc=oc, ps=ps: e.activation(
            g1[:], ps, AF.Silu, bias=b1s[:, oc:oc + 1]),
            ins=[bf["pA0"], bf["aAb1"]], outs=[bf["aAg1"]])
        for tc in range(16):
            psl = tc % 2
            sl8 = tc % 8
            pst = B.ps_ap[:, 2048 + 512 * psl:2048 + 512 * psl + 128]
            gt, gtb = gt_[sl8], bf[f"aAgt{sl8}"]
            xt, xtb = xt_[sl8], bf[f"aAx{sl8}"]
            st, stb = st_[sl8], bf[f"aAs{sl8}"]
            ot, otb = ot_[sl8], bf[f"aAo{sl8}"]
            rows = slice(tc * 128, tc * 128 + 128)
            cols = slice(oc * 128, oc * 128 + 128)
            ctx.op("tensor", lambda e, tc=tc, pst=pst: e.transpose(
                pst, g1[:, tc * 128:(tc + 1) * 128], sb["identf"][:]),
                ins=[bf["aAg1"], bf["identf"]], outs=[bf[f"pAt{psl}"]])
            ctx.op("vector", lambda e, gt=gt, pst=pst: e.tensor_copy(
                gt[:], pst), ins=[bf[f"pAt{psl}"]], outs=[gtb])
            ctx.dma("sync", xt[:], x1_dram.ap[rows, cols], xtb,
                    ins=[x1_dram], outs=[xtb])
            ctx.dma("sync", st[:], s1_dram.ap[rows, cols], stb,
                    ins=[s1_dram], outs=[stb])
            ctx.op("vector", lambda e, gt=gt, st=st: e.tensor_tensor(
                gt[:], gt[:], st[:], ALU.mult), ins=[gtb, stb], outs=[gtb])
            ctx.op("vector", lambda e, gt=gt, xt=xt, ot=ot: e.tensor_tensor(
                ot[:], gt[:], xt[:], ALU.add), ins=[gtb, xtb], outs=[otb])
            ctx.dma("sync", out_dram.ap[rows, cols], ot[:], otb,
                    ins=[otb], outs=[out_dram])


def _pd():
    pd = np.zeros(64, dtype=np.int64)
    for p in range(32):
        pd[p] = 2 * p
        pd[32 + p] = 2 * p + 1
    return pd


def host_l1(inp):
    pd = _pd()
    W = {}
    W["wqkvT"] = np.ascontiguousarray(
        inp["L1_attn_w_qkv"].T).reshape(32, 128, 1536).astype(np.float32)
    qc1 = np.asarray(inp["L1_attn_qc1_w"], dtype=np.float32)
    bd = np.zeros((24, 128, 128), np.float32)
    for oc in range(8):
        for tap in range(3):
            for o_rel in range(128):
                o = oc * 128 + o_rel
                g = o // 64
                for i in range(64):
                    r_in = g * 64 + i - oc * 128
                    bd[oc * 3 + tap, r_in, o_rel] = qc1[o, i, tap]
    W["qc1bd"] = bd
    W["qc1b"] = np.ascontiguousarray(
        np.asarray(inp["L1_attn_qc1_b"], np.float32).reshape(8, 128).T)
    qc2 = np.asarray(inp["L1_attn_qc2_w"], dtype=np.float32)
    q2 = np.zeros((24, 128, 1024), np.float32)
    operm = np.concatenate([(o // 64) * 64 + pd[o % 64]
                            for o in range(1024)], axis=None) \
        if False else np.array([(o // 64) * 64 + pd[o % 64]
                                for o in range(1024)])
    for tap in range(3):
        for ic in range(8):
            q2[tap * 8 + ic] = qc2[operm, ic * 128:(ic + 1) * 128, tap].T
    W["qc2T"] = q2
    W["qc2b"] = np.ascontiguousarray(
        np.asarray(inp["L1_attn_qc2_b"], np.float32)[operm].reshape(8, 128).T)
    kc1 = np.asarray(inp["L1_attn_kc1_w"], dtype=np.float32)
    bdk = np.zeros((6, 128, 128), np.float32)
    for oc in range(2):
        for tap in range(3):
            for o_rel in range(128):
                o = oc * 128 + o_rel
                g = o // 64
                for i in range(64):
                    r_in = g * 64 + i - oc * 128
                    bdk[oc * 3 + tap, r_in, o_rel] = kc1[o, i, tap]
    W["kc1bd"] = bdk
    W["kc1b"] = np.ascontiguousarray(
        np.asarray(inp["L1_attn_kc1_b"], np.float32).reshape(2, 128).T)
    kc2 = np.asarray(inp["L1_attn_kc2_w"], dtype=np.float32)
    kperm = np.array([(o // 64) * 64 + pd[o % 64] for o in range(256)])
    k2 = np.zeros((6, 128, 256), np.float32)
    for tap in range(3):
        for ic in range(2):
            k2[tap * 2 + ic] = kc2[kperm, ic * 128:(ic + 1) * 128, tap].T
    W["kc2T"] = k2
    W["kc2b"] = np.ascontiguousarray(
        np.asarray(inp["L1_attn_kc2_b"], np.float32)[kperm].reshape(2, 128).T)
    mk_ = np.zeros((2, 128, 64), np.float32)
    for c2 in range(2):
        for r in range(128):
            mk_[c2, r, (c2 * 128 + r) % 64] = 0.25
    W["mk"] = mk_
    mq = np.zeros((8, 128, 64), np.float32)
    for c8 in range(8):
        for r in range(128):
            mq[c8, r, r % 64] = 1.0 / 16.0
    W["mq"] = mq
    bq = np.zeros((8, 64, 128), np.float32)
    for oc in range(8):
        for o_rel in range(128):
            bq[oc, pd[(oc * 128 + o_rel) % 64], o_rel] = 0.5
    W["bq"] = bq
    bk = np.zeros((2, 64, 128), np.float32)
    for oc in range(2):
        for o_rel in range(128):
            bk[oc, pd[(oc * 128 + o_rel) % 64], o_rel] = 0.5
    W["bk"] = bk
    sq = np.zeros((8, 128, 16), np.float32)
    for oc in range(8):
        for r in range(128):
            sq[oc, r, (oc * 128 + r) // 64] = 1.0
    W["sq"] = sq
    sk = np.zeros((2, 128, 4), np.float32)
    for oc in range(2):
        for r in range(128):
            sk[oc, r, (oc * 128 + r) // 64] = 1.0
    W["sk"] = sk
    rq = np.zeros((8, 16, 128), np.float32)
    for oc in range(8):
        for o_rel in range(128):
            rq[oc, (oc * 128 + o_rel) // 64, o_rel] = 0.125
    W["rsbq"] = rq
    rk = np.zeros((2, 4, 128), np.float32)
    for oc in range(2):
        for o_rel in range(128):
            rk[oc, (oc * 128 + o_rel) // 64, o_rel] = 1.0
    W["rsbk"] = rk
    inv = 1.0 / 10000.0 ** (np.arange(0, 64, 2, dtype=np.float64) / 64.0)
    ang = np.arange(T, dtype=np.float64)[:, None] * inv[None, :]
    W["cosf"] = np.ascontiguousarray(
        np.tile(np.cos(ang).T.astype(np.float32), (4, 1)))
    sins = np.tile(np.sin(ang).T.astype(np.float32), (4, 1))
    sins[0:32] *= -1.0
    sins[64:96] *= -1.0
    W["sinf"] = np.ascontiguousarray(sins)
    swapm = np.zeros((128, 128), np.float32)
    for p in range(128):
        blk, off = p // 64, p % 64
        sw = blk * 64 + (off + 32) % 64
        swapm[sw, p] = 1.0
    W["swapm"] = swapm
    W["ktemp"] = np.broadcast_to(
        np.asarray(inp["L1_attn_key_temp"], np.float32).reshape(1, 1),
        (128, 1)).copy()
    cmask = np.zeros((128, 128), np.float32)
    for qq in range(128):
        cmask[qq, qq + 1:] = -1e30
    W["cmask"] = cmask
    W["woT"] = np.ascontiguousarray(
        np.asarray(inp["L1_attn_w_o"], np.float32).T).reshape(8, 128, 4096)
    W["W1T"] = np.ascontiguousarray(
        np.asarray(inp["L1_ffn_W"], np.float32).T).reshape(32, 128, 4096)
    A = np.asarray(inp["L1_ffn_A"], np.float32).reshape(128, 4096)
    W["ATp"] = np.ascontiguousarray(A.T).reshape(32, 128, 128)
    Bm = np.asarray(inp["L1_ffn_B"], np.float32)
    BT = np.ascontiguousarray(Bm.transpose(0, 2, 1).reshape(128, 4096))
    W["BTf"] = BT.reshape(1, 128, 4096)
    W["rwt"] = np.ascontiguousarray(
        np.asarray(inp["L1_ffn_router_W"], np.float32).T).reshape(32, 128, 8)
    W["rb"] = np.asarray(inp["L1_ffn_router_b"], np.float32).reshape(1, 8)
    e8 = np.zeros((8, 128), np.float32)
    for p in range(128):
        e8[p // 16, p] = 2.0
    W["e8"] = e8
    W["b1s"] = np.ascontiguousarray(
        np.asarray(inp["L1_ffn_b"], np.float32).reshape(32, 128).T)
    return W


def emit_allgather(ctx, B, shard_dram, ag_dram, rows, cols, key_buf):
    stream = ctx.dstream_for(key_buf)
    ctx.gather_streams.add(stream)
    waits = ctx._edges(stream, [shard_dram], [ag_dram])
    stream.n += 16

    def f(e, waits=waits, stream=stream):
        for st, v in waits:
            e.wait_ge(st.sem, v)
        e.collective_compute(
            "AllGather", ALU.bypass,
            replica_groups=[list(range(8))],
            ins=[shard_dram.ap.rearrange("a b c d e f -> (a b c d e) f")
                 if len(shard_dram.ap.shape) == 6 else
                 shard_dram.ap.rearrange("a b c d e -> (a b c d) e")],
            outs=[ag_dram.ap]).then_inc(stream.sem, 16)
    ctx.emit("gpsimd", f)
    ctx._post(stream, [shard_dram], [ag_dram])
    return stream.n



"""Driver section for kernel.py: per-launch builders + kernel()."""
import numpy as np
import ml_dtypes
from contextlib import ExitStack
import concourse.bass as bass
import concourse.mybir as mybir
from concourse.bass_utils import run_bass_kernel_spmd

WNAMES_F32 = {
    "wqkvT": (32, 128, 1536), "qc1bd": (24, 128, 128), "qc1b": (128, 8),
    "qc2T": (24, 128, 1024), "qc2b": (128, 8), "kc1bd": (6, 128, 128),
    "kc1b": (128, 2), "kc2T": (6, 128, 256), "kc2b": (128, 2),
    "mk": (2, 128, 64), "mq": (8, 128, 64), "bq": (8, 64, 128),
    "bk": (2, 64, 128), "sq": (8, 128, 16), "sk": (2, 128, 4),
    "rsbq": (8, 16, 128), "rsbk": (2, 4, 128), "cosf": (128, T),
    "sinf": (128, T), "ktemp": (128, 1), "cmask": (128, 128),
    "woT": (8, 128, 4096), "W1T": (32, 128, 4096), "ATp": (32, 128, 128),
    "BTf": (1, 128, 4096), "rwt": (32, 128, 8), "rb": (1, 8),
    "e8": (8, 128), "b1s": (128, 32), "swapm": (128, 128),
}
CASTS = [("wqkvT", "wqkvT_b"), ("qc2T", "qc2T_b"), ("qc1bd", "qc1bd_b"),
         ("kc1bd", "kc1bd_b"), ("woT", "woT_b"), ("W1T", "w1t_b"),
         ("ATp", "atp_b"), ("BTf", "btp_b"), ("bq", "bq_b"), ("mk", "mk_b"),
         ("mq", "mq_b"), ("kc2T", "kc2T_b"), ("bk", "bk_b")]


def host_consts():
    p4 = np.zeros((32, 128, 128), dtype=np.float32)
    for Tt in range(32):
        for k in range(128):
            p4[Tt, k, 4 * Tt + k // 32] = 1.0
    sel4 = np.zeros((4, 128, 128), dtype=np.float32)
    for b in range(4):
        for r in range(128):
            sel4[b, 32 * b + r // 4, r] = 1.0
    dup = np.zeros((128, 128), np.float32)
    for r in range(128):
        dup[r, r % 64] = 1.0
        dup[r, 64 + r % 64] = 1.0
    dupT = np.zeros((128, 128), np.float32)
    for r in range(128):
        for c in range(128):
            if c % 64 == r % 64:
                dupT[r, c] = 1.0
    return (p4, sel4.astype(BF), np.eye(128, dtype=np.float32).astype(BF),
            np.eye(128, dtype=np.float32), dupT.astype(BF))


def build_A():
    nc = bass.Bass("TRN2", target_bir_lowering=False)
    es = ExitStack()
    with es:
        ctx = Ctx(es, nc)
        B = Builder(es, nc, ctx)
        x = B.dram("x", [T, D], F32, "ExternalInput")
        ident_in = B.dram("ident_in", [128, 128], BF16, "ExternalInput")
        identf_in = B.dram("identf_in", [128, 128], F32, "ExternalInput")
        maskT = B.dram("maskT", [32, 128, D], I32, "ExternalInput")
        wtp = B.dram("wtp", [32, 128, D], F32, "ExternalInput")
        p4d = B.dram("p4d", [32, 128, 128], F32, "ExternalInput")
        sel4d = B.dram("sel4d", [4, 128, 128], BF16, "ExternalInput")
        xnp = B.dram("xnp", [ND, 128, T], BF16, "ExternalOutput")
        w0t = B.dram("w0t", [ND, 128, D], BF16, "Internal")
        s0 = B.dram("s0", [T, D], F32, "ExternalOutput")
        B.psum_main(4096)
        B.sbuf("ident", [128, 128], BF16)
        B.sbuf("identf", [128, 128], F32)
        ctx.dma("sync", B.sb["ident"][:], ident_in.ap[:, :], B.buf["ident"],
                ins=[ident_in], outs=[B.buf["ident"]])
        ctx.dma("sync", B.sb["identf"][:], identf_in.ap[:, :], B.buf["identf"],
                ins=[identf_in], outs=[B.buf["identf"]])
        B.push_scope()
        B.ln_bufs("l0")
        emit_ln(ctx, B, x, xnp, "l0", permuted=True)
        B.pop_scope()
        ctx.fence()
        B.push_scope()
        emit_w0prep(ctx, B, maskT, wtp, p4d, sel4d, w0t)
        B.pop_scope()
        ctx.fence()
        emit_resident_load(ctx, B, xnp, "xr")
        B.push_scope()
        emit_s0(ctx, B, w0t, s0)
        B.pop_scope()
        ctx.final_waits()
        ctx.run_block()
    mybir.codegen_inst_isa_subclasses(nc)
    return nc


def build_B(core):
    nc = bass.Bass("TRN2", target_bir_lowering=False)
    es = ExitStack()
    with es:
        ctx = Ctx(es, nc)
        B = Builder(es, nc, ctx)
        cb = B.dram("cb", [65536, 8], F32, "ExternalInput")
        idxc = B.dram("idxc", [128, 20480], I32, "ExternalInput")
        sgu = B.dram("sgu", [2, 11, 128, 4, 8, 128], BF16, "ExternalOutput")
        sd = B.dram("sd", [4, 128, 8, 11, 128], BF16, "ExternalOutput")
        B.push_scope()
        emit_gather(ctx, B, cb, idxc, sgu, sd, core)
        B.pop_scope()
        ctx.final_waits()
        ctx.run_block()
    mybir.codegen_inst_isa_subclasses(nc)
    return nc


def build_C():
    nc = bass.Bass("TRN2", target_bir_lowering=False)
    es = ExitStack()
    with es:
        ctx = Ctx(es, nc)
        B = Builder(es, nc, ctx)
        xnp = B.dram("xnp", [ND, 128, T], BF16, "ExternalInput")
        sgu = B.dram("sgu", [2, 88, 128, 4, 8, 128], BF16, "ExternalInput")
        sd = B.dram("sd", [32, 128, 8, 11, 128], BF16, "ExternalInput")
        x = B.dram("x", [T, D], F32, "ExternalInput")
        s0 = B.dram("s0", [T, D], F32, "ExternalInput")
        identf_in = B.dram("identf_in", [128, 128], F32, "ExternalInput")
        hd = B.dram("hd", [88, 128, T], BF16, "Internal")
        x1 = B.dram("x1", [T, D], F32, "ExternalOutput")
        B.psum_main(4096)
        B.sbuf("identf", [128, 128], F32)
        ctx.dma("sync", B.sb["identf"][:], identf_in.ap[:, :], B.buf["identf"],
                ins=[identf_in], outs=[B.buf["identf"]])
        B.push_scope()
        emit_resident_load(ctx, B, xnp, "xr")
        B.push_scope()
        emit_ffn_up(ctx, B, sgu, hd)
        B.pop_scope()
        B.pop_scope()
        ctx.fence()
        B.push_scope()
        emit_ffn_down(ctx, B, sd, hd, x, s0, x1)
        B.pop_scope()
        ctx.final_waits()
        ctx.run_block()
    mybir.codegen_inst_isa_subclasses(nc)
    return nc


def build_D():
    nc = bass.Bass("TRN2", target_bir_lowering=False)
    es = ExitStack()
    with es:
        ctx = Ctx(es, nc)
        B = Builder(es, nc, ctx)
        drams = {}
        drams["x1"] = B.dram("x1", [T, D], F32, "ExternalInput")
        for nm, shp in WNAMES_F32.items():
            drams[nm] = B.dram(nm, list(shp), F32, "ExternalInput")
        drams["ident_in"] = B.dram("ident_in", [128, 128], BF16,
                                   "ExternalInput")
        drams["identf_in"] = B.dram("identf_in", [128, 128], F32,
                                    "ExternalInput")
        drams["dup64_in"] = B.dram("dup64_in", [128, 128], BF16,
                                   "ExternalInput")
        drams["xn1"] = B.dram("xn1", [ND, 128, T], BF16, "Internal")
        drams["s1"] = B.dram("s1", [T, D], F32, "Internal")
        drams["out"] = B.dram("out", [T, D], F32, "ExternalOutput")
        for src, dst in CASTS:
            drams[dst] = B.dram(dst, list(WNAMES_F32[src]), BF16, "Internal")
        B.psum_main(4096)
        B.sbuf("ident", [128, 128], BF16)
        B.sbuf("identf", [128, 128], F32)
        ctx.dma("sync", B.sb["ident"][:], drams["ident_in"].ap[:, :],
                B.buf["ident"], ins=[drams["ident_in"]], outs=[B.buf["ident"]])
        ctx.dma("sync", B.sb["identf"][:], drams["identf_in"].ap[:, :],
                B.buf["identf"], ins=[drams["identf_in"]],
                outs=[B.buf["identf"]])
        B.sbuf("rsum", [128, 64], F32)
        B.push_scope()
        B.ln_bufs("l1")
        emit_ln(ctx, B, drams["x1"], drams["xn1"], "l1", permuted=False,
                rsum="rsum", rsum_lo=611, rsum_hi=2047)
        B.pop_scope()
        ctx.fence()
        B.push_scope()
        for i, (src, dst) in enumerate(CASTS):
            emit_cast_bf16(ctx, B, drams[src], drams[dst], f"c{i}")
        B.pop_scope()
        ctx.fence()
        B.push_scope()
        B.sbuf("qhat", [128, 8, T], BF16)
        B.sbuf("khat", [128, 2, T], BF16)
        B.sbuf("khat4", [128, 4, T], BF16)
        B.sbuf("vtm", [128, 4, 16, 64], BF16)
        B.sbuf("dup64", [128, 128], BF16)
        ctx.dma("sync", B.sb["dup64"][:], drams["dup64_in"].ap[:, :],
                B.buf["dup64"], ins=[drams["dup64_in"]],
                outs=[B.buf["dup64"]])
        B.push_scope()
        Wd = dict(drams)
        for src, dst in CASTS:
            Wd[src if src not in ("wqkvT", "qc2T", "qc1bd", "kc1bd", "woT")
               else src] = drams[dst]
        Wd["wqkvT_b"] = drams["wqkvT_b"]
        Wd["qc2T_b"] = drams["qc2T_b"]
        Wd["qc1bd_b"] = drams["qc1bd_b"]
        Wd["kc1bd_b"] = drams["kc1bd_b"]
        Wd["bq"] = drams["bq_b"]
        Wd["mk"] = drams["mk_b"]
        Wd["mq"] = drams["mq_b"]
        Wd["kc2T"] = drams["kc2T_b"]
        Wd["bk"] = drams["bk_b"]
        emit_attn_pre(ctx, B, Wd, drams["xn1"])
        B.pop_scope()
        ctx.fence()
        B.push_scope()
        Wd2 = dict(drams)
        Wd2["woT_b"] = drams["woT_b"]
        emit_attn(ctx, B, Wd2, drams["s1"])
        B.pop_scope()
        B.pop_scope()
        ctx.fence()
        B.push_scope()
        emit_resident_load(ctx, B, drams["xn1"], "xr1")
        Wd3 = dict(drams)
        Wd3["atp_b"] = drams["atp_b"]
        Wd3["w1t_b"] = drams["w1t_b"]
        Wd3["btp_b"] = drams["btp_b"]
        emit_ffn1(ctx, B, Wd3, drams["s1"], drams["x1"], drams["out"])
        B.pop_scope()
        ctx.final_waits()
        ctx.run_block()
    mybir.codegen_inst_isa_subclasses(nc)
    return nc



def build_F(with_gather=True):
    nc = bass.Bass("TRN2", target_bir_lowering=False)
    es = ExitStack()
    with es:
        ctx = Ctx(es, nc)
        B = Builder(es, nc, ctx)
        dr = {}
        dr["x"] = B.dram("x", [T, D], F32, "ExternalInput")
        dr["maskT"] = B.dram("maskT", [32, 128, D], I32, "ExternalInput")
        dr["wtp"] = B.dram("wtp", [32, 128, D], F32, "ExternalInput")
        dr["cb"] = B.dram("cb", [65536, 8], F32, "ExternalInput")
        dr["idxc"] = B.dram("idxc", [128, 20480], I32, "ExternalInput")
        dr["p4d"] = B.dram("p4d", [32, 128, 128], F32, "ExternalInput")
        dr["sel4d"] = B.dram("sel4d", [4, 128, 128], BF16, "ExternalInput")
        dr["ident_in"] = B.dram("ident_in", [128, 128], BF16, "ExternalInput")
        dr["identf_in"] = B.dram("identf_in", [128, 128], F32,
                                 "ExternalInput")
        dr["dup64_in"] = B.dram("dup64_in", [128, 128], BF16, "ExternalInput")
        for nm, shp in WNAMES_F32.items():
            dr[nm] = B.dram(nm, list(shp), F32, "ExternalInput")
        dr["xnp"] = B.dram("xnp", [ND, 128, T], BF16, "Internal")
        dr["w0t"] = B.dram("w0t", [ND, 128, D], BF16, "Internal")
        dr["s0"] = B.dram("s0", [T, D], F32, "Internal")
        if with_gather:
            dr["sgu_sh"] = B.dram("sgu_sh", [2, 11, 128, 4, 8, 128], BF16,
                                  "Internal")
            dr["sd_sh"] = B.dram("sd_sh", [4, 128, 8, 11, 128], BF16,
                                 "Internal")
            dr["ag_gu"] = B.dram("ag_gu", [8, 2, 11, 128, 4, 8, 128], BF16,
                                 "Internal")
            dr["ag_sd"] = B.dram("ag_sd", [8, 4, 128, 8, 11, 128], BF16,
                                 "Internal")
        else:
            dr["ag_gu"] = B.dram("sguF", [2, 88, 128, 4, 8, 128], BF16,
                                 "ExternalInput")
            dr["ag_sd"] = B.dram("sdF", [32, 128, 8, 11, 128], BF16,
                                 "ExternalInput")
        dr["hd"] = B.dram("hd", [88, 128, T], BF16, "Internal")
        dr["x1"] = B.dram("x1", [T, D], F32, "Internal")
        dr["xn1"] = B.dram("xn1", [ND, 128, T], BF16, "Internal")
        dr["s1"] = B.dram("s1", [T, D], F32, "Internal")
        dr["out"] = B.dram("out", [T, D], F32, "ExternalOutput")
        for src, dst in CASTS:
            dr[dst] = B.dram(dst, list(WNAMES_F32[src]), BF16, "Internal")

        B.psum_main(4096)
        B.sbuf("ident", [128, 128], BF16)
        B.sbuf("identf", [128, 128], F32)
        ctx.dma("sync", B.sb["ident"][:], dr["ident_in"].ap[:, :],
                B.buf["ident"], ins=[dr["ident_in"]], outs=[B.buf["ident"]])
        ctx.dma("sync", B.sb["identf"][:], dr["identf_in"].ap[:, :],
                B.buf["identf"], ins=[dr["identf_in"]],
                outs=[B.buf["identf"]])

        if with_gather:
            B.push_scope()
            emit_gather(ctx, B, dr["cb"], dr["idxc"], dr["sgu_sh"],
                        dr["sd_sh"], core=0, dma_eng="gpsimd")
            emit_allgather(ctx, B, dr["sgu_sh"], dr["ag_gu"],
                           2 * 11 * 4 * 8 * 128, 128, B.bufonly("agk1"))
            emit_allgather(ctx, B, dr["sd_sh"], dr["ag_sd"],
                           4 * 8 * 11 * 128, 128, B.bufonly("agk2"))
        gs = set(getattr(ctx, "gather_streams", ()))
        skipE = ("gpsimd",) if with_gather else ()

        B.push_scope()
        B.ln_bufs("l0")
        emit_ln(ctx, B, dr["x"], dr["xnp"], "l0", permuted=True)
        B.pop_scope()
        ctx.fence(skip_streams=gs, skip_engines=skipE)
        B.push_scope()
        emit_w0prep(ctx, B, dr["maskT"], dr["wtp"], dr["p4d"], dr["sel4d"],
                    dr["w0t"])
        B.pop_scope()
        ctx.fence(skip_streams=gs, skip_engines=skipE)
        ctx.fence()
        if with_gather:
            B.pop_scope()
        B.push_scope()
        cast_pairs = [(dr[src], dr[dst]) for src, dst in CASTS]
        bg = make_cast_bg(ctx, B, cast_pairs)
        B.push_scope()
        emit_resident_load(ctx, B, dr["xnp"], "xr")
        B.push_scope()
        emit_s0(ctx, B, dr["w0t"], dr["s0"])
        B.pop_scope()
        B.push_scope()
        emit_ffn_up(ctx, B, dr["ag_gu"], dr["hd"], bg=None)
        B.pop_scope()
        B.pop_scope()
        ctx.fence()
        B.push_scope()
        emit_ffn_down(ctx, B, dr["ag_sd"], dr["hd"], dr["x"], dr["s0"],
                      dr["x1"], bg=None)
        B.pop_scope()
        while bg():
            pass
        B.pop_scope()
        ctx.fence()

        B.sbuf("rsum", [128, 64], F32)
        B.push_scope()
        B.ln_bufs("l1")
        emit_ln(ctx, B, dr["x1"], dr["xn1"], "l1", permuted=False,
                rsum="rsum", rsum_lo=611, rsum_hi=2047)
        B.pop_scope()
        ctx.fence()
        B.push_scope()
        B.sbuf("qhat", [128, 8, T], BF16)
        B.sbuf("khat", [128, 2, T], BF16)
        B.sbuf("khat4", [128, 4, T], BF16)
        B.sbuf("vtm", [128, 4, 16, 64], BF16)
        B.sbuf("dup64", [128, 128], BF16)
        ctx.dma("sync", B.sb["dup64"][:], dr["dup64_in"].ap[:, :],
                B.buf["dup64"], ins=[dr["dup64_in"]], outs=[B.buf["dup64"]])
        B.push_scope()
        Wd = dict(dr)
        Wd["bq"] = dr["bq_b"]
        Wd["mk"] = dr["mk_b"]
        Wd["mq"] = dr["mq_b"]
        Wd["kc2T"] = dr["kc2T_b"]
        Wd["bk"] = dr["bk_b"]
        emit_attn_pre(ctx, B, Wd, dr["xn1"])
        B.pop_scope()
        ctx.fence()
        B.push_scope()
        emit_attn(ctx, B, dr, dr["s1"])
        B.pop_scope()
        B.pop_scope()
        ctx.fence()
        B.push_scope()
        emit_resident_load(ctx, B, dr["xn1"], "xr1")
        emit_ffn1(ctx, B, dr, dr["s1"], dr["x1"], dr["out"])
        B.pop_scope()
        ctx.final_waits()
        ctx.run_block()
    mybir.codegen_inst_isa_subclasses(nc)
    return nc, ctx.nops


def build_G():
    nc = bass.Bass("TRN2", target_bir_lowering=False)
    es = ExitStack()
    with es:
        ctx = Ctx(es, nc)
        B = Builder(es, nc, ctx)
        dr = {}
        dr["x"] = B.dram("x", [T, D], F32, "ExternalInput")
        dr["maskT"] = B.dram("maskT", [32, 128, D], I32, "ExternalInput")
        dr["wtp"] = B.dram("wtp", [32, 128, D], F32, "ExternalInput")
        dr["cb"] = B.dram("cb", [65536, 8], F32, "ExternalInput")
        dr["idxc"] = B.dram("idxc", [128, 18432], I32, "ExternalInput")
        dr["p4d"] = B.dram("p4d", [32, 128, 128], F32, "ExternalInput")
        dr["sel4d"] = B.dram("sel4d", [4, 128, 128], BF16, "ExternalInput")
        dr["ident_in"] = B.dram("ident_in", [128, 128], BF16, "ExternalInput")
        dr["identf_in"] = B.dram("identf_in", [128, 128], F32,
                                 "ExternalInput")
        dr["dup64_in"] = B.dram("dup64_in", [128, 128], BF16, "ExternalInput")
        for nm, shp in WNAMES_F32.items():
            dr[nm] = B.dram(nm, list(shp), F32, "ExternalInput")
        dr["rgu"] = B.dram("rgu", [11, 2, 4, 128, 128, 8], F32, "Internal")
        dr["rd"] = B.dram("rd", [4, 11, 128, 128, 8], F32, "Internal")
        dr["agu0"] = B.dram("agu0", [6, 8, 8, 128, 128, 8], F32, "Internal",
                            addr_space="Shared")
        dr["agu1"] = B.dram("agu1", [5, 8, 8, 128, 128, 8], F32, "Internal",
                            addr_space="Shared")
        dr["agd"] = B.dram("agd", [4, 8, 11, 128, 128, 8], F32, "Internal",
                           addr_space="Shared")
        dr["sdF"] = B.dram("sdF", [32, 128, 8, 11, 128], BF16, "Internal")
        dr["xnp"] = B.dram("xnp", [ND, 128, T], BF16, "Internal")
        dr["w0t"] = B.dram("w0t", [ND, 128, D], BF16, "Internal")
        dr["s0"] = B.dram("s0", [T, D], F32, "Internal")
        dr["hd"] = B.dram("hd", [88, 128, T], BF16, "Internal")
        dr["x1"] = B.dram("x1", [T, D], F32, "Internal")
        dr["xn1"] = B.dram("xn1", [ND, 128, T], BF16, "Internal")
        dr["s1"] = B.dram("s1", [T, D], F32, "Internal")
        dr["out"] = B.dram("out", [T, D], F32, "ExternalOutput")
        for src, dst in CASTS:
            dr[dst] = B.dram(dst, list(WNAMES_F32[src]), BF16, "Internal")

        B.psum_main(4096)
        B.sbuf("ident", [128, 128], BF16)
        B.sbuf("identf", [128, 128], F32)
        ctx.dma("sync", B.sb["ident"][:], dr["ident_in"].ap[:, :],
                B.buf["ident"], ins=[dr["ident_in"]], outs=[B.buf["ident"]])
        ctx.dma("sync", B.sb["identf"][:], dr["identf_in"].ap[:, :],
                B.buf["identf"], ins=[dr["identf_in"]],
                outs=[B.buf["identf"]])

        B.push_scope()
        agst = Stream(es, nc, "agc")
        agu_bufs = [Buf(f"agu{c}") for c in range(11)]
        agd_bufs = [Buf(f"agd{c}") for c in range(4)]
        agu_sel = (dr["agu0"], dr["agu1"])
        emit_gather_fused(ctx, B, dr["cb"], dr["idxc"], dr["rgu"], dr["rd"],
                          agu_sel, dr["agd"], agst, agu_bufs, agd_bufs)

        skipE = ("gpsimd",)
        B.push_scope()
        B.ln_bufs("l0")
        emit_ln(ctx, B, dr["x"], dr["xnp"], "l0", permuted=True)
        B.pop_scope()
        ctx.fence()
        B.push_scope()
        emit_w0prep(ctx, B, dr["maskT"], dr["wtp"], dr["p4d"], dr["sel4d"],
                    dr["w0t"])
        B.pop_scope()
        ctx.fence()
        B.push_scope()
        cast_pairs = [(dr[src], dr[dst]) for src, dst in CASTS]
        bg = make_cast_bg(ctx, B, cast_pairs)
        B.push_scope()
        emit_resident_load(ctx, B, dr["xnp"], "xr")
        B.push_scope()
        emit_s0(ctx, B, dr["w0t"], dr["s0"])
        B.pop_scope()
        ctx.fence()
        B.push_scope()
        emit_ffn_up_raw(ctx, B, agu_sel, agu_bufs, dr["hd"])
        B.pop_scope()
        B.pop_scope()
        ctx.fence()
        B.push_scope()
        dc_waits = {}
        dgroup = make_dcast(ctx, B, dr["agd"], agd_bufs, dr["sdF"], dc_waits)
        fdpart = make_ffn_down(ctx, B, dr["sdF"], dr["hd"], dr["x"],
                               dr["s0"], dr["x1"], dc_waits)
        for dci in range(4):
            dgroup(dci)
            fdpart(0, [4 * r + dci for r in range(8)])
        dc_all = [4 * r + dci for dci in range(4) for r in range(8)]
        for tq in (1, 2, 3):
            fdpart(tq, dc_all)
        B.pop_scope()
        while bg():
            pass
        B.pop_scope()
        B.pop_scope()
        ctx.fence()

        B.sbuf("rsum", [128, 64], F32)
        B.push_scope()
        B.ln_bufs("l1")
        emit_ln(ctx, B, dr["x1"], dr["xn1"], "l1", permuted=False,
                rsum="rsum", rsum_lo=611, rsum_hi=2047)
        B.pop_scope()
        ctx.fence()
        B.push_scope()
        B.sbuf("qhat", [128, 8, T], BF16)
        B.sbuf("khat", [128, 2, T], BF16)
        B.sbuf("khat4", [128, 4, T], BF16)
        B.sbuf("vtm", [128, 4, 16, 64], BF16)
        B.sbuf("dup64", [128, 128], BF16)
        ctx.dma("sync", B.sb["dup64"][:], dr["dup64_in"].ap[:, :],
                B.buf["dup64"], ins=[dr["dup64_in"]], outs=[B.buf["dup64"]])
        B.push_scope()
        Wd = dict(dr)
        Wd["bq"] = dr["bq_b"]
        Wd["mk"] = dr["mk_b"]
        Wd["mq"] = dr["mq_b"]
        Wd["kc2T"] = dr["kc2T_b"]
        Wd["bk"] = dr["bk_b"]
        emit_attn_pre(ctx, B, Wd, dr["xn1"])
        B.pop_scope()
        ctx.fence()
        B.push_scope()
        emit_attn(ctx, B, dr, dr["s1"])
        B.pop_scope()
        B.pop_scope()
        ctx.fence()
        B.push_scope()
        emit_resident_load(ctx, B, dr["xn1"], "xr1")
        emit_ffn1(ctx, B, dr, dr["s1"], dr["x1"], dr["out"])
        B.pop_scope()
        ctx.final_waits()
        ctx.run_block()
    mybir.codegen_inst_isa_subclasses(nc)
    return nc


def kernel(**inputs):
    inp = {k: np.asarray(v) for k, v in inputs.items()}
    x_all = inp["L0_attn_data"].astype(np.float32)
    weight = inp["L0_attn_weight"].astype(np.float32)
    mask = inp["L0_attn_mask"].astype(np.int32)
    cb = inp["L0_ffn_codebook"].astype(np.float32)
    p4, sel4, identb, identf, dup = host_consts()
    wtp = np.ascontiguousarray(weight.T[PERM_H]).reshape(32, 128, D)
    maskT = np.ascontiguousarray(mask.T).reshape(32, 128, D)
    W1 = host_l1(inp)
    nc = build_G()
    ims = []
    for c in range(8):
        im = {"x": np.ascontiguousarray(x_all[c]), "maskT": maskT,
              "wtp": wtp, "cb": cb,
              "idxc": host_idxcalls_f(inp["L0_ffn_idx_gate"],
                                      inp["L0_ffn_idx_up"],
                                      inp["L0_ffn_idx_down"], c),
              "p4d": p4, "sel4d": sel4, "ident_in": identb,
              "identf_in": identf, "dup64_in": dup}
        for nm in WNAMES_F32:
            im[nm] = np.ascontiguousarray(W1[nm]).astype(np.float32)
        ims.append(im)
    res = run_bass_kernel_spmd(nc, ims, core_ids=list(range(8)))
    out = np.stack([r["out"] for r in res.results], axis=0)
    return out.astype(np.float32)
